# version 87
# baseline (speedup 1.0000x reference)
import numpy as np
import contextlib
import concourse.bass as bass
import concourse.mybir as mybir
from concourse.bass_utils import run_bass_kernel_spmd

F32 = mybir.dt.float32
BF16 = mybir.dt.bfloat16
U32 = mybir.dt.uint32
AF = mybir.ActivationFunctionType
ALU = mybir.AluOpType
AXX = mybir.AxisListType.X

D = 1024
T = 2048
TFULL = 4096
NCH = 8
TT = 512
NT = T // TT
DC = 256
TQ = 1024
NQT = TFULL // TQ
PAIRS = [[0, 1], [2, 3], [4, 5], [6, 7]]


class Sem:
    def __init__(self, h):
        self.h = h
        self.c = 0

    def tick(self, inst, n=1):
        inst.then_inc(self.h, n)
        self.c += n
        return (self, self.c)

    @property
    def tok(self):
        return (self, self.c)


class Q:
    def __init__(self, eq):
        self.eq = eq
        self.last = {}

    def wait(self, tok):
        if tok is None:
            return
        if isinstance(tok, list):
            for t in tok:
                self.wait(t)
            return
        s, v = tok
        assert v <= s.c, "wait on a not-yet-emitted token (would deadlock)"
        if v <= 0 or self.last.get(id(s), 0) >= v:
            return
        self.eq.wait_ge(s.h, v)
        self.last[id(s)] = v


def build_program():
    nc = bass.Bass("TRN2", target_bir_lowering=False)

    xo = nc.dram_tensor("xo", [T, D], F32, kind="ExternalInput").ap()
    wdq = nc.dram_tensor("wdq", [D, DC], F32, kind="ExternalInput").ap()
    wdkv = nc.dram_tensor("wdkv", [D, DC], F32, kind="ExternalInput").ap()
    wuq = nc.dram_tensor("wuq", [DC, 512], F32, kind="ExternalInput").ap()
    wuk = nc.dram_tensor("wuk", [DC, 512], F32, kind="ExternalInput").ap()
    wuv = nc.dram_tensor("wuv", [DC, 512], F32, kind="ExternalInput").ap()
    wo = nc.dram_tensor("wo", [D, D], F32, kind="ExternalInput").ap()
    w1 = nc.dram_tensor("w1", [D, D], F32, kind="ExternalInput").ap()
    w2 = nc.dram_tensor("w2", [D, 512], F32, kind="ExternalInput").ap()
    w3 = nc.dram_tensor("w3", [512, 256], F32, kind="ExternalInput").ap()
    w4r = nc.dram_tensor("w4r", [128, 2], F32, kind="ExternalInput").ap()
    gw1 = nc.dram_tensor("gw1", [D, 512], F32, kind="ExternalInput").ap()
    gw2 = nc.dram_tensor("gw2", [512, 256], F32, kind="ExternalInput").ap()
    gw3r = nc.dram_tensor("gw3r", [128, 2], F32, kind="ExternalInput").ap()
    b1r = nc.dram_tensor("b1r", [128, 8], F32, kind="ExternalInput").ap()
    b2r = nc.dram_tensor("b2r", [128, 4], F32, kind="ExternalInput").ap()
    b3r = nc.dram_tensor("b3r", [128, 2], F32, kind="ExternalInput").ap()
    b4 = nc.dram_tensor("b4", [1, 1], F32, kind="ExternalInput").ap()
    gb1r = nc.dram_tensor("gb1r", [128, 4], F32, kind="ExternalInput").ap()
    gb2r = nc.dram_tensor("gb2r", [128, 2], F32, kind="ExternalInput").ap()
    gb3 = nc.dram_tensor("gb3", [1, 1], F32, kind="ExternalInput").ap()
    ar_in = nc.dram_tensor("ar_in", [1, 1], F32, kind="ExternalInput").ap()
    ident_in = nc.dram_tensor("ident_in", [128, 128], F32, kind="ExternalInput").ap()
    triu_in = nc.dram_tensor("triu_in", [128, 128], F32, kind="ExternalInput").ap()
    gidx_in = nc.dram_tensor("gidx_in", [128, 2], U32, kind="ExternalInput").ap()
    out = nc.dram_tensor("out", [T, D], F32, kind="ExternalOutput").ap()

    b_ctx_in = [nc.dram_tensor(f"b_ctx_in{i}", [128, 9], F32) for i in range(2)]
    b_ctx_out = [nc.dram_tensor(f"b_ctx_out{i}", [128, 9], F32) for i in range(2)]
    b_c_in = [nc.dram_tensor(f"b_c_in{i}", [4 * 128, 2 * TT], BF16)
              for i in range(2)]
    b_c_out = [nc.dram_tensor(f"b_c_out{i}", [8 * 128, 2 * TT], BF16)
               for i in range(2)]
    b_cq = nc.dram_tensor("b_cq", [DC, TFULL], BF16)
    b_ckv = nc.dram_tensor("b_ckv", [DC, TFULL], BF16)
    b_at_in = [nc.dram_tensor(f"b_at_in{p}", [128, TFULL], BF16) for p in range(4)]
    b_at_out = [nc.dram_tensor(f"b_at_out{p}", [512, T], BF16) for p in range(4)]

    st = contextlib.ExitStack()
    with st:
        sbt = lambda n, s, d: st.enter_context(nc.sbuf_tensor(n, s, d))
        pst = lambda n, s: st.enter_context(nc.psum_tensor(n, s, F32))
        sem = lambda n: Sem(st.enter_context(nc.semaphore(n)))

        X = sbt("X", [128, NCH * T], F32)
        ident = sbt("ident", [128, 128], F32)
        triu = sbt("triu", [128, 128], BF16)
        onesc = sbt("onesc", [128, 1], BF16)
        onesr = sbt("onesr", [1, 128], F32)
        onesrb = sbt("onesrb", [1, 128], BF16)
        w1t = sbt("w1t", [128, 8 * 1024], BF16)
        w2t = sbt("w2t", [128, 8 * 512], BF16)
        w3t = sbt("w3t", [128, 4 * 256], BF16)
        w4t = sbt("w4t", [128, 2], BF16)
        gw1t = sbt("gw1t", [128, 8 * 512], BF16)
        gw2t = sbt("gw2t", [128, 4 * 256], BF16)
        gw3t = sbt("gw3t", [128, 2], BF16)
        b1t = sbt("b1t", [128, 8], F32)
        b2t = sbt("b2t", [128, 4], F32)
        b3t = sbt("b3t", [128, 2], F32)
        gb1t = sbt("gb1t", [128, 4], F32)
        gb2t = sbt("gb2t", [128, 2], F32)
        ar_t = sbt("ar_t", [1, 1], F32)
        b4_t = sbt("b4_t", [1, 1], F32)
        gb3_t = sbt("gb3_t", [1, 1], F32)
        hb4_t = sbt("hb4_t", [1, 1], F32)
        hgb3_t = sbt("hgb3_t", [1, 1], F32)
        eps_t = sbt("eps_t", [1, 1], F32)
        arx_t = sbt("arx_t", [1, 1], F32)
        tgf_t = sbt("tgf_t", [1, 1], F32)
        ab_row = sbt("ab_row", [1, 2], F32)
        AB = sbt("AB", [128, 2], F32)
        ctx_acc = sbt("ctx_acc", [128, 9], F32)
        ctx_parts = sbt("ctx_parts", [128, 32], F32)
        corr_t = sbt("corr_t", [1, 1], F32)
        corr_red = sbt("corr_red", [1, 1], F32)
        corrb = sbt("corrb", [128, 1], F32)
        ones64 = sbt("ones64", [65, 128], F32)
        ones64b = sbt("ones64b", [65, 128], BF16)
        ctx_bf = sbt("ctx_bf", [128, 8], BF16)
        g1_bf = sbt("g1_bf", [128, 4], BF16)
        g2_bf = sbt("g2_bf", [128, 2], BF16)
        mean_row = sbt("mean_row", [1, T], F32)
        rstd_row = sbt("rstd_row", [1, T], F32)
        gidx = sbt("gidx", [128, 2], U32)

        pA = pst("pA", [128, 1024])
        pB = pst("pB", [128, 1024])
        pC = pst("pC", [128, 1024])
        pD = pst("pD", [128, 1024])

        s_pe = sem("s_pe")
        s_act = sem("s_act")
        s_dve = sem("s_dve")
        s_pool = sem("s_pool")
        dmaS = sem("dmaS")
        dmaY = sem("dmaY")
        dmaY2 = sem("dmaY2")
        dmaG8 = [sem(f"dmaG8_{i}") for i in range(8)]
        dmaWC = [sem(f"dmaWC_{i}") for i in range(8)]
        dmaC = sem("dmaC")
        dmaO = sem("dmaO")
        dmaSL = sem("dmaSL")
        dmaGT = sem("dmaGT")
        dmaGT2 = sem("dmaGT2")
        dmaW = sem("dmaW")
        dmaW2 = sem("dmaW2")
        dmaW4 = sem("dmaW4")
        dmaW3 = sem("dmaW3")
        slotY = [sem(f"slY{i}") for i in range(6)]
        slotC = [sem(f"slC{i}") for i in range(2)]
        cc = [sem(f"cc{i}") for i in range(7)]
        ccy = [sem(f"ccy{g}") for g in range(4)]

        blk = st.enter_context(nc.Block())

        @blk.gpsimd
        def _(gq):
            pe = Q(nc.tensor)
            act = Q(nc.scalar)
            dve = Q(nc.vector)
            pool = Q(nc.gpsimd)
            syn = Q(nc.sync)
            engines = [(pe, s_pe), (act, s_act), (dve, s_dve), (pool, s_pool)]

            def barrier():
                toks = [s.tok for (_, s) in engines]
                for (q, s) in engines:
                    for t in toks:
                        if t[0] is not s:
                            q.wait(t)

            def sdma(sem_, dst, src, q=syn):
                q.eq.dma_start(out=dst, in_=src).then_inc(sem_.h, 16)
                sem_.c += 16
                return (sem_, sem_.c)

            for dst, src in ((ident[:], ident_in), (b1t[:], b1r), (b2t[:], b2r),
                             (b3t[:], b3r), (gb1t[:], gb1r), (gb2t[:], gb2r),
                             (ar_t[:], ar_in), (b4_t[:], b4), (gb3_t[:], gb3),
                             (gidx[:], gidx_in)):
                sdma(dmaS, dst, src)
            ld_small = dmaS.tok

            def gload(sem_, dst_ap, src_ap):
                pool.eq.dma_start(out=dst_ap, in_=src_ap).then_inc(sem_.h, 16)
                sem_.c += 16
                return (sem_, sem_.c)

            def wview(ap_, p, a, n):
                return bass.AP(ap_.tensor, 0, [[n, p], [p * n, a], [1, n]])

            gload(dmaW, triu[:], triu_in)
            gload(dmaW, w1t[:].rearrange("p (a n) -> p a n", a=8), wview(w1, 128, 8, 1024))
            gload(dmaW, w2t[:].rearrange("p (a n) -> p a n", a=8), wview(w2, 128, 8, 512))
            gload(dmaW, gw1t[:].rearrange("p (a n) -> p a n", a=8), wview(gw1, 128, 8, 512))
            gload(dmaW, w3t[:].rearrange("p (a n) -> p a n", a=4), wview(w3, 128, 4, 256))
            gload(dmaW, gw2t[:].rearrange("p (a n) -> p a n", a=4), wview(gw2, 128, 4, 256))
            gload(dmaW, w4t[:], w4r)
            gload(dmaW, gw3t[:], gw3r)
            ld_w = dmaW.tok

            t_on1 = s_pool.tick(pool.eq.memset(onesc[:], 1.0))
            t_on2 = s_pool.tick(pool.eq.memset(onesr[:], 1.0))
            t_on2b = s_pool.tick(pool.eq.memset(onesrb[:], 1.0))
            t_on64 = s_pool.tick(pool.eq.memset(ones64[:], 1.0))
            t_on64b = s_pool.tick(pool.eq.memset(ones64b[:], 1.0))
            t_eps = s_pool.tick(pool.eq.memset(eps_t[:], 1.0e-5))
            t_ones = [t_on1, t_on2, t_on2b, t_eps]
            dve.wait(ld_small)
            t_hb4 = s_dve.tick(nc.vector.tensor_scalar(hb4_t[:], b4_t[:], 0.5, None, ALU.mult))
            t_hgb3 = s_dve.tick(nc.vector.tensor_scalar(hgb3_t[:], gb3_t[:], 0.5, None, ALU.mult))
            t_arx = s_dve.tick(nc.vector.tensor_scalar(arx_t[:], ar_t[:], 0.7, None, ALU.mult))

            PBANKS = [(pA, 0), (pA, 512), (pB, 0), (pB, 512),
                      (pC, 0), (pC, 512), (pD, 0), (pD, 512)]

            barrier()

            def ln_stats_group(g, xbf, sq, var_row, st_):
                g0 = g * TT
                pe.wait(t_ones)
                mm_m = mm_s = None
                for chh in range(NCH):
                    bi = chh % 2
                    dve.wait(st_["xbf_free"][bi])
                    tx = s_dve.tick(nc.vector.tensor_copy(
                        xbf[bi], X[:, chh * T + g0: chh * T + g0 + TT]))
                    act.wait(tx)
                    act.wait(st_["sq_free"][bi])
                    tsq = s_act.tick(nc.scalar.activation(
                        sq[bi], xbf[bi], AF.Square))
                    pe.wait(tx)
                    mm_m = s_pe.tick(nc.tensor.matmul(
                        pC[0:1, 0:TT], onesc[:], xbf[bi],
                        start=(chh == 0), stop=(chh == NCH - 1)))
                    st_["xbf_free"][bi] = [mm_m, tsq]
                    pe.wait(tsq)
                    mm_s = s_pe.tick(nc.tensor.matmul(
                        pC[0:1, 512:512 + TT], onesc[:], sq[bi],
                        start=(chh == 0), stop=(chh == NCH - 1)))
                    st_["sq_free"][bi] = mm_s
                dve.wait(mm_m)
                t1 = s_dve.tick(nc.vector.tensor_scalar(
                    mean_row[:, g0:g0 + TT], pC[0:1, 0:TT], 1.0 / D, None, ALU.mult))
                dve.wait(mm_s)
                dve.wait(st_["t6"])
                t2 = s_dve.tick(nc.vector.tensor_scalar(
                    var_row[:, 0:TT], pC[0:1, 512:512 + TT], 1.0 / D, None, ALU.mult))
                dve.wait([t1, t2])
                t3 = s_dve.tick(nc.vector.tensor_tensor(
                    rstd_row[:, g0:g0 + TT], mean_row[:, g0:g0 + TT],
                    mean_row[:, g0:g0 + TT], ALU.mult))
                dve.wait(t3)
                t4 = s_dve.tick(nc.vector.tensor_tensor(
                    var_row[:, 0:TT], var_row[:, 0:TT], rstd_row[:, g0:g0 + TT],
                    ALU.subtract))
                act.wait(t4)
                act.wait(t_ones)
                t5 = s_act.tick(nc.scalar.activation(
                    var_row[:, 0:TT], var_row[:, 0:TT], AF.Ln, bias=eps_t[0:1, 0:1]))
                act.wait(t5)
                st_["t6"] = s_act.tick(nc.scalar.activation(
                    rstd_row[:, g0:g0 + TT], var_row[:, 0:TT], AF.Exp, scale=-0.5))
                pe.wait([t1, t2])
                return st_["t6"]

            def ln_stats(xbf, sq, var_row, prev_rstd):
                st_ = {"xbf_free": [None, None], "sq_free": [None, None],
                       "t6": prev_rstd}
                for g in range(NT):
                    t6 = ln_stats_group(g, xbf, sq, var_row, st_)
                return t6

            pa_state = {}

            def passA_begin():
                pool.wait(s_pe.tok)
                pool.wait(s_dve.tok)
                pool.wait(dmaS.tok)
                tz = s_pool.tick(pool.eq.memset(ctx_acc[:], 0.0))
                tzc = s_pool.tick(pool.eq.memset(corr_t[:], 0.0))
                pa_state.clear()
                pa_state.update(ttr=[None, None], tm3=None, zero=[tz, tzc],
                                first=True)

            def passA_group(g, scraps, dumps, varbuf, rdy, varw):
                g0 = g * TT
                pe.wait(t_ones)
                pe.wait(rdy)
                trb_ = s_pe.tick(nc.tensor.matmul(
                    pC[:, 512:512 + TT], onesr[:, 0:128],
                    rstd_row[:, g0:g0 + TT], start=True, stop=True))
                first_d = True
                tmu = None
                for chh in range(NCH):
                    si = chh % 2
                    dve.wait(pa_state["zero"][0])
                    dve.wait(trb_)
                    dve.wait(pa_state["ttr"][si])
                    tmu = s_dve.tick(nc.vector.tensor_tensor(
                        scraps[si][:],
                        X[:, chh * T + g0: chh * T + g0 + TT],
                        pC[:, 512:512 + TT], ALU.mult))
                    if first_d:
                        act.wait(s_pe.tok)
                        if pa_state["first"]:
                            act.wait(s_dve.tok)
                            pa_state["first"] = False
                        first_d = False
                    act.wait(tmu)
                    pa_state["ttr"][si] = s_act.tick(nc.scalar.activation(
                        dumps[si], scraps[si][:], AF.Identity,
                        accum_out=ctx_parts[:, g * NCH + chh: g * NCH + chh + 1]))
                dve.wait(pa_state["zero"][1])
                dve.wait(pa_state["tm3"])
                dve.wait(varw)
                tm1 = s_dve.tick(nc.vector.tensor_tensor(
                    varbuf[:, 0:TT], mean_row[:, g0:g0 + TT],
                    rstd_row[:, g0:g0 + TT], ALU.mult))
                dve.wait(tm1)
                tm2 = s_dve.tick(nc.vector.reduce_sum(
                    corr_red[:], varbuf[:, 0:TT], axis=AXX))
                dve.wait(tm2)
                pa_state["tm3"] = s_dve.tick(nc.vector.tensor_tensor(
                    corr_t[:], corr_t[:], corr_red[:], ALU.add))
                pe.wait(tmu)

            def passA_finalize(call):
                dve.wait(pa_state["tm3"])
                s_dve.tick(nc.vector.tensor_copy(ctx_acc[0:1, 8:9], corr_t[:]))
                dve.wait(s_dve.tok)
                dve.wait(pa_state["ttr"][0])
                dve.wait(pa_state["ttr"][1])
                t_s1 = s_dve.tick(nc.vector.tensor_tensor(
                    ctx_parts[:, 0:8], ctx_parts[:, 0:8], ctx_parts[:, 8:16], ALU.add))
                t_s2 = s_dve.tick(nc.vector.tensor_tensor(
                    ctx_parts[:, 16:24], ctx_parts[:, 16:24], ctx_parts[:, 24:32], ALU.add))
                dve.wait([t_s1, t_s2])
                s_dve.tick(nc.vector.tensor_tensor(
                    ctx_acc[:, 0:8], ctx_parts[:, 0:8], ctx_parts[:, 16:24], ALU.add))
                syn.wait(s_dve.tok)
                sdma(dmaS, b_ctx_in[call].ap(), ctx_acc[:])
                pool.wait(dmaS.tok)
                icc = 0 if call == 0 else 6
                tcol = cc[icc].tick(nc.gpsimd.collective_compute(
                    "AllReduce", ALU.add, replica_groups=PAIRS,
                    ins=[b_ctx_in[call].ap()], outs=[b_ctx_out[call].ap()]))
                syn.wait(tcol)
                sdma(dmaS, ctx_acc[:], b_ctx_out[call].ap())
                return dmaS.tok

            def ln_bcast(g0, rdy=None):
                pe.wait(rdy)
                pe.wait(t_ones)
                tm = s_pe.tick(nc.tensor.matmul(
                    pC[:, 0:TT], onesr[:, 0:128], mean_row[:, g0:g0 + TT],
                    start=True, stop=True))
                tr = s_pe.tick(nc.tensor.matmul(
                    pC[:, 512:512 + TT], onesr[:, 0:128], rstd_row[:, g0:g0 + TT],
                    start=True, stop=True))
                return [tm, tr]

            prev_rstd = [None]
            hook_state = {"pd0": None, "pd1": None, "var_rd": None}

            def ffn(call, xload=None, stats_done=False, pre_ctx=None,
                    post_group=None):
                with contextlib.ExitStack() as fst:
                    fsb = lambda n, s, d: fst.enter_context(
                        nc.sbuf_tensor(f"{n}_f{call}", s, d))
                    stateA = fsb("stateA", [128, NCH * TT], F32)
                    stateB = fsb("stateB", [128, NCH * TT], F32)
                    stateC = fsb("stateC", [128, NCH * TT], F32)
                    state_bf = fsb("state_bf", [128, NCH * TT], BF16)
                    state_bf1 = fsb("state_bf1", [128, NCH * TT], BF16)
                    h1b = fsb("h1b", [128, 8 * TT], BF16)
                    h3b = h1b
                    t1f = fsb("t1f", [128, TT], F32)
                    cb = fsb("cb", [128, TT], F32)
                    tw = [fsb(f"tw{i}", [128, TT], F32) for i in range(2)]
                    hfb = [fsb("hfb0", [128, TT], F32), fsb("hfb1", [128, TT], F32)]
                    var_row = fsb("var_row", [1, TT], F32)
                    tanh_bf = fsb("tanh_bf", [1, TT], BF16)

                    xbf = [state_bf[:, 0:TT], state_bf[:, TT:2 * TT]]
                    sq = [state_bf1[:, 0:TT], state_bf1[:, TT:2 * TT]]
                    if not stats_done:
                        lnst0 = {"xbf_free": [None, None], "sq_free": [None, None],
                                 "t6": prev_rstd[0]}
                        passA_begin()
                        scrapsA = [hfb[0], hfb[1]]
                        dumpsA = [state_bf1[:, 0:TT], state_bf1[:, TT:2 * TT]]
                        for g in range(NT):
                            if xload is not None:
                                xload(g, stateC)
                            prev_rstd[0] = ln_stats_group(g, xbf, sq, var_row,
                                                          lnst0)
                        for g in range(NT):
                            passA_group(g, scrapsA, dumpsA, var_row,
                                        prev_rstd[0], prev_rstd[0])
                        ctx_ld = passA_finalize(call)
                    else:
                        ctx_ld = pre_ctx

                    H6 = [(pA, 0), (pA, 512), (pB, 0), (pB, 512),
                          (pD, 0), (pD, 512)]
                    bank_free = [xst["bank"][i] for i in range(6)]
                    hk6 = [0]
                    PERM = [4, 5, 6, 7, 0, 1, 2, 3]
                    SGN = [+1, +1, -1, -1, -1, -1, +1, +1]
                    prev_ttr = None
                    prev_cb_reader = None
                    res_done = None

                    res_free = [None, None]
                    SBUFS = [stateA, stateB, stateC]
                    BFS = [state_bf, state_bf1]
                    IDX_I = [0, 2, 1, 0]
                    IDX_O = [1, 0, 2, 1]
                    passb_dve_last = None

                    def pass_b(g):
                        nonlocal passb_dve_last
                        g0 = g * TT
                        if g == 1:
                            dve.wait(pa_state["ttr"][0])
                            dve.wait(pa_state["ttr"][1])
                        pe.wait(passb_dve_last)
                        bct = ln_bcast(g0)
                        dst_s = SBUFS[IDX_I[g]]
                        dst_b = BFS[g % 2]
                        bf_free = s_pe.tok
                        toks = [None] * NCH
                        tbm = None
                        for chh in range(NCH):
                            dve.wait(bct)
                            dve.wait(tbm)
                            dve.wait(hook_state["var_rd"])
                            ta = s_dve.tick(nc.vector.tensor_tensor(
                                t1f[:], X[:, chh * T + g0: chh * T + g0 + TT],
                                pC[:, 0:TT], ALU.subtract))
                            dve.wait(ta)
                            tbm = s_dve.tick(nc.vector.tensor_tensor(
                                dst_s[:, chh * TT:(chh + 1) * TT], t1f[:],
                                pC[:, 512:512 + TT], ALU.mult))
                            dve.wait(tbm)
                            dve.wait(bf_free)
                            toks[chh] = s_dve.tick(nc.vector.tensor_copy(
                                dst_b[:, chh * TT:(chh + 1) * TT],
                                dst_s[:, chh * TT:(chh + 1) * TT]))
                        passb_dve_last = tbm
                        return toks

                    pe.wait(ld_w)
                    act.wait(ld_small)
                    cast_next = pass_b(0)

                    gc_state = {}

                    def emit_gc():
                        pe.wait(ctx_ld)
                        pe.wait(t_ones)
                        pe.wait(bank_free[4])
                        tcmm = s_pe.tick(nc.tensor.matmul(
                            pD[:, 13:14], onesr[:, 0:128], ctx_acc[0:1, 8:9],
                            start=True, stop=True))
                        dve.wait(tcmm)
                        tcpy = s_dve.tick(nc.vector.tensor_copy(corrb[:], pD[:, 13:14]))
                        dve.wait(tcpy)
                        tcb = s_dve.tick(nc.vector.tensor_scalar(
                            ctx_bf[:], ctx_acc[:, 0:8], corrb[:, 0:1], 1.0 / TFULL,
                            ALU.subtract, ALU.mult))
                        pe.wait(tcb)
                        pe.wait(ld_w)
                        tga = None
                        for fo in range(4):
                            pe.wait(tga)
                            for fi in range(8):
                                tg = s_pe.tick(nc.tensor.matmul(
                                    pD[:, fo:fo + 1],
                                    gw1t[:, fi * 512 + fo * 128: fi * 512 + (fo + 1) * 128],
                                    ctx_bf[:, fi:fi + 1], start=(fi == 0), stop=(fi == 7)))
                            act.wait(tg)
                            tga = s_act.tick(nc.scalar.activation(
                                g1_bf[:, fo:fo + 1], pD[:, fo:fo + 1], AF.Gelu,
                                bias=gb1t[:, fo:fo + 1]))
                        tg2 = None
                        for fo in range(2):
                            pe.wait(tga if tg2 is None else tg2)
                            pe.wait(s_act.tok)
                            for fi in range(4):
                                tg = s_pe.tick(nc.tensor.matmul(
                                    pD[:, 8 + fo:9 + fo],
                                    gw2t[:, fi * 256 + fo * 128: fi * 256 + (fo + 1) * 128],
                                    g1_bf[:, fi:fi + 1], start=(fi == 0), stop=(fi == 3)))
                            act.wait(tg)
                            tg2 = s_act.tick(nc.scalar.activation(
                                g2_bf[:, fo:fo + 1], pD[:, 8 + fo:9 + fo], AF.Gelu,
                                bias=gb2t[:, fo:fo + 1]))
                        pe.wait(s_act.tok)
                        for fi in range(2):
                            tg = s_pe.tick(nc.tensor.matmul(
                                pD[0:1, 12:13], gw3t[:, fi:fi + 1], g2_bf[:, fi:fi + 1],
                                start=(fi == 0), stop=(fi == 1)))
                        act.wait(tg)
                        act.wait(t_hgb3)
                        ttg = s_act.tick(nc.scalar.activation(
                            tgf_t[:], pD[0:1, 12:13], AF.Tanh, bias=hgb3_t[0:1, 0:1],
                            scale=0.5))
                        dve.wait([ttg, t_arx])
                        ta1 = s_dve.tick(nc.vector.tensor_tensor(
                            ab_row[:, 0:1], tgf_t[:], arx_t[:], ALU.mult))
                        dve.wait(ta1)
                        ta2 = s_dve.tick(nc.vector.tensor_scalar(
                            ab_row[:, 0:1], ab_row[:, 0:1], 0.1, None, ALU.add))
                        dve.wait(ta2)
                        ta3 = s_dve.tick(nc.vector.tensor_scalar(
                            ab_row[:, 1:2], ar_t[:], 0.3, None, ALU.mult))
                        pe.wait(ta3)
                        tab = s_pe.tick(nc.tensor.matmul(
                            pD[:, 14:16], onesr[:, 0:128], ab_row[:], start=True, stop=True))
                        dve.wait(tab)
                        tAB = s_dve.tick(nc.vector.tensor_copy(AB[:], pD[:, 14:16]))
                        pe.wait(tAB)

                        gc_state["tAB"] = tAB
                        bank_free[4] = tAB

                    for g in range(NT):
                        g0 = g * TT
                        cast_tok = cast_next
                        s_in = SBUFS[IDX_I[g]]
                        s_out = SBUFS[IDX_O[g]]
                        state_b = BFS[g % 2]
                        for cyc in range(3):
                            src = s_in if cyc % 2 == 0 else s_out
                            dst = s_out if cyc % 2 == 0 else s_in
                            ev1 = [None] * 8
                            stopt = [None] * 8
                            b1m = [(hk6[0] + i) % 6 for i in range(8)]
                            b2m = [(hk6[0] + 8 + i) % 6 for i in range(4)]
                            b3m = [(hk6[0] + 12 + i) % 6 for i in range(2)]
                            hk6[0] = (hk6[0] + 14) % 6
                            for fog in range(2):
                                fos = list(range(fog * 4, fog * 4 + 4))
                                for fi in range(8):
                                    pe.wait(cast_tok[fi])
                                    for fo in fos:
                                        bk = H6[b1m[fo]]
                                        if fi == 0:
                                            pe.wait(bank_free[b1m[fo]])
                                        tmm = s_pe.tick(nc.tensor.matmul(
                                            bk[0][:, bk[1]:bk[1] + TT],
                                            w1t[:, fi * 1024 + fo * 128: fi * 1024 + (fo + 1) * 128],
                                            state_b[:, fi * TT:(fi + 1) * TT],
                                            start=(fi == 0), stop=(fi == 7)))
                                        if fi == 7:
                                            stopt[fo] = tmm
                                for fo in fos:
                                    bk = H6[b1m[fo]]
                                    act.wait(stopt[fo])
                                    ev1[fo] = s_act.tick(nc.scalar.activation(
                                        h1b[:, fo * TT:(fo + 1) * TT],
                                        bk[0][:, bk[1]:bk[1] + TT],
                                        AF.Gelu, bias=b1t[:, fo:fo + 1]))
                                    bank_free[b1m[fo]] = ev1[fo]
                            ev2 = [None] * 4
                            for fi in range(8):
                                pe.wait(ev1[fi])
                                for fo in range(4):
                                    bk = H6[b2m[fo]]
                                    if fi == 0:
                                        pe.wait(bank_free[b2m[fo]])
                                    tmm = s_pe.tick(nc.tensor.matmul(
                                        bk[0][:, bk[1]:bk[1] + TT],
                                        w2t[:, fi * 512 + fo * 128: fi * 512 + (fo + 1) * 128],
                                        h1b[:, fi * TT:(fi + 1) * TT],
                                        start=(fi == 0), stop=(fi == 7)))
                                    if fi == 7:
                                        stopt[fo] = tmm
                            for fo in range(4):
                                bk = H6[b2m[fo]]
                                act.wait(stopt[fo])
                                ev2[fo] = s_act.tick(nc.scalar.activation(
                                    h1b[:, (2 + fo) * TT:(3 + fo) * TT],
                                    bk[0][:, bk[1]:bk[1] + TT],
                                    AF.Gelu, bias=b2t[:, fo:fo + 1]))
                                bank_free[b2m[fo]] = ev2[fo]
                            ev3 = [None] * 2
                            for fi in range(4):
                                pe.wait(ev2[fi])
                                for fo in range(2):
                                    bk = H6[b3m[fo]]
                                    if fi == 0:
                                        pe.wait(bank_free[b3m[fo]])
                                    tmm = s_pe.tick(nc.tensor.matmul(
                                        bk[0][:, bk[1]:bk[1] + TT],
                                        w3t[:, fi * 256 + fo * 128: fi * 256 + (fo + 1) * 128],
                                        h1b[:, (2 + fi) * TT:(3 + fi) * TT],
                                        start=(fi == 0), stop=(fi == 3)))
                                    if fi == 3:
                                        stopt[fo] = tmm
                            for fo in range(2):
                                bk = H6[b3m[fo]]
                                act.wait(stopt[fo])
                                ev3[fo] = s_act.tick(nc.scalar.activation(
                                    h3b[:, fo * TT:(fo + 1) * TT],
                                    bk[0][:, bk[1]:bk[1] + TT],
                                    AF.Gelu, bias=b3t[:, fo:fo + 1]))
                                bank_free[b3m[fo]] = ev3[fo]
                            if g == 0 and cyc == 0:
                                emit_gc()
                            pe.wait(bank_free[4])
                            for fi in range(2):
                                pe.wait(ev3[fi])
                                tz2 = s_pe.tick(nc.tensor.matmul(
                                    pD[0:1, 0:TT], w4t[:, fi:fi + 1],
                                    h3b[:, fi * TT:(fi + 1) * TT],
                                    start=(fi == 0), stop=(fi == 1)))
                            act.wait([tz2, t_hb4])
                            prev_ttr = s_act.tick(nc.scalar.activation(
                                tanh_bf[:, 0:TT], pD[0:1, 0:TT], AF.Tanh,
                                bias=hb4_t[0:1, 0:1], scale=0.5))
                            pe.wait(prev_ttr)
                            pe.wait(bank_free[5])
                            tbc = s_pe.tick(nc.tensor.matmul(
                                pD[:, 512:512 + TT], onesrb[:, 0:128],
                                tanh_bf[:, 0:TT], start=True, stop=True))
                            dve.wait([tbc, gc_state["tAB"]])
                            tcb2 = s_dve.tick(nc.vector.tensor_scalar(
                                cb[:], pD[:, 512:512 + TT], AB[:, 1:2], AB[:, 0:1],
                                ALU.mult, ALU.add))
                            prev_cb_reader = tcb2
                            bank_free[4] = prev_ttr
                            bank_free[5] = tcb2
                            pe.wait(tcb2)
                            tw_free = [None] * 2
                            tw1s = [None] * NCH
                            tw2s = [None] * NCH
                            for ch0 in range(0, NCH, 2):
                                for chh in (ch0, ch0 + 1):
                                    dve.wait(tcb2)
                                    dve.wait(tw_free[chh % 2])
                                    tw1s[chh] = s_dve.tick(nc.vector.tensor_tensor(
                                        tw[chh % 2][:],
                                        src[:, PERM[chh] * TT:(PERM[chh] + 1) * TT],
                                        cb[:], ALU.mult))
                                for chh in (ch0, ch0 + 1):
                                    dve.wait(tw1s[chh])
                                    tw2s[chh] = s_dve.tick(nc.vector.tensor_tensor(
                                        dst[:, chh * TT:(chh + 1) * TT],
                                        src[:, chh * TT:(chh + 1) * TT],
                                        tw[chh % 2][:],
                                        ALU.add if SGN[chh] > 0 else ALU.subtract))
                                    tw_free[chh % 2] = tw2s[chh]
                                if cyc < 2:
                                    for chh in (ch0, ch0 + 1):
                                        dve.wait(tw2s[chh])
                                        cast_tok[chh] = s_dve.tick(nc.vector.tensor_copy(
                                            state_b[:, chh * TT:(chh + 1) * TT],
                                            dst[:, chh * TT:(chh + 1) * TT]))
                                else:
                                    cast_tok[ch0] = tw2s[ch0]
                                    cast_tok[ch0 + 1] = tw2s[ch0 + 1]
                            if cyc == 1 and g + 1 < NT:
                                cast_next = pass_b(g + 1)
                        for chh in range(NCH):
                            hi = chh % 2
                            dve.wait(cast_tok[chh])
                            dve.wait(res_free[hi])
                            th = s_dve.tick(nc.vector.tensor_scalar(
                                hfb[hi][:], s_out[:, chh * TT:(chh + 1) * TT],
                                0.5, None, ALU.mult))
                            dve.wait(th)
                            res_free[hi] = s_dve.tick(nc.vector.tensor_tensor(
                                X[:, chh * T + g0: chh * T + g0 + TT],
                                X[:, chh * T + g0: chh * T + g0 + TT],
                                hfb[hi][:], ALU.add))
                    barrier()

            PB6 = [(pA, 0), (pA, 512), (pB, 0), (pB, 512), (pD, 0), (pD, 512)]
            xst = {"bank": [None] * 6, "slot": [None] * 3, "k": 0}

            def xload(g, stg):
                pe.wait(dmaS.tok)
                for i in range(4 * g, 4 * g + 4):
                    sl = i % 3
                    syn.wait(xst["slot"][sl])
                    sdma(slotY[sl], stg[:, sl * 1024:(sl + 1) * 1024],
                         xo[i * 128:(i + 1) * 128, :])
                    ldt = slotY[sl].tok
                    pe.wait(ldt)
                    last = None
                    for ch in range(NCH):
                        bnk = xst["k"] % 6
                        pt, off = PB6[bnk]
                        pe.wait(xst["bank"][bnk])
                        ttr = s_pe.tick(nc.tensor.transpose(
                            pt[:, off:off + 128],
                            stg[:, sl * 1024 + ch * 128: sl * 1024 + (ch + 1) * 128],
                            ident[:]))
                        dve.wait(ttr)
                        last = s_dve.tick(nc.vector.tensor_copy(
                            X[:, ch * T + i * 128: ch * T + (i + 1) * 128],
                            pt[:, off:off + 128]))
                        xst["bank"][bnk] = last
                        xst["k"] += 1
                    xst["slot"][sl] = last

            ffn(0, xload)

            with contextlib.ExitStack() as ast_:
                asbt = lambda n, s, d: ast_.enter_context(nc.sbuf_tensor(n, s, d))
                wuqt = asbt("wuqt", [128, 2 * 512], BF16)
                wukt = asbt("wukt", [128, 2 * 512], BF16)
                wuvt = asbt("wuvt", [128, 2 * 512], BF16)
                cqs_stack = contextlib.ExitStack()
                cq_sb = cqs_stack.enter_context(nc.sbuf_tensor("cq_sb", [128, 2 * TFULL], BF16))
                ckv_sb = cqs_stack.enter_context(nc.sbuf_tensor("ckv_sb", [128, 2 * TFULL], BF16))

                prj = contextlib.ExitStack()
                psbt = lambda n, s, d: prj.enter_context(nc.sbuf_tensor(n, s, d))
                xbf = [psbt("a_xbf0", [128, TT], BF16)[:], psbt("a_xbf1", [128, TT], BF16)[:]]
                sq = [psbt("a_sq0", [128, TT], BF16)[:], psbt("a_sq1", [128, TT], BF16)[:]]
                var_row = psbt("a_var", [1, TT], F32)
                t1f = psbt("a_t1f", [128, TT], F32)
                ybufs = [psbt("a_y2b0", [128, TT], BF16), psbt("a_y2b1", [128, TT], BF16)]
                y2g_sb2 = [psbt("y2gA", [128, NCH * TT], BF16),
                           psbt("y2gB", [128, NCH * TT], BF16)]
                y2g_rd = [[None] * NCH, [None] * NCH]
                slot_rd = [None] * 4
                wdqt = psbt("wdqt", [128, 8 * 256], BF16)
                wdkvt = psbt("wdkvt", [128, 8 * 256], BF16)

                gload(dmaW2, wdqt[:].rearrange("p (a n) -> p a n", a=8), wview(wdq, 128, 8, 256))
                gload(dmaW2, wdkvt[:].rearrange("p (a n) -> p a n", a=8), wview(wdkv, 128, 8, 256))
                ld_attw = dmaW2.tok
                gload(dmaW4, wuqt[:].rearrange("p (a n) -> p a n", a=2), wview(wuq, 128, 2, 512))
                gload(dmaW4, wukt[:].rearrange("p (a n) -> p a n", a=2), wview(wuk, 128, 2, 512))
                gload(dmaW4, wuvt[:].rearrange("p (a n) -> p a n", a=2), wview(wuv, 128, 2, 512))
                ld_attw2 = dmaW4.tok

                lnst = {"xbf_free": [None, None], "sq_free": [None, None],
                        "t6": prev_rstd[0]}
                t6s = [None] * NT

                def ln_stats0(g):
                    g0 = g * TT
                    pe.wait(t_ones)
                    mm_m = mm_s = None
                    for chh in range(NCH):
                        bi = chh % 2
                        act.wait(lnst["xbf_free"][bi])
                        tx = s_act.tick(nc.scalar.activation(
                            xbf[bi], X[:, chh * T + g0: chh * T + g0 + TT],
                            AF.Identity))
                        pe.wait(tx)
                        mm_m = s_pe.tick(nc.tensor.matmul(
                            pC[0:1, 0:TT], onesc[:], xbf[bi],
                            start=(chh == 0), stop=(chh == NCH - 1)))
                        lnst["xbf_free"][bi] = mm_m
                        act.wait(lnst["sq_free"][bi])
                        tsq = s_act.tick(nc.scalar.activation(
                            sq[bi], X[:, chh * T + g0: chh * T + g0 + TT],
                            AF.Square))
                        pe.wait(tsq)
                        mm_s = s_pe.tick(nc.tensor.matmul(
                            pC[0:1, 512:512 + TT], onesc[:], sq[bi],
                            start=(chh == 0), stop=(chh == NCH - 1)))
                        lnst["sq_free"][bi] = mm_s
                    dve.wait(mm_m)
                    t1 = s_dve.tick(nc.vector.tensor_scalar(
                        mean_row[:, g0:g0 + TT], pC[0:1, 0:TT], 1.0 / D, None,
                        ALU.mult))
                    dve.wait(mm_s)
                    dve.wait(lnst["t6"])
                    t2 = s_dve.tick(nc.vector.tensor_scalar(
                        var_row[:, 0:TT], pC[0:1, 512:512 + TT], 1.0 / D, None,
                        ALU.mult))
                    dve.wait([t1, t2])
                    t3 = s_dve.tick(nc.vector.tensor_tensor(
                        rstd_row[:, g0:g0 + TT], mean_row[:, g0:g0 + TT],
                        mean_row[:, g0:g0 + TT], ALU.mult))
                    dve.wait(t3)
                    t4 = s_dve.tick(nc.vector.tensor_tensor(
                        var_row[:, 0:TT], var_row[:, 0:TT],
                        rstd_row[:, g0:g0 + TT], ALU.subtract))
                    act.wait(t4)
                    act.wait(t_ones)
                    t5 = s_act.tick(nc.scalar.activation(
                        var_row[:, 0:TT], var_row[:, 0:TT], AF.Ln,
                        bias=eps_t[0:1, 0:1]))
                    act.wait(t5)
                    lnst["t6"] = s_act.tick(nc.scalar.activation(
                        rstd_row[:, g0:g0 + TT], var_row[:, 0:TT], AF.Exp,
                        scale=-0.5))
                    pe.wait([t1, t2])
                    return lnst["t6"]
                tal = [None] * NT
                tbl = [None] * NT
                pjbank = [(pA, 0), (pA, 512), (pB, 0), (pB, 512)]
                bank_ev = [None] * 4
                tbg = [[None] * NCH for _ in range(NT)]
                tb = None

                def emit_proj(g):
                    gb = g % 2
                    pe.wait(ld_attw)
                    stop4 = [None] * 4
                    for ci in range(4):
                        kind, co = ci // 2, ci % 2
                        pt, off = pjbank[ci]
                        for fi in range(NCH):
                            if fi == 0:
                                pe.wait(bank_ev[ci])
                            pe.wait(tbg[g][fi])
                            tmm = s_pe.tick(nc.tensor.matmul(
                                pt[:, off:off + TT],
                                (wdqt if kind == 0 else wdkvt)[
                                    :, fi * 256 + co * 128: fi * 256 + (co + 1) * 128],
                                y2g_sb2[gb][:, fi * TT:(fi + 1) * TT],
                                start=(fi == 0), stop=(fi == NCH - 1)))
                            if ci == 3:
                                y2g_rd[gb][fi] = tmm
                        stop4[ci] = tmm
                    for ci in range(4):
                        kind, co = ci // 2, ci % 2
                        pt, off = pjbank[ci]
                        sb = ybufs[ci % 2]
                        ysem = dmaY if ci % 2 == 0 else dmaY2
                        dve.wait(stop4[ci])
                        dve.wait(ysem.tok)
                        tev = s_dve.tick(nc.vector.tensor_copy(
                            sb[:], pt[:, off:off + TT]))
                        bank_ev[ci] = tev
                        syn.wait(tev)
                        sdma(ysem,
                             bass.AP(b_c_in[g // 2],
                                     (kind * 2 + co) * 128 * 2 * TT
                                     + (g % 2) * TT,
                                     [[2 * TT, 128], [1, TT]]),
                             sb[:])

                def emit_y2(g):
                    g0 = g * TT
                    pe.wait([tal[g - 1], tbl[g - 1]] if g >= 1 else None)
                    pe.wait(t6s[g])
                    pe.wait(t_ones)
                    bm = s_pe.tick(nc.tensor.matmul(
                        pD[:, 0:TT], onesr[:, 0:128], mean_row[:, g0:g0 + TT],
                        start=True, stop=True))
                    br = s_pe.tick(nc.tensor.matmul(
                        pD[:, 512:512 + TT], onesr[:, 0:128],
                        rstd_row[:, g0:g0 + TT], start=True, stop=True))
                    ta = tb = None
                    for chh in range(NCH):
                        dve.wait([bm, br])
                        dve.wait(tb)
                        ta = s_dve.tick(nc.vector.tensor_tensor(
                            t1f[:], X[:, chh * T + g0: chh * T + g0 + TT],
                            pD[:, 0:TT], ALU.subtract))
                        dve.wait(ta)
                        dve.wait(y2g_rd[g % 2][chh])
                        tb = s_dve.tick(nc.vector.tensor_tensor(
                            y2g_sb2[g % 2][:, chh * TT:(chh + 1) * TT], t1f[:],
                            pD[:, 512:512 + TT], ALU.mult))
                        tbg[g][chh] = tb
                    tal[g] = ta
                    tbl[g] = tb

                t_cag = [None, None]

                def fire_cag(hf):
                    pool.wait(dmaY.tok)
                    pool.wait(dmaY2.tok)
                    t_cag[hf] = ccy[hf].tick(nc.gpsimd.collective_compute(
                        "AllGather", ALU.bypass, replica_groups=PAIRS,
                        ins=[b_c_in[hf].ap()], outs=[b_c_out[hf].ap()]))

                for g in range(NT):
                    t6s[g] = (ln_stats0(g) if g <= 1 else
                              ln_stats_group(g, xbf, sq, var_row, lnst))
                    emit_y2(g - 1) if g >= 1 else None
                    if g >= 2:
                        emit_proj(g - 2)
                    if g == 3:
                        fire_cag(0)
                prev_rstd[0] = t6s[NT - 1]
                emit_y2(NT - 1)
                emit_proj(NT - 2)
                emit_proj(NT - 1)
                fire_cag(1)

                sl = 0
                for hf in range(2):
                    for rank, kind, co in ((r, k, c) for r in range(2)
                                           for k in range(2) for c in range(2)):
                        syn.wait(t_cag[hf])
                        syn.wait(slot_rd[sl % 4])
                        dst = cq_sb if kind == 0 else ckv_sb
                        sdma(slotY[sl % 4],
                             dst[:, co * TFULL + rank * T + hf * 2 * TT:
                                 co * TFULL + rank * T + (hf + 1) * 2 * TT],
                             bass.AP(b_c_out[hf],
                                     (rank * 4 + kind * 2 + co) * 128 * 2 * TT,
                                     [[2 * TT, 128], [1, 2 * TT]]))
                        slot_rd[sl % 4] = slotY[sl % 4].tok
                        sl += 1
                for s_ in slotY:
                    pe.wait((s_, s_.c))
                barrier()
                prj.close()

                plp = contextlib.ExitStack()
                lsbt = lambda n, s, d: plp.enter_context(nc.sbuf_tensor(n, s, d))
                kpair = lsbt("kpair", [128, TFULL], BF16)
                qpair = lsbt("qpair", [128, TFULL], BF16)
                vhA = lsbt("vhA", [128, 32 * 65], BF16)
                vhB = lsbt("vhB", [128, 32 * 65], BF16)
                expb2 = [[lsbt(f"exp{h}{par}", [128, 1024], BF16)
                          for par in range(2)] for h in range(2)]
                slab_t = lsbt("slab", [128, TFULL], BF16)
                recT = lsbt("recT", [65, 1024], F32)
                recB = lsbt("recB", [65, 1024], BF16)
                rbc4 = lsbt("rbc4", [64, 1024], F32)

                vA_ones = vhA[:].rearrange("p (c k) -> p c k", k=65)[:, :, 64:65]
                vB_ones = vhB[:].rearrange("p (c k) -> p c k", k=65)[:, :, 64:65]
                t_vo = [s_pool.tick(nc.gpsimd.memset(vA_ones, 1.0)),
                        s_pool.tick(nc.gpsimd.memset(vB_ones, 1.0))]

                score_ps = [pA, pB]
                av_ps = [pC, pD]
                sbl = {}
                avl = {}
                ebl = {}
                rec_free = [None] * 2
                rbc_free = [None] * 2
                slab_store = None
                dve.wait(ld_w)
                pe.wait(ld_attw2)
                for p in range(4):
                    kev_g = [sbl.get((0, 0)), sbl.get((0, 1))]
                    for sub in range(8):
                        pt, off = (pA, 0) if sub % 2 == 0 else (pA, 512)
                        pe.wait(kev_g[sub % 2])
                        for a in range(2):
                            tkm = s_pe.tick(nc.tensor.matmul(
                                pt[:, off:off + TT],
                                wukt[:, a * 512 + p * 128: a * 512 + (p + 1) * 128],
                                ckv_sb[:, a * TFULL + sub * TT: a * TFULL + (sub + 1) * TT],
                                start=(a == 0), stop=(a == 1)))
                        dve.wait(tkm)
                        kev_g[sub % 2] = s_dve.tick(nc.vector.tensor_copy(
                            kpair[:, sub * TT:(sub + 1) * TT], pt[:, off:off + TT]))
                    vev_g = [sbl.get((1, 0)), sbl.get((1, 1))]
                    for tc in range(32):
                        pt, off = (pB, 0) if tc % 2 == 0 else (pB, 512)
                        pe.wait(vev_g[tc % 2])
                        for a in range(2):
                            tvm = s_pe.tick(nc.tensor.matmul(
                                pt[:, off:off + 128],
                                ckv_sb[:, a * TFULL + tc * 128: a * TFULL + (tc + 1) * 128],
                                wuvt[:, a * 512 + p * 128: a * 512 + (p + 1) * 128],
                                start=(a == 0), stop=(a == 1)))
                        dve.wait(tvm)
                        s_dve.tick(nc.vector.tensor_copy(
                            vhA[:, tc * 65: tc * 65 + 64], pt[:, off:off + 64]))
                        vev_g[tc % 2] = s_dve.tick(nc.vector.tensor_copy(
                            vhB[:, tc * 65: tc * 65 + 64], pt[:, off + 64:off + 128]))
                    qev_g = [kev_g[0], kev_g[1]]
                    for sub in range(8):
                        pt, off = (pA, 0) if sub % 2 == 0 else (pA, 512)
                        pe.wait(qev_g[sub % 2])
                        for a in range(2):
                            tqm = s_pe.tick(nc.tensor.matmul(
                                pt[:, off:off + TT],
                                wuqt[:, a * 512 + p * 128: a * 512 + (p + 1) * 128],
                                cq_sb[:, a * TFULL + sub * TT: a * TFULL + (sub + 1) * TT],
                                start=(a == 0), stop=(a == 1)))
                        dve.wait(tqm)
                        qev_g[sub % 2] = s_dve.tick(nc.vector.tensor_copy(
                            qpair[:, sub * TT:(sub + 1) * TT], pt[:, off:off + TT]))
                    pe.wait(t_vo)
                    sbl[(0, 0)] = qev_g[0]
                    sbl[(0, 1)] = qev_g[1]
                    sbl[(1, 0)] = vev_g[0]
                    sbl[(1, 1)] = vev_g[1]

                    for rnd in range(4):
                        tA = 2 * rnd
                        tB = 2 * rnd + 1
                        q0A = tA * 512
                        q0B = tB * 512
                        nchkA = q0A // 128 + 4
                        nchkB = q0B // 128 + 4
                        last_av = {}
                        pend = None

                        def emit_avs(c, texp_c):
                            k0 = c * 128
                            for h in range(2):
                                vh = vhA if h == 0 else vhB
                                eb = expb2[h][c % 2]
                                avts = []
                                if c < nchkA:
                                    cstA = max(0, k0 - q0A)
                                    pe.wait(texp_c[(h, 0)])
                                    if c == 0:
                                        pe.wait(avl.get((h, 0)))
                                    avts.append(s_pe.tick(nc.tensor.matmul(
                                        av_ps[h][0:65, cstA:512],
                                        vh[:, c * 65:(c + 1) * 65],
                                        eb[:, cstA:512],
                                        start=(c == 0), stop=(c == nchkA - 1))))
                                    last_av[(h, 0)] = avts[-1]
                                cstB = max(0, k0 - q0B)
                                pe.wait(texp_c[(h, 1)])
                                if c == 0:
                                    pe.wait(avl.get((h, 1)))
                                avts.append(s_pe.tick(nc.tensor.matmul(
                                    av_ps[h][0:65, 512 + cstB:1024],
                                    vh[:, c * 65:(c + 1) * 65],
                                    eb[:, 512 + cstB:1024],
                                    start=(c == 0), stop=(c == nchkB - 1))))
                                last_av[(h, 1)] = avts[-1]
                                ebl[(h, c % 2)] = avts

                        for c in range(nchkB):
                            k0 = c * 128
                            texp_c = {}
                            for h in range(2):
                                r0 = h * 64
                                if c < nchkA:
                                    cstA = max(0, k0 - q0A)
                                    pe.wait(sbl.get((h, 0)))
                                    s_pe.tick(nc.tensor.matmul(
                                        score_ps[h][:, cstA:512],
                                        kpair[r0:r0 + 64, k0:k0 + 128],
                                        qpair[r0:r0 + 64, q0A + cstA:q0A + 512],
                                        start=True, stop=True))
                                cstB = max(0, k0 - q0B)
                                pe.wait(sbl.get((h, 1)))
                                sc = s_pe.tick(nc.tensor.matmul(
                                    score_ps[h][:, 512 + cstB:1024],
                                    kpair[r0:r0 + 64, k0:k0 + 128],
                                    qpair[r0:r0 + 64, q0B + cstB:q0B + 512],
                                    start=True, stop=True))
                                lo = max(0, k0 - q0A) if c < nchkA else 512 + cstB
                                eb = expb2[h][c % 2]
                                act.wait(sc)
                                act.wait(ebl.get((h, c % 2)))
                                texp = s_act.tick(nc.scalar.activation(
                                    eb[:, lo:1024], score_ps[h][:, lo:1024],
                                    AF.Exp, scale=0.125))
                                sbl[(h, 0)] = texp if c < nchkA else sbl.get((h, 0))
                                sbl[(h, 1)] = texp
                                mka = mkb = texp
                                if c < nchkA and k0 >= q0A:
                                    dve.wait(texp)
                                    mka = s_dve.tick(nc.vector.tensor_tensor(
                                        eb[:, cstA:cstA + 128], eb[:, cstA:cstA + 128],
                                        triu[:], ALU.mult))
                                if k0 >= q0B:
                                    dve.wait(texp)
                                    mkb = s_dve.tick(nc.vector.tensor_tensor(
                                        eb[:, 512 + cstB:512 + cstB + 128],
                                        eb[:, 512 + cstB:512 + cstB + 128],
                                        triu[:], ALU.mult))
                                texp_c[(h, 0)] = mka
                                texp_c[(h, 1)] = mkb
                            if pend is not None:
                                emit_avs(*pend)
                            pend = (c, texp_c)
                        emit_avs(*pend)
                        streams = [(tA, 0), (tA, 1), (tB, 0), (tB, 1)]
                        for g2 in range(2):
                            tln = {}
                            tex = {}
                            tbc = {}
                            trbs = {}
                            grp = [(g2 * 2 + j, streams[g2 * 2 + j]) for j in range(2)]
                            for si, (t, h) in grp:
                                ti = t % 2
                                off = ti * 512
                                act.wait(last_av[(h, ti)])
                                act.wait(rec_free[si % 2])
                                tln[si] = s_act.tick(nc.scalar.activation(
                                    recT[64:65, (si % 2) * 512:(si % 2) * 512 + 512],
                                    av_ps[h][64:65, off:off + 512], AF.Ln))
                            for si, (t, h) in grp:
                                act.wait(tln[si])
                                tex[si] = s_act.tick(nc.scalar.activation(
                                    recB[64:65, (si % 2) * 512:(si % 2) * 512 + 512],
                                    recT[64:65, (si % 2) * 512:(si % 2) * 512 + 512],
                                    AF.Exp, scale=-1.0))
                            for si, (t, h) in grp:
                                ti = t % 2
                                off = ti * 512
                                pe.wait(tex[si])
                                pe.wait(sbl[(h, ti)])
                                tbc[si] = s_pe.tick(nc.tensor.matmul(
                                    score_ps[h][0:64, off:off + 512],
                                    ones64b[64:65, 0:64],
                                    recB[64:65, (si % 2) * 512:(si % 2) * 512 + 512],
                                    start=True, stop=True))
                                rec_free[si % 2] = tbc[si]
                            for si, (t, h) in grp:
                                ti = t % 2
                                off = ti * 512
                                dve.wait(tbc[si])
                                dve.wait(rbc_free[si % 2])
                                trbs[si] = s_dve.tick(nc.vector.tensor_copy(
                                    rbc4[0:64, (si % 2) * 512:(si % 2) * 512 + 512],
                                    score_ps[h][0:64, off:off + 512]))
                                sbl[(h, ti)] = trbs[si]
                            for si, (t, h) in grp:
                                ti = t % 2
                                off = ti * 512
                                q0 = t * 512
                                dve.wait(trbs[si])
                                dve.wait(slab_store)
                                tsl = s_dve.tick(nc.vector.tensor_tensor(
                                    slab_t[h * 64:h * 64 + 64, q0:q0 + 512],
                                    av_ps[h][0:64, off:off + 512],
                                    rbc4[0:64, (si % 2) * 512:(si % 2) * 512 + 512],
                                    ALU.mult))
                                rbc_free[si % 2] = tsl
                                avl[(h, ti)] = tsl
                    syn.wait(s_dve.tok)
                    sdma(dmaSL, b_at_in[p].ap(), slab_t[:])
                    slab_store = dmaSL.tok
                    if p < 3:
                        pool.wait(slab_store)
                        cc[2 + p].tick(nc.gpsimd.collective_compute(
                            "AllGather", ALU.bypass, replica_groups=PAIRS,
                            ins=[b_at_in[p].ap()], outs=[b_at_out[p].ap()]))
                    else:
                        slab3_store = slab_store
                barrier()
                plp.close()
                cqs_stack.close()

                with contextlib.ExitStack() as wst:
                    wsb = lambda n, s, d: wst.enter_context(nc.sbuf_tensor(n, s, d))
                    wot = wsb("wot", [128, 8 * 1024], BF16)
                    xbf3 = [wsb("w_xbf0", [128, TT], BF16)[:], wsb("w_xbf1", [128, TT], BF16)[:]]
                    sq3 = [wsb("w_sq0", [128, TT], BF16)[:], wsb("w_sq1", [128, TT], BF16)[:]]
                    var3 = wsb("w_var", [1, TT], F32)
                    scraps3 = [wsb("w_scr0", [128, TT], F32),
                               wsb("w_scr1", [128, TT], F32)]
                    gfull = wsb("gfull", [128, 8 * T], BF16)
                    ld_woc = [None] * 8
                    for hdc in range(8):
                        ld_woc[hdc] = gload(
                            dmaWC[hdc], wot[:, hdc * 1024:(hdc + 1) * 1024],
                            bass.AP(wo.tensor, hdc * 128 * 1024,
                                    [[1024, 128], [1, 1024]]))
                    pool.wait(ld_small)
                    gtok = [None] * 8

                    def emit_gather(p, r):
                        hdc = r * 4 + p
                        gsem = dmaG8[hdc]
                        pool.wait(cc[2 + p].tok)
                        pool.eq.indirect_dma_start(
                            out=gfull[:, hdc * T:(hdc + 1) * T], out_offset=None,
                            in_=b_at_out[p].ap(),
                            in_offset=bass.IndirectOffsetOnAxis(
                                ap=gidx[:, r:r + 1], axis=0),
                        ).then_inc(gsem.h, 16)
                        gsem.c += 16
                        gtok[hdc] = (gsem, gsem.c)

                    for p, r in ((0, 0), (1, 0), (2, 0), (0, 1), (1, 1),
                                 (2, 1)):
                        emit_gather(p, r)
                    pool.wait(slab3_store)
                    cc[5].tick(nc.gpsimd.collective_compute(
                        "AllGather", ALU.bypass, replica_groups=PAIRS,
                        ins=[b_at_in[3].ap()], outs=[b_at_out[3].ap()]))
                    emit_gather(3, 0)
                    emit_gather(3, 1)
                    bank_last = [None] * 4
                    wbank8 = [(pA, 0), (pA, 512), (pB, 0), (pB, 512)]
                    HDC_ORDER = [0, 1, 2, 4, 5, 6, 3, 7]
                    ln3 = {"xbf_free": [None, None], "sq_free": [None, None],
                           "t6": prev_rstd[0]}
                    passA_begin()
                    rdy_q = None
                    for qs in range(4):
                        for fop in range(2):
                            wo_stop = [None] * 4
                            for hi, hdc in enumerate(HDC_ORDER):
                                pe.wait(gtok[hdc])
                                pe.wait(ld_woc[hdc])
                                for fo4 in range(4):
                                    fo = fop * 4 + fo4
                                    if hi == 0:
                                        pe.wait(bank_last[fo4])
                                    tmm = s_pe.tick(nc.tensor.matmul(
                                        wbank8[fo4][0][:, wbank8[fo4][1]:wbank8[fo4][1] + 512],
                                        wot[:, hdc * 1024 + fo * 128: hdc * 1024 + (fo + 1) * 128],
                                        gfull[:, hdc * T + qs * 512: hdc * T + (qs + 1) * 512],
                                        start=(hi == 0), stop=(hi == 7)))
                                    if hi == 7:
                                        wo_stop[fo4] = tmm
                            for fo4 in range(4):
                                fo = fop * 4 + fo4
                                dve.wait(wo_stop[fo4])
                                bank_last[fo4] = s_dve.tick(nc.vector.tensor_tensor(
                                    X[:, fo * T + qs * 512: fo * T + (qs + 1) * 512],
                                    X[:, fo * T + qs * 512: fo * T + (qs + 1) * 512],
                                    wbank8[fo4][0][:, wbank8[fo4][1]:wbank8[fo4][1] + 512],
                                    ALU.add))
                        t6q = ln_stats_group(qs, xbf3, sq3, var3, ln3)
                        prev_rstd[0] = t6q
                        if qs > 0:
                            passA_group(qs - 1, scraps3, sq3, var3, rdy_q, t6q)
                            ln3["sq_free"] = [
                                [ln3["sq_free"][0], pa_state["ttr"][0]],
                                [ln3["sq_free"][1], pa_state["ttr"][1]]]
                        rdy_q = t6q
                    passA_group(3, scraps3, sq3, var3, rdy_q, rdy_q)
                    barrier()

            ctx_ld2 = passA_finalize(1)
            ffn(1, stats_done=True, pre_ctx=ctx_ld2)

            with nc.sbuf_tensor("xout", [128, 16 * 1024], F32) as xout:
                bank_free = [None] * 8
                k = 0
                for i in range(16):
                    for ch in range(NCH):
                        pt, off = PBANKS[k % 8]
                        pe.wait(bank_free[k % 8])
                        ttr = s_pe.tick(nc.tensor.transpose(
                            pt[:, off:off + 128],
                            X[:, ch * T + i * 128: ch * T + (i + 1) * 128],
                            ident[:]))
                        dve.wait(ttr)
                        bank_free[k % 8] = s_dve.tick(nc.vector.tensor_copy(
                            xout[:, i * 1024 + ch * 128: i * 1024 + (ch + 1) * 128],
                            pt[:, off:off + 128]))
                        k += 1
                    if ch == NCH - 1 and i % 4 == 3:
                        syn.wait(bank_free[(k - 1) % 8])
                        sdma(dmaO,
                             bass.AP(out.tensor, (i - 3) * 128 * D,
                                     [[D, 128], [128 * D, 4], [1, D]]),
                             xout[:, (i - 3) * 1024:(i + 1) * 1024].rearrange(
                                 "p (a n) -> p a n", a=4))
                syn.wait(dmaO.tok)

    return nc


_CACHED = {}


def kernel(L0_attn_x, L0_attn_W_dq, L0_attn_W_uq, L0_attn_W_dkv, L0_attn_W_uk,
           L0_attn_W_uv, L0_attn_W_o, L0_ffn_cn_w1, L0_ffn_cn_b1, L0_ffn_cn_w2,
           L0_ffn_cn_b2, L0_ffn_cn_w3, L0_ffn_cn_b3, L0_ffn_cn_w4, L0_ffn_cn_b4,
           L0_ffn_gc_w1, L0_ffn_gc_b1, L0_ffn_gc_w2, L0_ffn_gc_b2, L0_ffn_gc_w3,
           L0_ffn_gc_b3, L0_ffn_adaptive_range):
    if "nc" not in _CACHED:
        _CACHED["nc"] = build_program()
    nc = _CACHED["nc"]

    x = np.asarray(L0_attn_x, dtype=np.float32)
    c = np.ascontiguousarray
    common = {
        "wdq": c(np.asarray(L0_attn_W_dq, np.float32)),
        "wdkv": c(np.asarray(L0_attn_W_dkv, np.float32)),
        "wo": c(np.asarray(L0_attn_W_o, np.float32)),
        "w1": c(np.asarray(L0_ffn_cn_w1, np.float32)),
        "w2": c(np.asarray(L0_ffn_cn_w2, np.float32)),
        "w3": c(np.asarray(L0_ffn_cn_w3, np.float32)),
        "w4r": c(np.asarray(L0_ffn_cn_w4, np.float32).reshape(2, 128).T),
        "gw1": c(np.asarray(L0_ffn_gc_w1, np.float32)),
        "gw2": c(np.asarray(L0_ffn_gc_w2, np.float32)),
        "gw3r": c(np.asarray(L0_ffn_gc_w3, np.float32).reshape(2, 128).T),
        "b1r": c(np.asarray(L0_ffn_cn_b1, np.float32).reshape(8, 128).T),
        "b2r": c(np.asarray(L0_ffn_cn_b2, np.float32).reshape(4, 128).T),
        "b3r": c(np.asarray(L0_ffn_cn_b3, np.float32).reshape(2, 128).T),
        "b4": c(np.asarray(L0_ffn_cn_b4, np.float32).reshape(1, 1)),
        "gb1r": c(np.asarray(L0_ffn_gc_b1, np.float32).reshape(4, 128).T),
        "gb2r": c(np.asarray(L0_ffn_gc_b2, np.float32).reshape(2, 128).T),
        "gb3": c(np.asarray(L0_ffn_gc_b3, np.float32).reshape(1, 1)),
        "ar_in": c(np.asarray(L0_ffn_adaptive_range, np.float32).reshape(1, 1)),
        "ident_in": np.eye(128, dtype=np.float32),
        "triu_in": np.triu(np.ones((128, 128), dtype=np.float32)),
    }
    wuq_f = np.asarray(L0_attn_W_uq, np.float32)
    wuk_f = np.asarray(L0_attn_W_uk, np.float32)
    wuv_f = np.asarray(L0_attn_W_uv, np.float32)

    in_maps = []
    for core in range(8):
        b, j = core // 2, core % 2
        m = dict(common)
        m["xo"] = c(x[b, j * T:(j + 1) * T, :])
        m["wuq"] = c(wuq_f[:, j * 512:(j + 1) * 512])
        m["wuk"] = c(wuk_f[:, j * 512:(j + 1) * 512])
        m["wuv"] = c(wuv_f[:, j * 512:(j + 1) * 512])
        gi = np.zeros((128, 2), dtype=np.uint32)
        for r in range(2):
            gi[:, r] = r * 256 + np.arange(128, dtype=np.uint32) * 2 + j
        m["gidx_in"] = gi
        in_maps.append(m)

    res = run_bass_kernel_spmd(nc, in_maps, core_ids=list(range(8)))
    outp = np.empty((4, TFULL, D), dtype=np.float32)
    for core in range(8):
        b, j = core // 2, core % 2
        outp[b, j * T:(j + 1) * T, :] = res.results[core]["out"]
    return outp


# version 88
# speedup vs baseline: 1.0262x; 1.0262x over previous
import numpy as np
import contextlib
import concourse.bass as bass
import concourse.mybir as mybir
from concourse.bass_utils import run_bass_kernel_spmd

F32 = mybir.dt.float32
BF16 = mybir.dt.bfloat16
U32 = mybir.dt.uint32
AF = mybir.ActivationFunctionType
ALU = mybir.AluOpType
AXX = mybir.AxisListType.X

D = 1024
T = 2048
TFULL = 4096
NCH = 8
TT = 512
NT = T // TT
DC = 256
TQ = 1024
NQT = TFULL // TQ
PAIRS = [[0, 1], [2, 3], [4, 5], [6, 7]]


class Sem:
    def __init__(self, h):
        self.h = h
        self.c = 0

    def tick(self, inst, n=1):
        inst.then_inc(self.h, n)
        self.c += n
        return (self, self.c)

    @property
    def tok(self):
        return (self, self.c)


class Q:
    def __init__(self, eq):
        self.eq = eq
        self.last = {}

    def wait(self, tok):
        if tok is None:
            return
        if isinstance(tok, list):
            for t in tok:
                self.wait(t)
            return
        s, v = tok
        assert v <= s.c, "wait on a not-yet-emitted token (would deadlock)"
        if v <= 0 or self.last.get(id(s), 0) >= v:
            return
        self.eq.wait_ge(s.h, v)
        self.last[id(s)] = v


def build_program():
    nc = bass.Bass("TRN2", target_bir_lowering=False)

    xo = nc.dram_tensor("xo", [T, D], F32, kind="ExternalInput").ap()
    wdq = nc.dram_tensor("wdq", [D, DC], F32, kind="ExternalInput").ap()
    wdkv = nc.dram_tensor("wdkv", [D, DC], F32, kind="ExternalInput").ap()
    wuq = nc.dram_tensor("wuq", [DC, 512], F32, kind="ExternalInput").ap()
    wuk = nc.dram_tensor("wuk", [DC, 512], F32, kind="ExternalInput").ap()
    wuv = nc.dram_tensor("wuv", [DC, 512], F32, kind="ExternalInput").ap()
    wo = nc.dram_tensor("wo", [D, D], F32, kind="ExternalInput").ap()
    w1 = nc.dram_tensor("w1", [D, D], F32, kind="ExternalInput").ap()
    w2 = nc.dram_tensor("w2", [D, 512], F32, kind="ExternalInput").ap()
    w3 = nc.dram_tensor("w3", [512, 256], F32, kind="ExternalInput").ap()
    w4r = nc.dram_tensor("w4r", [128, 2], F32, kind="ExternalInput").ap()
    gw1 = nc.dram_tensor("gw1", [D, 512], F32, kind="ExternalInput").ap()
    gw2 = nc.dram_tensor("gw2", [512, 256], F32, kind="ExternalInput").ap()
    gw3r = nc.dram_tensor("gw3r", [128, 2], F32, kind="ExternalInput").ap()
    b1r = nc.dram_tensor("b1r", [128, 8], F32, kind="ExternalInput").ap()
    b2r = nc.dram_tensor("b2r", [128, 4], F32, kind="ExternalInput").ap()
    b3r = nc.dram_tensor("b3r", [128, 2], F32, kind="ExternalInput").ap()
    b4 = nc.dram_tensor("b4", [1, 1], F32, kind="ExternalInput").ap()
    gb1r = nc.dram_tensor("gb1r", [128, 4], F32, kind="ExternalInput").ap()
    gb2r = nc.dram_tensor("gb2r", [128, 2], F32, kind="ExternalInput").ap()
    gb3 = nc.dram_tensor("gb3", [1, 1], F32, kind="ExternalInput").ap()
    ar_in = nc.dram_tensor("ar_in", [1, 1], F32, kind="ExternalInput").ap()
    ident_in = nc.dram_tensor("ident_in", [128, 128], F32, kind="ExternalInput").ap()
    triu_in = nc.dram_tensor("triu_in", [128, 128], F32, kind="ExternalInput").ap()
    gidx_in = nc.dram_tensor("gidx_in", [128, 2], U32, kind="ExternalInput").ap()
    out = nc.dram_tensor("out", [T, D], F32, kind="ExternalOutput").ap()

    b_ctx_in = [nc.dram_tensor(f"b_ctx_in{i}", [128, 9], F32) for i in range(2)]
    b_ctx_out = [nc.dram_tensor(f"b_ctx_out{i}", [128, 9], F32) for i in range(2)]
    b_c_in = [nc.dram_tensor(f"b_c_in{i}", [4 * 128, 2 * TT], BF16)
              for i in range(2)]
    b_c_out = [nc.dram_tensor(f"b_c_out{i}", [8 * 128, 2 * TT], BF16)
               for i in range(2)]
    b_cq = nc.dram_tensor("b_cq", [DC, TFULL], BF16)
    b_ckv = nc.dram_tensor("b_ckv", [DC, TFULL], BF16)
    b_at_in = [nc.dram_tensor(f"b_at_in{p}", [128, TFULL], BF16) for p in range(4)]
    b_at_out = [nc.dram_tensor(f"b_at_out{p}", [512, T], BF16) for p in range(4)]

    st = contextlib.ExitStack()
    with st:
        sbt = lambda n, s, d: st.enter_context(nc.sbuf_tensor(n, s, d))
        pst = lambda n, s: st.enter_context(nc.psum_tensor(n, s, F32))
        sem = lambda n: Sem(st.enter_context(nc.semaphore(n)))

        X = sbt("X", [128, NCH * T], F32)
        ident = sbt("ident", [128, 128], F32)
        triu = sbt("triu", [128, 128], BF16)
        onesc = sbt("onesc", [128, 1], BF16)
        onesr = sbt("onesr", [1, 128], F32)
        onesrb = sbt("onesrb", [1, 128], BF16)
        w1t = sbt("w1t", [128, 8 * 1024], BF16)
        w2t = sbt("w2t", [128, 8 * 512], BF16)
        w3t = sbt("w3t", [128, 4 * 256], BF16)
        w4t = sbt("w4t", [128, 2], BF16)
        gw1t = sbt("gw1t", [128, 8 * 512], BF16)
        gw2t = sbt("gw2t", [128, 4 * 256], BF16)
        gw3t = sbt("gw3t", [128, 2], BF16)
        b1t = sbt("b1t", [128, 8], F32)
        b2t = sbt("b2t", [128, 4], F32)
        b3t = sbt("b3t", [128, 2], F32)
        gb1t = sbt("gb1t", [128, 4], F32)
        gb2t = sbt("gb2t", [128, 2], F32)
        ar_t = sbt("ar_t", [1, 1], F32)
        b4_t = sbt("b4_t", [1, 1], F32)
        gb3_t = sbt("gb3_t", [1, 1], F32)
        hb4_t = sbt("hb4_t", [1, 1], F32)
        hgb3_t = sbt("hgb3_t", [1, 1], F32)
        eps_t = sbt("eps_t", [1, 1], F32)
        arx_t = sbt("arx_t", [1, 1], F32)
        tgf_t = sbt("tgf_t", [1, 1], F32)
        ab_row = sbt("ab_row", [1, 2], F32)
        AB = sbt("AB", [128, 2], F32)
        ctx_acc = sbt("ctx_acc", [128, 9], F32)
        ctx_parts = sbt("ctx_parts", [128, 32], F32)
        corr_t = sbt("corr_t", [1, 1], F32)
        corr_red = sbt("corr_red", [1, 1], F32)
        corrb = sbt("corrb", [128, 1], F32)
        ones64 = sbt("ones64", [65, 128], F32)
        ones64b = sbt("ones64b", [65, 128], BF16)
        ctx_bf = sbt("ctx_bf", [128, 8], BF16)
        g1_bf = sbt("g1_bf", [128, 4], BF16)
        g2_bf = sbt("g2_bf", [128, 2], BF16)
        mean_row = sbt("mean_row", [1, T], F32)
        rstd_row = sbt("rstd_row", [1, T], F32)
        gidx = sbt("gidx", [128, 2], U32)

        pA = pst("pA", [128, 1024])
        pB = pst("pB", [128, 1024])
        pC = pst("pC", [128, 1024])
        pD = pst("pD", [128, 1024])

        s_pe = sem("s_pe")
        s_act = sem("s_act")
        s_dve = sem("s_dve")
        s_pool = sem("s_pool")
        dmaS = sem("dmaS")
        dmaY = sem("dmaY")
        dmaY2 = sem("dmaY2")
        dmaG8 = [sem(f"dmaG8_{i}") for i in range(8)]
        dmaWC = [sem(f"dmaWC_{i}") for i in range(8)]
        dmaC = sem("dmaC")
        dmaO = sem("dmaO")
        dmaSL = sem("dmaSL")
        dmaGT = sem("dmaGT")
        dmaGT2 = sem("dmaGT2")
        dmaW = sem("dmaW")
        dmaW2 = sem("dmaW2")
        dmaW4 = sem("dmaW4")
        dmaW3 = sem("dmaW3")
        slotY = [sem(f"slY{i}") for i in range(6)]
        slotC = [sem(f"slC{i}") for i in range(2)]
        cc = [sem(f"cc{i}") for i in range(7)]
        ccy = [sem(f"ccy{g}") for g in range(4)]

        blk = st.enter_context(nc.Block())

        @blk.gpsimd
        def _(gq):
            pe = Q(nc.tensor)
            act = Q(nc.scalar)
            dve = Q(nc.vector)
            pool = Q(nc.gpsimd)
            syn = Q(nc.sync)
            engines = [(pe, s_pe), (act, s_act), (dve, s_dve), (pool, s_pool)]

            def barrier():
                toks = [s.tok for (_, s) in engines]
                for (q, s) in engines:
                    for t in toks:
                        if t[0] is not s:
                            q.wait(t)

            def sdma(sem_, dst, src, q=syn):
                q.eq.dma_start(out=dst, in_=src).then_inc(sem_.h, 16)
                sem_.c += 16
                return (sem_, sem_.c)

            for dst, src in ((ident[:], ident_in), (b1t[:], b1r), (b2t[:], b2r),
                             (b3t[:], b3r), (gb1t[:], gb1r), (gb2t[:], gb2r),
                             (ar_t[:], ar_in), (b4_t[:], b4), (gb3_t[:], gb3),
                             (gidx[:], gidx_in)):
                sdma(dmaS, dst, src)
            ld_small = dmaS.tok

            def gload(sem_, dst_ap, src_ap):
                pool.eq.dma_start(out=dst_ap, in_=src_ap).then_inc(sem_.h, 16)
                sem_.c += 16
                return (sem_, sem_.c)

            def wview(ap_, p, a, n):
                return bass.AP(ap_.tensor, 0, [[n, p], [p * n, a], [1, n]])

            gload(dmaW, triu[:], triu_in)
            gload(dmaW, w1t[:].rearrange("p (a n) -> p a n", a=8), wview(w1, 128, 8, 1024))
            gload(dmaW, w2t[:].rearrange("p (a n) -> p a n", a=8), wview(w2, 128, 8, 512))
            gload(dmaW, gw1t[:].rearrange("p (a n) -> p a n", a=8), wview(gw1, 128, 8, 512))
            gload(dmaW, w3t[:].rearrange("p (a n) -> p a n", a=4), wview(w3, 128, 4, 256))
            gload(dmaW, gw2t[:].rearrange("p (a n) -> p a n", a=4), wview(gw2, 128, 4, 256))
            gload(dmaW, w4t[:], w4r)
            gload(dmaW, gw3t[:], gw3r)
            ld_w = dmaW.tok

            t_on1 = s_pool.tick(pool.eq.memset(onesc[:], 1.0))
            t_on2 = s_pool.tick(pool.eq.memset(onesr[:], 1.0))
            t_on2b = s_pool.tick(pool.eq.memset(onesrb[:], 1.0))
            t_on64 = s_pool.tick(pool.eq.memset(ones64[:], 1.0))
            t_on64b = s_pool.tick(pool.eq.memset(ones64b[:], 1.0))
            t_eps = s_pool.tick(pool.eq.memset(eps_t[:], 1.0e-5))
            t_ones = [t_on1, t_on2, t_on2b, t_eps]
            dve.wait(ld_small)
            t_hb4 = s_dve.tick(nc.vector.tensor_scalar(hb4_t[:], b4_t[:], 0.5, None, ALU.mult))
            t_hgb3 = s_dve.tick(nc.vector.tensor_scalar(hgb3_t[:], gb3_t[:], 0.5, None, ALU.mult))
            t_arx = s_dve.tick(nc.vector.tensor_scalar(arx_t[:], ar_t[:], 0.7, None, ALU.mult))

            PBANKS = [(pA, 0), (pA, 512), (pB, 0), (pB, 512),
                      (pC, 0), (pC, 512), (pD, 0), (pD, 512)]

            barrier()

            def ln_stats_group(g, xbf, sq, var_row, st_):
                g0 = g * TT
                pe.wait(t_ones)
                mm_m = mm_s = None
                for chh in range(NCH):
                    bi = chh % 2
                    dve.wait(st_["xbf_free"][bi])
                    tx = s_dve.tick(nc.vector.tensor_copy(
                        xbf[bi], X[:, chh * T + g0: chh * T + g0 + TT]))
                    act.wait(tx)
                    act.wait(st_["sq_free"][bi])
                    tsq = s_act.tick(nc.scalar.activation(
                        sq[bi], xbf[bi], AF.Square))
                    pe.wait(tx)
                    mm_m = s_pe.tick(nc.tensor.matmul(
                        pC[0:1, 0:TT], onesc[:], xbf[bi],
                        start=(chh == 0), stop=(chh == NCH - 1)))
                    st_["xbf_free"][bi] = [mm_m, tsq]
                    pe.wait(tsq)
                    mm_s = s_pe.tick(nc.tensor.matmul(
                        pC[0:1, 512:512 + TT], onesc[:], sq[bi],
                        start=(chh == 0), stop=(chh == NCH - 1)))
                    st_["sq_free"][bi] = mm_s
                dve.wait(mm_m)
                t1 = s_dve.tick(nc.vector.tensor_scalar(
                    mean_row[:, g0:g0 + TT], pC[0:1, 0:TT], 1.0 / D, None, ALU.mult))
                dve.wait(mm_s)
                dve.wait(st_["t6"])
                t2 = s_dve.tick(nc.vector.tensor_scalar(
                    var_row[:, 0:TT], pC[0:1, 512:512 + TT], 1.0 / D, None, ALU.mult))
                dve.wait([t1, t2])
                t3 = s_dve.tick(nc.vector.tensor_tensor(
                    rstd_row[:, g0:g0 + TT], mean_row[:, g0:g0 + TT],
                    mean_row[:, g0:g0 + TT], ALU.mult))
                dve.wait(t3)
                t4 = s_dve.tick(nc.vector.tensor_tensor(
                    var_row[:, 0:TT], var_row[:, 0:TT], rstd_row[:, g0:g0 + TT],
                    ALU.subtract))
                act.wait(t4)
                act.wait(t_ones)
                t5 = s_act.tick(nc.scalar.activation(
                    var_row[:, 0:TT], var_row[:, 0:TT], AF.Ln, bias=eps_t[0:1, 0:1]))
                act.wait(t5)
                st_["t6"] = s_act.tick(nc.scalar.activation(
                    rstd_row[:, g0:g0 + TT], var_row[:, 0:TT], AF.Exp, scale=-0.5))
                pe.wait([t1, t2])
                return st_["t6"]

            def ln_stats(xbf, sq, var_row, prev_rstd):
                st_ = {"xbf_free": [None, None], "sq_free": [None, None],
                       "t6": prev_rstd}
                for g in range(NT):
                    t6 = ln_stats_group(g, xbf, sq, var_row, st_)
                return t6

            pa_state = {}

            def passA_begin():
                pool.wait(s_pe.tok)
                pool.wait(s_dve.tok)
                pool.wait(dmaS.tok)
                tz = s_pool.tick(pool.eq.memset(ctx_acc[:], 0.0))
                tzc = s_pool.tick(pool.eq.memset(corr_t[:], 0.0))
                pa_state.clear()
                pa_state.update(ttr=[None, None], tm3=None, zero=[tz, tzc],
                                first=True)

            def passA_group(g, scraps, dumps, varbuf, rdy, varw):
                g0 = g * TT
                pe.wait(t_ones)
                pe.wait(rdy)
                trb_ = s_pe.tick(nc.tensor.matmul(
                    pC[:, 512:512 + TT], onesr[:, 0:128],
                    rstd_row[:, g0:g0 + TT], start=True, stop=True))
                first_d = True
                tmu = None
                for chh in range(NCH):
                    si = chh % 2
                    dve.wait(pa_state["zero"][0])
                    dve.wait(trb_)
                    dve.wait(pa_state["ttr"][si])
                    tmu = s_dve.tick(nc.vector.tensor_tensor(
                        scraps[si][:],
                        X[:, chh * T + g0: chh * T + g0 + TT],
                        pC[:, 512:512 + TT], ALU.mult))
                    if first_d:
                        act.wait(s_pe.tok)
                        if pa_state["first"]:
                            act.wait(s_dve.tok)
                            pa_state["first"] = False
                        first_d = False
                    act.wait(tmu)
                    pa_state["ttr"][si] = s_act.tick(nc.scalar.activation(
                        dumps[si], scraps[si][:], AF.Identity,
                        accum_out=ctx_parts[:, g * NCH + chh: g * NCH + chh + 1]))
                dve.wait(pa_state["zero"][1])
                dve.wait(pa_state["tm3"])
                dve.wait(varw)
                tm1 = s_dve.tick(nc.vector.tensor_tensor(
                    varbuf[:, 0:TT], mean_row[:, g0:g0 + TT],
                    rstd_row[:, g0:g0 + TT], ALU.mult))
                dve.wait(tm1)
                tm2 = s_dve.tick(nc.vector.reduce_sum(
                    corr_red[:], varbuf[:, 0:TT], axis=AXX))
                dve.wait(tm2)
                pa_state["tm3"] = s_dve.tick(nc.vector.tensor_tensor(
                    corr_t[:], corr_t[:], corr_red[:], ALU.add))
                pe.wait(tmu)

            def passA_finalize(call):
                dve.wait(pa_state["tm3"])
                s_dve.tick(nc.vector.tensor_copy(ctx_acc[0:1, 8:9], corr_t[:]))
                dve.wait(s_dve.tok)
                dve.wait(pa_state["ttr"][0])
                dve.wait(pa_state["ttr"][1])
                t_s1 = s_dve.tick(nc.vector.tensor_tensor(
                    ctx_parts[:, 0:8], ctx_parts[:, 0:8], ctx_parts[:, 8:16], ALU.add))
                t_s2 = s_dve.tick(nc.vector.tensor_tensor(
                    ctx_parts[:, 16:24], ctx_parts[:, 16:24], ctx_parts[:, 24:32], ALU.add))
                dve.wait([t_s1, t_s2])
                s_dve.tick(nc.vector.tensor_tensor(
                    ctx_acc[:, 0:8], ctx_parts[:, 0:8], ctx_parts[:, 16:24], ALU.add))
                syn.wait(s_dve.tok)
                sdma(dmaS, b_ctx_in[call].ap(), ctx_acc[:])
                pool.wait(dmaS.tok)
                icc = 0 if call == 0 else 6
                tcol = cc[icc].tick(nc.gpsimd.collective_compute(
                    "AllReduce", ALU.add, replica_groups=PAIRS,
                    ins=[b_ctx_in[call].ap()], outs=[b_ctx_out[call].ap()]))
                syn.wait(tcol)
                sdma(dmaS, ctx_acc[:], b_ctx_out[call].ap())
                return dmaS.tok

            def ln_bcast(g0, rdy=None):
                pe.wait(rdy)
                pe.wait(t_ones)
                tm = s_pe.tick(nc.tensor.matmul(
                    pC[:, 0:TT], onesr[:, 0:128], mean_row[:, g0:g0 + TT],
                    start=True, stop=True))
                tr = s_pe.tick(nc.tensor.matmul(
                    pC[:, 512:512 + TT], onesr[:, 0:128], rstd_row[:, g0:g0 + TT],
                    start=True, stop=True))
                return [tm, tr]

            prev_rstd = [None]
            hook_state = {"pd0": None, "pd1": None, "var_rd": None}

            def ffn(call, xload=None, stats_done=False, pre_ctx=None,
                    post_group=None):
                with contextlib.ExitStack() as fst:
                    fsb = lambda n, s, d: fst.enter_context(
                        nc.sbuf_tensor(f"{n}_f{call}", s, d))
                    stateA = fsb("stateA", [128, NCH * TT], F32)
                    stateB = fsb("stateB", [128, NCH * TT], F32)
                    stateC = fsb("stateC", [128, NCH * TT], F32)
                    state_bf = fsb("state_bf", [128, NCH * TT], BF16)
                    state_bf1 = fsb("state_bf1", [128, NCH * TT], BF16)
                    h1b = fsb("h1b", [128, 8 * TT], BF16)
                    h3b = h1b
                    t1f = fsb("t1f", [128, TT], F32)
                    cb = fsb("cb", [128, TT], F32)
                    tw = [fsb(f"tw{i}", [128, TT], F32) for i in range(2)]
                    hfb = [fsb("hfb0", [128, TT], F32), fsb("hfb1", [128, TT], F32)]
                    var_row = fsb("var_row", [1, TT], F32)
                    tanh_bf = fsb("tanh_bf", [1, TT], BF16)

                    xbf = [state_bf[:, 0:TT], state_bf[:, TT:2 * TT]]
                    sq = [state_bf1[:, 0:TT], state_bf1[:, TT:2 * TT]]
                    if not stats_done:
                        lnst0 = {"xbf_free": [None, None], "sq_free": [None, None],
                                 "t6": prev_rstd[0]}
                        passA_begin()
                        scrapsA = [hfb[0], hfb[1]]
                        dumpsA = [state_bf1[:, 0:TT], state_bf1[:, TT:2 * TT]]
                        for g in range(NT):
                            if xload is not None:
                                xload(g, stateC)
                            prev_rstd[0] = ln_stats_group(g, xbf, sq, var_row,
                                                          lnst0)
                        for g in range(NT):
                            passA_group(g, scrapsA, dumpsA, var_row,
                                        prev_rstd[0], prev_rstd[0])
                        ctx_ld = passA_finalize(call)
                    else:
                        ctx_ld = pre_ctx

                    H6 = [(pA, 0), (pA, 512), (pB, 0), (pB, 512),
                          (pD, 0), (pD, 512)]
                    bank_free = [xst["bank"][i] for i in range(6)]
                    hk6 = [0]
                    PERM = [4, 5, 6, 7, 0, 1, 2, 3]
                    SGN = [+1, +1, -1, -1, -1, -1, +1, +1]
                    prev_ttr = None
                    prev_cb_reader = None
                    res_done = None

                    res_free = [None, None]
                    SBUFS = [stateA, stateB, stateC]
                    BFS = [state_bf, state_bf1]
                    IDX_I = [0, 2, 1, 0]
                    IDX_O = [1, 0, 2, 1]
                    passb_dve_last = None

                    def pass_b(g):
                        nonlocal passb_dve_last
                        g0 = g * TT
                        if g == 1:
                            dve.wait(pa_state["ttr"][0])
                            dve.wait(pa_state["ttr"][1])
                        pe.wait(passb_dve_last)
                        bct = ln_bcast(g0)
                        dst_s = SBUFS[IDX_I[g]]
                        dst_b = BFS[g % 2]
                        bf_free = s_pe.tok
                        toks = [None] * NCH
                        tbm = None
                        for chh in range(NCH):
                            dve.wait(bct)
                            dve.wait(tbm)
                            dve.wait(hook_state["var_rd"])
                            ta = s_dve.tick(nc.vector.tensor_tensor(
                                t1f[:], X[:, chh * T + g0: chh * T + g0 + TT],
                                pC[:, 0:TT], ALU.subtract))
                            dve.wait(ta)
                            tbm = s_dve.tick(nc.vector.tensor_tensor(
                                dst_s[:, chh * TT:(chh + 1) * TT], t1f[:],
                                pC[:, 512:512 + TT], ALU.mult))
                            dve.wait(tbm)
                            dve.wait(bf_free)
                            toks[chh] = s_dve.tick(nc.vector.tensor_copy(
                                dst_b[:, chh * TT:(chh + 1) * TT],
                                dst_s[:, chh * TT:(chh + 1) * TT]))
                        passb_dve_last = tbm
                        return toks

                    pe.wait(ld_w)
                    act.wait(ld_small)
                    cast_next = pass_b(0)

                    gc_state = {}

                    def emit_gc():
                        pe.wait(ctx_ld)
                        pe.wait(t_ones)
                        pe.wait(bank_free[4])
                        tcmm = s_pe.tick(nc.tensor.matmul(
                            pD[:, 13:14], onesr[:, 0:128], ctx_acc[0:1, 8:9],
                            start=True, stop=True))
                        dve.wait(tcmm)
                        tcpy = s_dve.tick(nc.vector.tensor_copy(corrb[:], pD[:, 13:14]))
                        dve.wait(tcpy)
                        tcb = s_dve.tick(nc.vector.tensor_scalar(
                            ctx_bf[:], ctx_acc[:, 0:8], corrb[:, 0:1], 1.0 / TFULL,
                            ALU.subtract, ALU.mult))
                        pe.wait(tcb)
                        pe.wait(ld_w)
                        tga = None
                        for fo in range(4):
                            pe.wait(tga)
                            for fi in range(8):
                                tg = s_pe.tick(nc.tensor.matmul(
                                    pD[:, fo:fo + 1],
                                    gw1t[:, fi * 512 + fo * 128: fi * 512 + (fo + 1) * 128],
                                    ctx_bf[:, fi:fi + 1], start=(fi == 0), stop=(fi == 7)))
                            act.wait(tg)
                            tga = s_act.tick(nc.scalar.activation(
                                g1_bf[:, fo:fo + 1], pD[:, fo:fo + 1], AF.Gelu,
                                bias=gb1t[:, fo:fo + 1]))
                        tg2 = None
                        for fo in range(2):
                            pe.wait(tga if tg2 is None else tg2)
                            pe.wait(s_act.tok)
                            for fi in range(4):
                                tg = s_pe.tick(nc.tensor.matmul(
                                    pD[:, 8 + fo:9 + fo],
                                    gw2t[:, fi * 256 + fo * 128: fi * 256 + (fo + 1) * 128],
                                    g1_bf[:, fi:fi + 1], start=(fi == 0), stop=(fi == 3)))
                            act.wait(tg)
                            tg2 = s_act.tick(nc.scalar.activation(
                                g2_bf[:, fo:fo + 1], pD[:, 8 + fo:9 + fo], AF.Gelu,
                                bias=gb2t[:, fo:fo + 1]))
                        pe.wait(s_act.tok)
                        for fi in range(2):
                            tg = s_pe.tick(nc.tensor.matmul(
                                pD[0:1, 12:13], gw3t[:, fi:fi + 1], g2_bf[:, fi:fi + 1],
                                start=(fi == 0), stop=(fi == 1)))
                        act.wait(tg)
                        act.wait(t_hgb3)
                        ttg = s_act.tick(nc.scalar.activation(
                            tgf_t[:], pD[0:1, 12:13], AF.Tanh, bias=hgb3_t[0:1, 0:1],
                            scale=0.5))
                        dve.wait([ttg, t_arx])
                        ta1 = s_dve.tick(nc.vector.tensor_tensor(
                            ab_row[:, 0:1], tgf_t[:], arx_t[:], ALU.mult))
                        dve.wait(ta1)
                        ta2 = s_dve.tick(nc.vector.tensor_scalar(
                            ab_row[:, 0:1], ab_row[:, 0:1], 0.1, None, ALU.add))
                        dve.wait(ta2)
                        ta3 = s_dve.tick(nc.vector.tensor_scalar(
                            ab_row[:, 1:2], ar_t[:], 0.3, None, ALU.mult))
                        pe.wait(ta3)
                        tab = s_pe.tick(nc.tensor.matmul(
                            pD[:, 14:16], onesr[:, 0:128], ab_row[:], start=True, stop=True))
                        dve.wait(tab)
                        tAB = s_dve.tick(nc.vector.tensor_copy(AB[:], pD[:, 14:16]))
                        pe.wait(tAB)

                        gc_state["tAB"] = tAB
                        bank_free[4] = tAB

                    for g in range(NT):
                        g0 = g * TT
                        cast_tok = cast_next
                        s_in = SBUFS[IDX_I[g]]
                        s_out = SBUFS[IDX_O[g]]
                        state_b = BFS[g % 2]
                        for cyc in range(3):
                            src = s_in if cyc % 2 == 0 else s_out
                            dst = s_out if cyc % 2 == 0 else s_in
                            ev1 = [None] * 8
                            stopt = [None] * 8
                            b1m = [(hk6[0] + i) % 6 for i in range(8)]
                            b2m = [(hk6[0] + 8 + i) % 6 for i in range(4)]
                            b3m = [(hk6[0] + 12 + i) % 6 for i in range(2)]
                            hk6[0] = (hk6[0] + 14) % 6
                            for fog in range(2):
                                fos = list(range(fog * 4, fog * 4 + 4))
                                for fi in range(8):
                                    pe.wait(cast_tok[fi])
                                    for fo in fos:
                                        bk = H6[b1m[fo]]
                                        if fi == 0:
                                            pe.wait(bank_free[b1m[fo]])
                                        tmm = s_pe.tick(nc.tensor.matmul(
                                            bk[0][:, bk[1]:bk[1] + TT],
                                            w1t[:, fi * 1024 + fo * 128: fi * 1024 + (fo + 1) * 128],
                                            state_b[:, fi * TT:(fi + 1) * TT],
                                            start=(fi == 0), stop=(fi == 7)))
                                        if fi == 7:
                                            stopt[fo] = tmm
                                for fo in fos:
                                    bk = H6[b1m[fo]]
                                    act.wait(stopt[fo])
                                    ev1[fo] = s_act.tick(nc.scalar.activation(
                                        h1b[:, fo * TT:(fo + 1) * TT],
                                        bk[0][:, bk[1]:bk[1] + TT],
                                        AF.Gelu, bias=b1t[:, fo:fo + 1]))
                                    bank_free[b1m[fo]] = ev1[fo]
                            ev2 = [None] * 4
                            for fi in range(8):
                                pe.wait(ev1[fi])
                                for fo in range(4):
                                    bk = H6[b2m[fo]]
                                    if fi == 0:
                                        pe.wait(bank_free[b2m[fo]])
                                    tmm = s_pe.tick(nc.tensor.matmul(
                                        bk[0][:, bk[1]:bk[1] + TT],
                                        w2t[:, fi * 512 + fo * 128: fi * 512 + (fo + 1) * 128],
                                        h1b[:, fi * TT:(fi + 1) * TT],
                                        start=(fi == 0), stop=(fi == 7)))
                                    if fi == 7:
                                        stopt[fo] = tmm
                            for fo in range(4):
                                bk = H6[b2m[fo]]
                                act.wait(stopt[fo])
                                ev2[fo] = s_act.tick(nc.scalar.activation(
                                    h1b[:, (2 + fo) * TT:(3 + fo) * TT],
                                    bk[0][:, bk[1]:bk[1] + TT],
                                    AF.Gelu, bias=b2t[:, fo:fo + 1]))
                                bank_free[b2m[fo]] = ev2[fo]
                            ev3 = [None] * 2
                            for fi in range(4):
                                pe.wait(ev2[fi])
                                for fo in range(2):
                                    bk = H6[b3m[fo]]
                                    if fi == 0:
                                        pe.wait(bank_free[b3m[fo]])
                                    tmm = s_pe.tick(nc.tensor.matmul(
                                        bk[0][:, bk[1]:bk[1] + TT],
                                        w3t[:, fi * 256 + fo * 128: fi * 256 + (fo + 1) * 128],
                                        h1b[:, (2 + fi) * TT:(3 + fi) * TT],
                                        start=(fi == 0), stop=(fi == 3)))
                                    if fi == 3:
                                        stopt[fo] = tmm
                            for fo in range(2):
                                bk = H6[b3m[fo]]
                                act.wait(stopt[fo])
                                ev3[fo] = s_act.tick(nc.scalar.activation(
                                    h3b[:, fo * TT:(fo + 1) * TT],
                                    bk[0][:, bk[1]:bk[1] + TT],
                                    AF.Gelu, bias=b3t[:, fo:fo + 1]))
                                bank_free[b3m[fo]] = ev3[fo]
                            if g == 0 and cyc == 0:
                                emit_gc()
                            pe.wait(bank_free[4])
                            for fi in range(2):
                                pe.wait(ev3[fi])
                                tz2 = s_pe.tick(nc.tensor.matmul(
                                    pD[0:1, 0:TT], w4t[:, fi:fi + 1],
                                    h3b[:, fi * TT:(fi + 1) * TT],
                                    start=(fi == 0), stop=(fi == 1)))
                            act.wait([tz2, t_hb4])
                            prev_ttr = s_act.tick(nc.scalar.activation(
                                tanh_bf[:, 0:TT], pD[0:1, 0:TT], AF.Tanh,
                                bias=hb4_t[0:1, 0:1], scale=0.5))
                            pe.wait(prev_ttr)
                            pe.wait(bank_free[5])
                            tbc = s_pe.tick(nc.tensor.matmul(
                                pD[:, 512:512 + TT], onesrb[:, 0:128],
                                tanh_bf[:, 0:TT], start=True, stop=True))
                            dve.wait([tbc, gc_state["tAB"]])
                            tcb2 = s_dve.tick(nc.vector.tensor_scalar(
                                cb[:], pD[:, 512:512 + TT], AB[:, 1:2], AB[:, 0:1],
                                ALU.mult, ALU.add))
                            prev_cb_reader = tcb2
                            bank_free[4] = prev_ttr
                            bank_free[5] = tcb2
                            pe.wait(tcb2)
                            tw_free = [None] * 2
                            tw1s = [None] * NCH
                            tw2s = [None] * NCH
                            for ch0 in range(0, NCH, 2):
                                for chh in (ch0, ch0 + 1):
                                    dve.wait(tcb2)
                                    dve.wait(tw_free[chh % 2])
                                    tw1s[chh] = s_dve.tick(nc.vector.tensor_tensor(
                                        tw[chh % 2][:],
                                        src[:, PERM[chh] * TT:(PERM[chh] + 1) * TT],
                                        cb[:], ALU.mult))
                                for chh in (ch0, ch0 + 1):
                                    dve.wait(tw1s[chh])
                                    tw2s[chh] = s_dve.tick(nc.vector.tensor_tensor(
                                        dst[:, chh * TT:(chh + 1) * TT],
                                        src[:, chh * TT:(chh + 1) * TT],
                                        tw[chh % 2][:],
                                        ALU.add if SGN[chh] > 0 else ALU.subtract))
                                    tw_free[chh % 2] = tw2s[chh]
                                if cyc < 2:
                                    for chh in (ch0, ch0 + 1):
                                        dve.wait(tw2s[chh])
                                        cast_tok[chh] = s_dve.tick(nc.vector.tensor_copy(
                                            state_b[:, chh * TT:(chh + 1) * TT],
                                            dst[:, chh * TT:(chh + 1) * TT]))
                                else:
                                    cast_tok[ch0] = tw2s[ch0]
                                    cast_tok[ch0 + 1] = tw2s[ch0 + 1]
                            if cyc == 1 and g + 1 < NT:
                                cast_next = pass_b(g + 1)
                        for chh in range(NCH):
                            hi = chh % 2
                            dve.wait(cast_tok[chh])
                            dve.wait(res_free[hi])
                            th = s_dve.tick(nc.vector.tensor_scalar(
                                hfb[hi][:], s_out[:, chh * TT:(chh + 1) * TT],
                                0.5, None, ALU.mult))
                            dve.wait(th)
                            res_free[hi] = s_dve.tick(nc.vector.tensor_tensor(
                                X[:, chh * T + g0: chh * T + g0 + TT],
                                X[:, chh * T + g0: chh * T + g0 + TT],
                                hfb[hi][:], ALU.add))
                    barrier()

            PB6 = [(pA, 0), (pA, 512), (pB, 0), (pB, 512), (pD, 0), (pD, 512)]
            xst = {"bank": [None] * 6, "slot": [None] * 4, "k": 0}

            def xload(g, stg):
                pe.wait(dmaS.tok)
                for i in range(4 * g, 4 * g + 4):
                    sl = i % 4
                    syn.wait(xst["slot"][sl])
                    sdma(slotY[sl], stg[:, sl * 1024:(sl + 1) * 1024],
                         xo[i * 128:(i + 1) * 128, :])
                    ldt = slotY[sl].tok
                    pe.wait(ldt)
                    last = None
                    for ch in range(NCH):
                        bnk = xst["k"] % 6
                        pt, off = PB6[bnk]
                        pe.wait(xst["bank"][bnk])
                        ttr = s_pe.tick(nc.tensor.transpose(
                            pt[:, off:off + 128],
                            stg[:, sl * 1024 + ch * 128: sl * 1024 + (ch + 1) * 128],
                            ident[:]))
                        dve.wait(ttr)
                        last = s_dve.tick(nc.vector.tensor_copy(
                            X[:, ch * T + i * 128: ch * T + (i + 1) * 128],
                            pt[:, off:off + 128]))
                        xst["bank"][bnk] = last
                        xst["k"] += 1
                    xst["slot"][sl] = last

            ffn(0, xload)

            with contextlib.ExitStack() as ast_:
                asbt = lambda n, s, d: ast_.enter_context(nc.sbuf_tensor(n, s, d))
                wuqt = asbt("wuqt", [128, 2 * 512], BF16)
                wukt = asbt("wukt", [128, 2 * 512], BF16)
                wuvt = asbt("wuvt", [128, 2 * 512], BF16)
                cqs_stack = contextlib.ExitStack()
                cq_sb = cqs_stack.enter_context(nc.sbuf_tensor("cq_sb", [128, 2 * TFULL], BF16))
                ckv_sb = cqs_stack.enter_context(nc.sbuf_tensor("ckv_sb", [128, 2 * TFULL], BF16))

                prj = contextlib.ExitStack()
                psbt = lambda n, s, d: prj.enter_context(nc.sbuf_tensor(n, s, d))
                xbf = [psbt("a_xbf0", [128, TT], BF16)[:], psbt("a_xbf1", [128, TT], BF16)[:]]
                sq = [psbt("a_sq0", [128, TT], BF16)[:], psbt("a_sq1", [128, TT], BF16)[:]]
                var_row = psbt("a_var", [1, TT], F32)
                t1f = psbt("a_t1f", [128, TT], F32)
                ybufs = [psbt("a_y2b0", [128, TT], BF16), psbt("a_y2b1", [128, TT], BF16)]
                y2g_sb2 = [psbt("y2gA", [128, NCH * TT], BF16),
                           psbt("y2gB", [128, NCH * TT], BF16)]
                y2g_rd = [[None] * NCH, [None] * NCH]
                slot_rd = [None] * 4
                wdqt = psbt("wdqt", [128, 8 * 256], BF16)
                wdkvt = psbt("wdkvt", [128, 8 * 256], BF16)

                gload(dmaW2, wdqt[:].rearrange("p (a n) -> p a n", a=8), wview(wdq, 128, 8, 256))
                gload(dmaW2, wdkvt[:].rearrange("p (a n) -> p a n", a=8), wview(wdkv, 128, 8, 256))
                ld_attw = dmaW2.tok
                gload(dmaW4, wuqt[:].rearrange("p (a n) -> p a n", a=2), wview(wuq, 128, 2, 512))
                gload(dmaW4, wukt[:].rearrange("p (a n) -> p a n", a=2), wview(wuk, 128, 2, 512))
                gload(dmaW4, wuvt[:].rearrange("p (a n) -> p a n", a=2), wview(wuv, 128, 2, 512))
                ld_attw2 = dmaW4.tok

                lnst = {"xbf_free": [None, None], "sq_free": [None, None],
                        "t6": prev_rstd[0]}
                t6s = [None] * NT

                def ln_stats0(g):
                    g0 = g * TT
                    pe.wait(t_ones)
                    mm_m = mm_s = None
                    for chh in range(NCH):
                        bi = chh % 2
                        act.wait(lnst["xbf_free"][bi])
                        tx = s_act.tick(nc.scalar.activation(
                            xbf[bi], X[:, chh * T + g0: chh * T + g0 + TT],
                            AF.Identity))
                        pe.wait(tx)
                        mm_m = s_pe.tick(nc.tensor.matmul(
                            pC[0:1, 0:TT], onesc[:], xbf[bi],
                            start=(chh == 0), stop=(chh == NCH - 1)))
                        lnst["xbf_free"][bi] = mm_m
                        act.wait(lnst["sq_free"][bi])
                        tsq = s_act.tick(nc.scalar.activation(
                            sq[bi], X[:, chh * T + g0: chh * T + g0 + TT],
                            AF.Square))
                        pe.wait(tsq)
                        mm_s = s_pe.tick(nc.tensor.matmul(
                            pC[0:1, 512:512 + TT], onesc[:], sq[bi],
                            start=(chh == 0), stop=(chh == NCH - 1)))
                        lnst["sq_free"][bi] = mm_s
                    dve.wait(mm_m)
                    t1 = s_dve.tick(nc.vector.tensor_scalar(
                        mean_row[:, g0:g0 + TT], pC[0:1, 0:TT], 1.0 / D, None,
                        ALU.mult))
                    dve.wait(mm_s)
                    dve.wait(lnst["t6"])
                    t2 = s_dve.tick(nc.vector.tensor_scalar(
                        var_row[:, 0:TT], pC[0:1, 512:512 + TT], 1.0 / D, None,
                        ALU.mult))
                    dve.wait([t1, t2])
                    t3 = s_dve.tick(nc.vector.tensor_tensor(
                        rstd_row[:, g0:g0 + TT], mean_row[:, g0:g0 + TT],
                        mean_row[:, g0:g0 + TT], ALU.mult))
                    dve.wait(t3)
                    t4 = s_dve.tick(nc.vector.tensor_tensor(
                        var_row[:, 0:TT], var_row[:, 0:TT],
                        rstd_row[:, g0:g0 + TT], ALU.subtract))
                    act.wait(t4)
                    act.wait(t_ones)
                    t5 = s_act.tick(nc.scalar.activation(
                        var_row[:, 0:TT], var_row[:, 0:TT], AF.Ln,
                        bias=eps_t[0:1, 0:1]))
                    act.wait(t5)
                    lnst["t6"] = s_act.tick(nc.scalar.activation(
                        rstd_row[:, g0:g0 + TT], var_row[:, 0:TT], AF.Exp,
                        scale=-0.5))
                    pe.wait([t1, t2])
                    return lnst["t6"]
                tal = [None] * NT
                tbl = [None] * NT
                pjbank = [(pA, 0), (pA, 512), (pB, 0), (pB, 512)]
                bank_ev = [None] * 4
                tbg = [[None] * NCH for _ in range(NT)]
                tb = None

                def emit_proj(g):
                    gb = g % 2
                    pe.wait(ld_attw)
                    stop4 = [None] * 4
                    for ci in range(4):
                        kind, co = ci // 2, ci % 2
                        pt, off = pjbank[ci]
                        for fi in range(NCH):
                            if fi == 0:
                                pe.wait(bank_ev[ci])
                            pe.wait(tbg[g][fi])
                            tmm = s_pe.tick(nc.tensor.matmul(
                                pt[:, off:off + TT],
                                (wdqt if kind == 0 else wdkvt)[
                                    :, fi * 256 + co * 128: fi * 256 + (co + 1) * 128],
                                y2g_sb2[gb][:, fi * TT:(fi + 1) * TT],
                                start=(fi == 0), stop=(fi == NCH - 1)))
                            if ci == 3:
                                y2g_rd[gb][fi] = tmm
                        stop4[ci] = tmm
                    for ci in range(4):
                        kind, co = ci // 2, ci % 2
                        pt, off = pjbank[ci]
                        sb = ybufs[ci % 2]
                        ysem = dmaY if ci % 2 == 0 else dmaY2
                        dve.wait(stop4[ci])
                        dve.wait(ysem.tok)
                        tev = s_dve.tick(nc.vector.tensor_copy(
                            sb[:], pt[:, off:off + TT]))
                        bank_ev[ci] = tev
                        syn.wait(tev)
                        sdma(ysem,
                             bass.AP(b_c_in[g // 2],
                                     (kind * 2 + co) * 128 * 2 * TT
                                     + (g % 2) * TT,
                                     [[2 * TT, 128], [1, TT]]),
                             sb[:])

                def emit_y2(g):
                    g0 = g * TT
                    pe.wait([tal[g - 1], tbl[g - 1]] if g >= 1 else None)
                    pe.wait(t6s[g])
                    pe.wait(t_ones)
                    bm = s_pe.tick(nc.tensor.matmul(
                        pD[:, 0:TT], onesr[:, 0:128], mean_row[:, g0:g0 + TT],
                        start=True, stop=True))
                    br = s_pe.tick(nc.tensor.matmul(
                        pD[:, 512:512 + TT], onesr[:, 0:128],
                        rstd_row[:, g0:g0 + TT], start=True, stop=True))
                    ta = tb = None
                    for chh in range(NCH):
                        dve.wait([bm, br])
                        dve.wait(tb)
                        ta = s_dve.tick(nc.vector.tensor_tensor(
                            t1f[:], X[:, chh * T + g0: chh * T + g0 + TT],
                            pD[:, 0:TT], ALU.subtract))
                        dve.wait(ta)
                        dve.wait(y2g_rd[g % 2][chh])
                        tb = s_dve.tick(nc.vector.tensor_tensor(
                            y2g_sb2[g % 2][:, chh * TT:(chh + 1) * TT], t1f[:],
                            pD[:, 512:512 + TT], ALU.mult))
                        tbg[g][chh] = tb
                    tal[g] = ta
                    tbl[g] = tb

                t_cag = [None, None]

                def fire_cag(hf):
                    pool.wait(dmaY.tok)
                    pool.wait(dmaY2.tok)
                    t_cag[hf] = ccy[hf].tick(nc.gpsimd.collective_compute(
                        "AllGather", ALU.bypass, replica_groups=PAIRS,
                        ins=[b_c_in[hf].ap()], outs=[b_c_out[hf].ap()]))

                for g in range(NT):
                    t6s[g] = (ln_stats0(g) if g <= 1 else
                              ln_stats_group(g, xbf, sq, var_row, lnst))
                    emit_y2(g - 1) if g >= 1 else None
                    if g >= 2:
                        emit_proj(g - 2)
                    if g == 3:
                        fire_cag(0)
                prev_rstd[0] = t6s[NT - 1]
                emit_y2(NT - 1)
                emit_proj(NT - 2)
                emit_proj(NT - 1)
                fire_cag(1)

                sl = 0
                for hf in range(2):
                    for rank, kind, co in ((r, k, c) for r in range(2)
                                           for k in range(2) for c in range(2)):
                        syn.wait(t_cag[hf])
                        syn.wait(slot_rd[sl % 4])
                        dst = cq_sb if kind == 0 else ckv_sb
                        sdma(slotY[sl % 4],
                             dst[:, co * TFULL + rank * T + hf * 2 * TT:
                                 co * TFULL + rank * T + (hf + 1) * 2 * TT],
                             bass.AP(b_c_out[hf],
                                     (rank * 4 + kind * 2 + co) * 128 * 2 * TT,
                                     [[2 * TT, 128], [1, 2 * TT]]))
                        slot_rd[sl % 4] = slotY[sl % 4].tok
                        sl += 1
                for s_ in slotY:
                    pe.wait((s_, s_.c))
                barrier()
                prj.close()

                plp = contextlib.ExitStack()
                lsbt = lambda n, s, d: plp.enter_context(nc.sbuf_tensor(n, s, d))
                kpair = lsbt("kpair", [128, TFULL], BF16)
                qpair = lsbt("qpair", [128, TFULL], BF16)
                vhA = lsbt("vhA", [128, 32 * 65], BF16)
                vhB = lsbt("vhB", [128, 32 * 65], BF16)
                expb2 = [[lsbt(f"exp{h}{par}", [128, 1024], BF16)
                          for par in range(2)] for h in range(2)]
                slab_t = lsbt("slab", [128, TFULL], BF16)
                recT = lsbt("recT", [65, 1024], F32)
                recB = lsbt("recB", [65, 1024], BF16)
                rbc4 = lsbt("rbc4", [64, 1024], F32)

                vA_ones = vhA[:].rearrange("p (c k) -> p c k", k=65)[:, :, 64:65]
                vB_ones = vhB[:].rearrange("p (c k) -> p c k", k=65)[:, :, 64:65]
                t_vo = [s_pool.tick(nc.gpsimd.memset(vA_ones, 1.0)),
                        s_pool.tick(nc.gpsimd.memset(vB_ones, 1.0))]

                score_ps = [pA, pB]
                av_ps = [pC, pD]
                sbl = {}
                avl = {}
                ebl = {}
                rec_free = [None] * 2
                rbc_free = [None] * 2
                slab_store = None
                dve.wait(ld_w)
                pe.wait(ld_attw2)
                for p in range(4):
                    kev_g = [sbl.get((0, 0)), sbl.get((0, 1))]
                    for sub in range(8):
                        pt, off = (pA, 0) if sub % 2 == 0 else (pA, 512)
                        pe.wait(kev_g[sub % 2])
                        for a in range(2):
                            tkm = s_pe.tick(nc.tensor.matmul(
                                pt[:, off:off + TT],
                                wukt[:, a * 512 + p * 128: a * 512 + (p + 1) * 128],
                                ckv_sb[:, a * TFULL + sub * TT: a * TFULL + (sub + 1) * TT],
                                start=(a == 0), stop=(a == 1)))
                        dve.wait(tkm)
                        kev_g[sub % 2] = s_dve.tick(nc.vector.tensor_copy(
                            kpair[:, sub * TT:(sub + 1) * TT], pt[:, off:off + TT]))
                    vev_g = [sbl.get((1, 0)), sbl.get((1, 1))]
                    for tc in range(32):
                        pt, off = (pB, 0) if tc % 2 == 0 else (pB, 512)
                        pe.wait(vev_g[tc % 2])
                        for a in range(2):
                            tvm = s_pe.tick(nc.tensor.matmul(
                                pt[:, off:off + 128],
                                ckv_sb[:, a * TFULL + tc * 128: a * TFULL + (tc + 1) * 128],
                                wuvt[:, a * 512 + p * 128: a * 512 + (p + 1) * 128],
                                start=(a == 0), stop=(a == 1)))
                        dve.wait(tvm)
                        s_dve.tick(nc.vector.tensor_copy(
                            vhA[:, tc * 65: tc * 65 + 64], pt[:, off:off + 64]))
                        vev_g[tc % 2] = s_dve.tick(nc.vector.tensor_copy(
                            vhB[:, tc * 65: tc * 65 + 64], pt[:, off + 64:off + 128]))
                    qev_g = [kev_g[0], kev_g[1]]
                    for sub in range(8):
                        pt, off = (pA, 0) if sub % 2 == 0 else (pA, 512)
                        pe.wait(qev_g[sub % 2])
                        for a in range(2):
                            tqm = s_pe.tick(nc.tensor.matmul(
                                pt[:, off:off + TT],
                                wuqt[:, a * 512 + p * 128: a * 512 + (p + 1) * 128],
                                cq_sb[:, a * TFULL + sub * TT: a * TFULL + (sub + 1) * TT],
                                start=(a == 0), stop=(a == 1)))
                        dve.wait(tqm)
                        qev_g[sub % 2] = s_dve.tick(nc.vector.tensor_copy(
                            qpair[:, sub * TT:(sub + 1) * TT], pt[:, off:off + TT]))
                    pe.wait(t_vo)
                    sbl[(0, 0)] = qev_g[0]
                    sbl[(0, 1)] = qev_g[1]
                    sbl[(1, 0)] = vev_g[0]
                    sbl[(1, 1)] = vev_g[1]

                    for rnd in range(4):
                        tA = 2 * rnd
                        tB = 2 * rnd + 1
                        q0A = tA * 512
                        q0B = tB * 512
                        nchkA = q0A // 128 + 4
                        nchkB = q0B // 128 + 4
                        last_av = {}
                        pend = None

                        def emit_avs(c, texp_c):
                            k0 = c * 128
                            for h in range(2):
                                vh = vhA if h == 0 else vhB
                                eb = expb2[h][c % 2]
                                avts = []
                                if c < nchkA:
                                    cstA = max(0, k0 - q0A)
                                    pe.wait(texp_c[(h, 0)])
                                    if c == 0:
                                        pe.wait(avl.get((h, 0)))
                                    avts.append(s_pe.tick(nc.tensor.matmul(
                                        av_ps[h][0:65, cstA:512],
                                        vh[:, c * 65:(c + 1) * 65],
                                        eb[:, cstA:512],
                                        start=(c == 0), stop=(c == nchkA - 1))))
                                    last_av[(h, 0)] = avts[-1]
                                cstB = max(0, k0 - q0B)
                                pe.wait(texp_c[(h, 1)])
                                if c == 0:
                                    pe.wait(avl.get((h, 1)))
                                avts.append(s_pe.tick(nc.tensor.matmul(
                                    av_ps[h][0:65, 512 + cstB:1024],
                                    vh[:, c * 65:(c + 1) * 65],
                                    eb[:, 512 + cstB:1024],
                                    start=(c == 0), stop=(c == nchkB - 1))))
                                last_av[(h, 1)] = avts[-1]
                                ebl[(h, c % 2)] = avts

                        for c in range(nchkB):
                            k0 = c * 128
                            texp_c = {}
                            for h in range(2):
                                r0 = h * 64
                                if c < nchkA:
                                    cstA = max(0, k0 - q0A)
                                    pe.wait(sbl.get((h, 0)))
                                    s_pe.tick(nc.tensor.matmul(
                                        score_ps[h][:, cstA:512],
                                        kpair[r0:r0 + 64, k0:k0 + 128],
                                        qpair[r0:r0 + 64, q0A + cstA:q0A + 512],
                                        start=True, stop=True))
                                cstB = max(0, k0 - q0B)
                                pe.wait(sbl.get((h, 1)))
                                sc = s_pe.tick(nc.tensor.matmul(
                                    score_ps[h][:, 512 + cstB:1024],
                                    kpair[r0:r0 + 64, k0:k0 + 128],
                                    qpair[r0:r0 + 64, q0B + cstB:q0B + 512],
                                    start=True, stop=True))
                                lo = max(0, k0 - q0A) if c < nchkA else 512 + cstB
                                eb = expb2[h][c % 2]
                                act.wait(sc)
                                act.wait(ebl.get((h, c % 2)))
                                texp = s_act.tick(nc.scalar.activation(
                                    eb[:, lo:1024], score_ps[h][:, lo:1024],
                                    AF.Exp, scale=0.125))
                                sbl[(h, 0)] = texp if c < nchkA else sbl.get((h, 0))
                                sbl[(h, 1)] = texp
                                mka = mkb = texp
                                if c < nchkA and k0 >= q0A:
                                    dve.wait(texp)
                                    mka = s_dve.tick(nc.vector.tensor_tensor(
                                        eb[:, cstA:cstA + 128], eb[:, cstA:cstA + 128],
                                        triu[:], ALU.mult))
                                if k0 >= q0B:
                                    dve.wait(texp)
                                    mkb = s_dve.tick(nc.vector.tensor_tensor(
                                        eb[:, 512 + cstB:512 + cstB + 128],
                                        eb[:, 512 + cstB:512 + cstB + 128],
                                        triu[:], ALU.mult))
                                texp_c[(h, 0)] = mka
                                texp_c[(h, 1)] = mkb
                            if pend is not None:
                                emit_avs(*pend)
                            pend = (c, texp_c)
                        emit_avs(*pend)
                        streams = [(tA, 0), (tA, 1), (tB, 0), (tB, 1)]
                        for g2 in range(2):
                            tln = {}
                            tex = {}
                            tbc = {}
                            trbs = {}
                            grp = [(g2 * 2 + j, streams[g2 * 2 + j]) for j in range(2)]
                            for si, (t, h) in grp:
                                ti = t % 2
                                off = ti * 512
                                act.wait(last_av[(h, ti)])
                                act.wait(rec_free[si % 2])
                                tln[si] = s_act.tick(nc.scalar.activation(
                                    recT[64:65, (si % 2) * 512:(si % 2) * 512 + 512],
                                    av_ps[h][64:65, off:off + 512], AF.Ln))
                            for si, (t, h) in grp:
                                act.wait(tln[si])
                                tex[si] = s_act.tick(nc.scalar.activation(
                                    recB[64:65, (si % 2) * 512:(si % 2) * 512 + 512],
                                    recT[64:65, (si % 2) * 512:(si % 2) * 512 + 512],
                                    AF.Exp, scale=-1.0))
                            for si, (t, h) in grp:
                                ti = t % 2
                                off = ti * 512
                                pe.wait(tex[si])
                                pe.wait(sbl[(h, ti)])
                                tbc[si] = s_pe.tick(nc.tensor.matmul(
                                    score_ps[h][0:64, off:off + 512],
                                    ones64b[64:65, 0:64],
                                    recB[64:65, (si % 2) * 512:(si % 2) * 512 + 512],
                                    start=True, stop=True))
                                rec_free[si % 2] = tbc[si]
                            for si, (t, h) in grp:
                                ti = t % 2
                                off = ti * 512
                                dve.wait(tbc[si])
                                dve.wait(rbc_free[si % 2])
                                trbs[si] = s_dve.tick(nc.vector.tensor_copy(
                                    rbc4[0:64, (si % 2) * 512:(si % 2) * 512 + 512],
                                    score_ps[h][0:64, off:off + 512]))
                                sbl[(h, ti)] = trbs[si]
                            for si, (t, h) in grp:
                                ti = t % 2
                                off = ti * 512
                                q0 = t * 512
                                dve.wait(trbs[si])
                                dve.wait(slab_store)
                                tsl = s_dve.tick(nc.vector.tensor_tensor(
                                    slab_t[h * 64:h * 64 + 64, q0:q0 + 512],
                                    av_ps[h][0:64, off:off + 512],
                                    rbc4[0:64, (si % 2) * 512:(si % 2) * 512 + 512],
                                    ALU.mult))
                                rbc_free[si % 2] = tsl
                                avl[(h, ti)] = tsl
                    syn.wait(s_dve.tok)
                    sdma(dmaSL, b_at_in[p].ap(), slab_t[:])
                    slab_store = dmaSL.tok
                    if p < 3:
                        pool.wait(slab_store)
                        cc[2 + p].tick(nc.gpsimd.collective_compute(
                            "AllGather", ALU.bypass, replica_groups=PAIRS,
                            ins=[b_at_in[p].ap()], outs=[b_at_out[p].ap()]))
                    else:
                        slab3_store = slab_store
                barrier()
                plp.close()
                cqs_stack.close()

                with contextlib.ExitStack() as wst:
                    wsb = lambda n, s, d: wst.enter_context(nc.sbuf_tensor(n, s, d))
                    wot = wsb("wot", [128, 8 * 1024], BF16)
                    xbf3 = [wsb("w_xbf0", [128, TT], BF16)[:], wsb("w_xbf1", [128, TT], BF16)[:]]
                    sq3 = [wsb("w_sq0", [128, TT], BF16)[:], wsb("w_sq1", [128, TT], BF16)[:]]
                    var3 = wsb("w_var", [1, TT], F32)
                    scraps3 = [wsb("w_scr0", [128, TT], F32),
                               wsb("w_scr1", [128, TT], F32)]
                    gfull = wsb("gfull", [128, 8 * T], BF16)
                    ld_woc = [None] * 8
                    for hdc in range(8):
                        ld_woc[hdc] = gload(
                            dmaWC[hdc], wot[:, hdc * 1024:(hdc + 1) * 1024],
                            bass.AP(wo.tensor, hdc * 128 * 1024,
                                    [[1024, 128], [1, 1024]]))
                    pool.wait(ld_small)
                    gtok = [None] * 8

                    def emit_gather(p, r):
                        hdc = r * 4 + p
                        gsem = dmaG8[hdc]
                        pool.wait(cc[2 + p].tok)
                        pool.eq.indirect_dma_start(
                            out=gfull[:, hdc * T:(hdc + 1) * T], out_offset=None,
                            in_=b_at_out[p].ap(),
                            in_offset=bass.IndirectOffsetOnAxis(
                                ap=gidx[:, r:r + 1], axis=0),
                        ).then_inc(gsem.h, 16)
                        gsem.c += 16
                        gtok[hdc] = (gsem, gsem.c)

                    for p, r in ((0, 0), (1, 0), (2, 0), (0, 1), (1, 1),
                                 (2, 1)):
                        emit_gather(p, r)
                    pool.wait(slab3_store)
                    cc[5].tick(nc.gpsimd.collective_compute(
                        "AllGather", ALU.bypass, replica_groups=PAIRS,
                        ins=[b_at_in[3].ap()], outs=[b_at_out[3].ap()]))
                    emit_gather(3, 0)
                    emit_gather(3, 1)
                    bank_last = [None] * 4
                    wbank8 = [(pA, 0), (pA, 512), (pB, 0), (pB, 512)]
                    HDC_ORDER = [0, 1, 2, 4, 5, 6, 3, 7]
                    ln3 = {"xbf_free": [None, None], "sq_free": [None, None],
                           "t6": prev_rstd[0]}
                    passA_begin()
                    rdy_q = None
                    for qs in range(4):
                        for fop in range(2):
                            wo_stop = [None] * 4
                            for hi, hdc in enumerate(HDC_ORDER):
                                pe.wait(gtok[hdc])
                                pe.wait(ld_woc[hdc])
                                for fo4 in range(4):
                                    fo = fop * 4 + fo4
                                    if hi == 0:
                                        pe.wait(bank_last[fo4])
                                    tmm = s_pe.tick(nc.tensor.matmul(
                                        wbank8[fo4][0][:, wbank8[fo4][1]:wbank8[fo4][1] + 512],
                                        wot[:, hdc * 1024 + fo * 128: hdc * 1024 + (fo + 1) * 128],
                                        gfull[:, hdc * T + qs * 512: hdc * T + (qs + 1) * 512],
                                        start=(hi == 0), stop=(hi == 7)))
                                    if hi == 7:
                                        wo_stop[fo4] = tmm
                            for fo4 in range(4):
                                fo = fop * 4 + fo4
                                dve.wait(wo_stop[fo4])
                                bank_last[fo4] = s_dve.tick(nc.vector.tensor_tensor(
                                    X[:, fo * T + qs * 512: fo * T + (qs + 1) * 512],
                                    X[:, fo * T + qs * 512: fo * T + (qs + 1) * 512],
                                    wbank8[fo4][0][:, wbank8[fo4][1]:wbank8[fo4][1] + 512],
                                    ALU.add))
                        t6q = ln_stats_group(qs, xbf3, sq3, var3, ln3)
                        prev_rstd[0] = t6q
                        if qs > 0:
                            passA_group(qs - 1, scraps3, sq3, var3, rdy_q, t6q)
                            ln3["sq_free"] = [
                                [ln3["sq_free"][0], pa_state["ttr"][0]],
                                [ln3["sq_free"][1], pa_state["ttr"][1]]]
                        rdy_q = t6q
                    passA_group(3, scraps3, sq3, var3, rdy_q, rdy_q)
                    barrier()

            ctx_ld2 = passA_finalize(1)
            ffn(1, stats_done=True, pre_ctx=ctx_ld2)

            with nc.sbuf_tensor("xout", [128, 16 * 1024], F32) as xout:
                bank_free = [None] * 8
                k = 0
                for i in range(16):
                    for ch in range(NCH):
                        pt, off = PBANKS[k % 8]
                        pe.wait(bank_free[k % 8])
                        ttr = s_pe.tick(nc.tensor.transpose(
                            pt[:, off:off + 128],
                            X[:, ch * T + i * 128: ch * T + (i + 1) * 128],
                            ident[:]))
                        dve.wait(ttr)
                        bank_free[k % 8] = s_dve.tick(nc.vector.tensor_copy(
                            xout[:, i * 1024 + ch * 128: i * 1024 + (ch + 1) * 128],
                            pt[:, off:off + 128]))
                        k += 1
                    if ch == NCH - 1:
                        syn.wait(bank_free[(k - 1) % 8])
                        sdma(dmaO,
                             bass.AP(out.tensor, i * 128 * D, [[D, 128], [1, D]]),
                             xout[:, i * 1024:(i + 1) * 1024])
                syn.wait(dmaO.tok)

    return nc


_CACHED = {}


def kernel(L0_attn_x, L0_attn_W_dq, L0_attn_W_uq, L0_attn_W_dkv, L0_attn_W_uk,
           L0_attn_W_uv, L0_attn_W_o, L0_ffn_cn_w1, L0_ffn_cn_b1, L0_ffn_cn_w2,
           L0_ffn_cn_b2, L0_ffn_cn_w3, L0_ffn_cn_b3, L0_ffn_cn_w4, L0_ffn_cn_b4,
           L0_ffn_gc_w1, L0_ffn_gc_b1, L0_ffn_gc_w2, L0_ffn_gc_b2, L0_ffn_gc_w3,
           L0_ffn_gc_b3, L0_ffn_adaptive_range):
    if "nc" not in _CACHED:
        _CACHED["nc"] = build_program()
    nc = _CACHED["nc"]

    x = np.asarray(L0_attn_x, dtype=np.float32)
    c = np.ascontiguousarray
    common = {
        "wdq": c(np.asarray(L0_attn_W_dq, np.float32)),
        "wdkv": c(np.asarray(L0_attn_W_dkv, np.float32)),
        "wo": c(np.asarray(L0_attn_W_o, np.float32)),
        "w1": c(np.asarray(L0_ffn_cn_w1, np.float32)),
        "w2": c(np.asarray(L0_ffn_cn_w2, np.float32)),
        "w3": c(np.asarray(L0_ffn_cn_w3, np.float32)),
        "w4r": c(np.asarray(L0_ffn_cn_w4, np.float32).reshape(2, 128).T),
        "gw1": c(np.asarray(L0_ffn_gc_w1, np.float32)),
        "gw2": c(np.asarray(L0_ffn_gc_w2, np.float32)),
        "gw3r": c(np.asarray(L0_ffn_gc_w3, np.float32).reshape(2, 128).T),
        "b1r": c(np.asarray(L0_ffn_cn_b1, np.float32).reshape(8, 128).T),
        "b2r": c(np.asarray(L0_ffn_cn_b2, np.float32).reshape(4, 128).T),
        "b3r": c(np.asarray(L0_ffn_cn_b3, np.float32).reshape(2, 128).T),
        "b4": c(np.asarray(L0_ffn_cn_b4, np.float32).reshape(1, 1)),
        "gb1r": c(np.asarray(L0_ffn_gc_b1, np.float32).reshape(4, 128).T),
        "gb2r": c(np.asarray(L0_ffn_gc_b2, np.float32).reshape(2, 128).T),
        "gb3": c(np.asarray(L0_ffn_gc_b3, np.float32).reshape(1, 1)),
        "ar_in": c(np.asarray(L0_ffn_adaptive_range, np.float32).reshape(1, 1)),
        "ident_in": np.eye(128, dtype=np.float32),
        "triu_in": np.triu(np.ones((128, 128), dtype=np.float32)),
    }
    wuq_f = np.asarray(L0_attn_W_uq, np.float32)
    wuk_f = np.asarray(L0_attn_W_uk, np.float32)
    wuv_f = np.asarray(L0_attn_W_uv, np.float32)

    in_maps = []
    for core in range(8):
        b, j = core // 2, core % 2
        m = dict(common)
        m["xo"] = c(x[b, j * T:(j + 1) * T, :])
        m["wuq"] = c(wuq_f[:, j * 512:(j + 1) * 512])
        m["wuk"] = c(wuk_f[:, j * 512:(j + 1) * 512])
        m["wuv"] = c(wuv_f[:, j * 512:(j + 1) * 512])
        gi = np.zeros((128, 2), dtype=np.uint32)
        for r in range(2):
            gi[:, r] = r * 256 + np.arange(128, dtype=np.uint32) * 2 + j
        m["gidx_in"] = gi
        in_maps.append(m)

    res = run_bass_kernel_spmd(nc, in_maps, core_ids=list(range(8)))
    outp = np.empty((4, TFULL, D), dtype=np.float32)
    for core in range(8):
        b, j = core // 2, core % 2
        outp[b, j * T:(j + 1) * T, :] = res.results[core]["out"]
    return outp


# version 89 (final)
# speedup vs baseline: 1.0286x; 1.0023x over previous
import numpy as np
import contextlib
import concourse.bass as bass
import concourse.mybir as mybir
from concourse.bass_utils import run_bass_kernel_spmd

F32 = mybir.dt.float32
BF16 = mybir.dt.bfloat16
U32 = mybir.dt.uint32
AF = mybir.ActivationFunctionType
ALU = mybir.AluOpType
AXX = mybir.AxisListType.X

D = 1024
T = 2048
TFULL = 4096
NCH = 8
TT = 512
NT = T // TT
DC = 256
TQ = 1024
NQT = TFULL // TQ
PAIRS = [[0, 1], [2, 3], [4, 5], [6, 7]]


class Sem:
    def __init__(self, h):
        self.h = h
        self.c = 0

    def tick(self, inst, n=1):
        inst.then_inc(self.h, n)
        self.c += n
        return (self, self.c)

    @property
    def tok(self):
        return (self, self.c)


class Q:
    def __init__(self, eq):
        self.eq = eq
        self.last = {}

    def wait(self, tok):
        if tok is None:
            return
        if isinstance(tok, list):
            for t in tok:
                self.wait(t)
            return
        s, v = tok
        assert v <= s.c, "wait on a not-yet-emitted token (would deadlock)"
        if v <= 0 or self.last.get(id(s), 0) >= v:
            return
        self.eq.wait_ge(s.h, v)
        self.last[id(s)] = v


def build_program():
    nc = bass.Bass("TRN2", target_bir_lowering=False)

    xo = nc.dram_tensor("xo", [T, D], F32, kind="ExternalInput").ap()
    wdq = nc.dram_tensor("wdq", [D, DC], F32, kind="ExternalInput").ap()
    wdkv = nc.dram_tensor("wdkv", [D, DC], F32, kind="ExternalInput").ap()
    wuq = nc.dram_tensor("wuq", [DC, 512], F32, kind="ExternalInput").ap()
    wuk = nc.dram_tensor("wuk", [DC, 512], F32, kind="ExternalInput").ap()
    wuv = nc.dram_tensor("wuv", [DC, 512], F32, kind="ExternalInput").ap()
    wo = nc.dram_tensor("wo", [D, D], F32, kind="ExternalInput").ap()
    w1 = nc.dram_tensor("w1", [D, D], F32, kind="ExternalInput").ap()
    w2 = nc.dram_tensor("w2", [D, 512], F32, kind="ExternalInput").ap()
    w3 = nc.dram_tensor("w3", [512, 256], F32, kind="ExternalInput").ap()
    w4r = nc.dram_tensor("w4r", [128, 2], F32, kind="ExternalInput").ap()
    gw1 = nc.dram_tensor("gw1", [D, 512], F32, kind="ExternalInput").ap()
    gw2 = nc.dram_tensor("gw2", [512, 256], F32, kind="ExternalInput").ap()
    gw3r = nc.dram_tensor("gw3r", [128, 2], F32, kind="ExternalInput").ap()
    b1r = nc.dram_tensor("b1r", [128, 8], F32, kind="ExternalInput").ap()
    b2r = nc.dram_tensor("b2r", [128, 4], F32, kind="ExternalInput").ap()
    b3r = nc.dram_tensor("b3r", [128, 2], F32, kind="ExternalInput").ap()
    b4 = nc.dram_tensor("b4", [1, 1], F32, kind="ExternalInput").ap()
    gb1r = nc.dram_tensor("gb1r", [128, 4], F32, kind="ExternalInput").ap()
    gb2r = nc.dram_tensor("gb2r", [128, 2], F32, kind="ExternalInput").ap()
    gb3 = nc.dram_tensor("gb3", [1, 1], F32, kind="ExternalInput").ap()
    ar_in = nc.dram_tensor("ar_in", [1, 1], F32, kind="ExternalInput").ap()
    ident_in = nc.dram_tensor("ident_in", [128, 128], F32, kind="ExternalInput").ap()
    triu_in = nc.dram_tensor("triu_in", [128, 128], F32, kind="ExternalInput").ap()
    gidx_in = nc.dram_tensor("gidx_in", [128, 2], U32, kind="ExternalInput").ap()
    out = nc.dram_tensor("out", [T, D], F32, kind="ExternalOutput").ap()

    b_ctx_in = [nc.dram_tensor(f"b_ctx_in{i}", [128, 9], F32) for i in range(2)]
    b_ctx_out = [nc.dram_tensor(f"b_ctx_out{i}", [128, 9], F32) for i in range(2)]
    b_c_in = [nc.dram_tensor(f"b_c_in{i}", [4 * 128, 2 * TT], BF16)
              for i in range(2)]
    b_c_out = [nc.dram_tensor(f"b_c_out{i}", [8 * 128, 2 * TT], BF16)
               for i in range(2)]
    b_cq = nc.dram_tensor("b_cq", [DC, TFULL], BF16)
    b_ckv = nc.dram_tensor("b_ckv", [DC, TFULL], BF16)
    b_at_in = [nc.dram_tensor(f"b_at_in{p}", [128, TFULL], BF16) for p in range(4)]
    b_at_out = [nc.dram_tensor(f"b_at_out{p}", [512, T], BF16) for p in range(4)]

    st = contextlib.ExitStack()
    with st:
        sbt = lambda n, s, d: st.enter_context(nc.sbuf_tensor(n, s, d))
        pst = lambda n, s: st.enter_context(nc.psum_tensor(n, s, F32))
        sem = lambda n: Sem(st.enter_context(nc.semaphore(n)))

        X = sbt("X", [128, NCH * T], F32)
        ident = sbt("ident", [128, 128], F32)
        triu = sbt("triu", [128, 128], BF16)
        onesc = sbt("onesc", [128, 1], BF16)
        onesr = sbt("onesr", [1, 128], F32)
        onesrb = sbt("onesrb", [1, 128], BF16)
        w1t = sbt("w1t", [128, 8 * 1024], BF16)
        w2t = sbt("w2t", [128, 8 * 512], BF16)
        w3t = sbt("w3t", [128, 4 * 256], BF16)
        w4t = sbt("w4t", [128, 2], BF16)
        gw1t = sbt("gw1t", [128, 8 * 512], BF16)
        gw2t = sbt("gw2t", [128, 4 * 256], BF16)
        gw3t = sbt("gw3t", [128, 2], BF16)
        b1t = sbt("b1t", [128, 8], F32)
        b2t = sbt("b2t", [128, 4], F32)
        b3t = sbt("b3t", [128, 2], F32)
        gb1t = sbt("gb1t", [128, 4], F32)
        gb2t = sbt("gb2t", [128, 2], F32)
        ar_t = sbt("ar_t", [1, 1], F32)
        b4_t = sbt("b4_t", [1, 1], F32)
        gb3_t = sbt("gb3_t", [1, 1], F32)
        hb4_t = sbt("hb4_t", [1, 1], F32)
        hgb3_t = sbt("hgb3_t", [1, 1], F32)
        eps_t = sbt("eps_t", [1, 1], F32)
        arx_t = sbt("arx_t", [1, 1], F32)
        tgf_t = sbt("tgf_t", [1, 1], F32)
        ab_row = sbt("ab_row", [1, 2], F32)
        AB = sbt("AB", [128, 2], F32)
        ctx_acc = sbt("ctx_acc", [128, 9], F32)
        ctx_parts = sbt("ctx_parts", [128, 32], F32)
        corr_t = sbt("corr_t", [1, 1], F32)
        corr_red = sbt("corr_red", [1, 1], F32)
        corrb = sbt("corrb", [128, 1], F32)
        ones64 = sbt("ones64", [65, 128], F32)
        ones64b = sbt("ones64b", [65, 128], BF16)
        ctx_bf = sbt("ctx_bf", [128, 8], BF16)
        g1_bf = sbt("g1_bf", [128, 4], BF16)
        g2_bf = sbt("g2_bf", [128, 2], BF16)
        mean_row = sbt("mean_row", [1, T], BF16)
        rstd_row = sbt("rstd_row", [1, T], F32)
        gidx = sbt("gidx", [128, 2], U32)

        pA = pst("pA", [128, 1024])
        pB = pst("pB", [128, 1024])
        pC = pst("pC", [128, 1024])
        pD = pst("pD", [128, 1024])

        s_pe = sem("s_pe")
        s_act = sem("s_act")
        s_dve = sem("s_dve")
        s_pool = sem("s_pool")
        dmaS = sem("dmaS")
        dmaY = sem("dmaY")
        dmaY2 = sem("dmaY2")
        dmaG8 = [sem(f"dmaG8_{i}") for i in range(8)]
        dmaWC = [sem(f"dmaWC_{i}") for i in range(8)]
        dmaC = sem("dmaC")
        dmaO = sem("dmaO")
        dmaSL = sem("dmaSL")
        dmaGT = sem("dmaGT")
        dmaGT2 = sem("dmaGT2")
        dmaW = sem("dmaW")
        dmaW2 = sem("dmaW2")
        dmaW4 = sem("dmaW4")
        dmaW3 = sem("dmaW3")
        slotY = [sem(f"slY{i}") for i in range(6)]
        slotC = [sem(f"slC{i}") for i in range(2)]
        cc = [sem(f"cc{i}") for i in range(7)]
        ccy = [sem(f"ccy{g}") for g in range(4)]

        blk = st.enter_context(nc.Block())

        @blk.gpsimd
        def _(gq):
            pe = Q(nc.tensor)
            act = Q(nc.scalar)
            dve = Q(nc.vector)
            pool = Q(nc.gpsimd)
            syn = Q(nc.sync)
            engines = [(pe, s_pe), (act, s_act), (dve, s_dve), (pool, s_pool)]

            def barrier():
                toks = [s.tok for (_, s) in engines]
                for (q, s) in engines:
                    for t in toks:
                        if t[0] is not s:
                            q.wait(t)

            def sdma(sem_, dst, src, q=syn):
                q.eq.dma_start(out=dst, in_=src).then_inc(sem_.h, 16)
                sem_.c += 16
                return (sem_, sem_.c)

            for dst, src in ((ident[:], ident_in), (b1t[:], b1r), (b2t[:], b2r),
                             (b3t[:], b3r), (gb1t[:], gb1r), (gb2t[:], gb2r),
                             (ar_t[:], ar_in), (b4_t[:], b4), (gb3_t[:], gb3),
                             (gidx[:], gidx_in)):
                sdma(dmaS, dst, src)
            ld_small = dmaS.tok

            def gload(sem_, dst_ap, src_ap):
                pool.eq.dma_start(out=dst_ap, in_=src_ap).then_inc(sem_.h, 16)
                sem_.c += 16
                return (sem_, sem_.c)

            def wview(ap_, p, a, n):
                return bass.AP(ap_.tensor, 0, [[n, p], [p * n, a], [1, n]])

            gload(dmaW, triu[:], triu_in)
            gload(dmaW, w1t[:].rearrange("p (a n) -> p a n", a=8), wview(w1, 128, 8, 1024))
            gload(dmaW, w2t[:].rearrange("p (a n) -> p a n", a=8), wview(w2, 128, 8, 512))
            gload(dmaW, gw1t[:].rearrange("p (a n) -> p a n", a=8), wview(gw1, 128, 8, 512))
            gload(dmaW, w3t[:].rearrange("p (a n) -> p a n", a=4), wview(w3, 128, 4, 256))
            gload(dmaW, gw2t[:].rearrange("p (a n) -> p a n", a=4), wview(gw2, 128, 4, 256))
            gload(dmaW, w4t[:], w4r)
            gload(dmaW, gw3t[:], gw3r)
            ld_w = dmaW.tok

            t_on1 = s_pool.tick(pool.eq.memset(onesc[:], 1.0))
            t_on2 = s_pool.tick(pool.eq.memset(onesr[:], 1.0))
            t_on2b = s_pool.tick(pool.eq.memset(onesrb[:], 1.0))
            t_on64 = s_pool.tick(pool.eq.memset(ones64[:], 1.0))
            t_on64b = s_pool.tick(pool.eq.memset(ones64b[:], 1.0))
            t_eps = s_pool.tick(pool.eq.memset(eps_t[:], 1.0e-5))
            t_ones = [t_on1, t_on2, t_on2b, t_eps]
            dve.wait(ld_small)
            t_hb4 = s_dve.tick(nc.vector.tensor_scalar(hb4_t[:], b4_t[:], 0.5, None, ALU.mult))
            t_hgb3 = s_dve.tick(nc.vector.tensor_scalar(hgb3_t[:], gb3_t[:], 0.5, None, ALU.mult))
            t_arx = s_dve.tick(nc.vector.tensor_scalar(arx_t[:], ar_t[:], 0.7, None, ALU.mult))

            PBANKS = [(pA, 0), (pA, 512), (pB, 0), (pB, 512),
                      (pC, 0), (pC, 512), (pD, 0), (pD, 512)]

            barrier()

            def ln_stats_group(g, xbf, sq, var_row, st_):
                g0 = g * TT
                pe.wait(t_ones)
                mm_m = mm_s = None
                for chh in range(NCH):
                    bi = chh % 2
                    dve.wait(st_["xbf_free"][bi])
                    tx = s_dve.tick(nc.vector.tensor_copy(
                        xbf[bi], X[:, chh * T + g0: chh * T + g0 + TT]))
                    act.wait(tx)
                    act.wait(st_["sq_free"][bi])
                    tsq = s_act.tick(nc.scalar.activation(
                        sq[bi], xbf[bi], AF.Square))
                    pe.wait(tx)
                    mm_m = s_pe.tick(nc.tensor.matmul(
                        pC[0:1, 0:TT], onesc[:], xbf[bi],
                        start=(chh == 0), stop=(chh == NCH - 1)))
                    st_["xbf_free"][bi] = [mm_m, tsq]
                    pe.wait(tsq)
                    mm_s = s_pe.tick(nc.tensor.matmul(
                        pC[0:1, 512:512 + TT], onesc[:], sq[bi],
                        start=(chh == 0), stop=(chh == NCH - 1)))
                    st_["sq_free"][bi] = mm_s
                dve.wait(mm_m)
                t1 = s_dve.tick(nc.vector.tensor_scalar(
                    mean_row[:, g0:g0 + TT], pC[0:1, 0:TT], 1.0 / D, None, ALU.mult))
                dve.wait(mm_s)
                dve.wait(st_["t6"])
                t2 = s_dve.tick(nc.vector.tensor_scalar(
                    var_row[:, 0:TT], pC[0:1, 512:512 + TT], 1.0 / D, None, ALU.mult))
                dve.wait([t1, t2])
                t3 = s_dve.tick(nc.vector.tensor_tensor(
                    rstd_row[:, g0:g0 + TT], mean_row[:, g0:g0 + TT],
                    mean_row[:, g0:g0 + TT], ALU.mult))
                dve.wait(t3)
                t4 = s_dve.tick(nc.vector.tensor_tensor(
                    var_row[:, 0:TT], var_row[:, 0:TT], rstd_row[:, g0:g0 + TT],
                    ALU.subtract))
                act.wait(t4)
                act.wait(t_ones)
                t5 = s_act.tick(nc.scalar.activation(
                    var_row[:, 0:TT], var_row[:, 0:TT], AF.Ln, bias=eps_t[0:1, 0:1]))
                act.wait(t5)
                st_["t6"] = s_act.tick(nc.scalar.activation(
                    rstd_row[:, g0:g0 + TT], var_row[:, 0:TT], AF.Exp, scale=-0.5))
                pe.wait([t1, t2])
                return st_["t6"]

            def ln_stats(xbf, sq, var_row, prev_rstd):
                st_ = {"xbf_free": [None, None], "sq_free": [None, None],
                       "t6": prev_rstd}
                for g in range(NT):
                    t6 = ln_stats_group(g, xbf, sq, var_row, st_)
                return t6

            pa_state = {}

            def passA_begin():
                pool.wait(s_pe.tok)
                pool.wait(s_dve.tok)
                pool.wait(dmaS.tok)
                tz = s_pool.tick(pool.eq.memset(ctx_acc[:], 0.0))
                tzc = s_pool.tick(pool.eq.memset(corr_t[:], 0.0))
                pa_state.clear()
                pa_state.update(ttr=[None, None], tm3=None, zero=[tz, tzc],
                                first=True)

            def passA_group(g, scraps, dumps, varbuf, rdy, varw):
                g0 = g * TT
                pe.wait(t_ones)
                pe.wait(rdy)
                trb_ = s_pe.tick(nc.tensor.matmul(
                    pC[:, 512:512 + TT], onesr[:, 0:128],
                    rstd_row[:, g0:g0 + TT], start=True, stop=True))
                first_d = True
                tmu = None
                for chh in range(NCH):
                    si = chh % 2
                    dve.wait(pa_state["zero"][0])
                    dve.wait(trb_)
                    dve.wait(pa_state["ttr"][si])
                    tmu = s_dve.tick(nc.vector.tensor_tensor(
                        scraps[si][:],
                        X[:, chh * T + g0: chh * T + g0 + TT],
                        pC[:, 512:512 + TT], ALU.mult))
                    if first_d:
                        act.wait(s_pe.tok)
                        if pa_state["first"]:
                            act.wait(s_dve.tok)
                            pa_state["first"] = False
                        first_d = False
                    act.wait(tmu)
                    pa_state["ttr"][si] = s_act.tick(nc.scalar.activation(
                        dumps[si], scraps[si][:], AF.Identity,
                        accum_out=ctx_parts[:, g * NCH + chh: g * NCH + chh + 1]))
                dve.wait(pa_state["zero"][1])
                dve.wait(pa_state["tm3"])
                dve.wait(varw)
                tm1 = s_dve.tick(nc.vector.tensor_tensor(
                    varbuf[:, 0:TT], mean_row[:, g0:g0 + TT],
                    rstd_row[:, g0:g0 + TT], ALU.mult))
                dve.wait(tm1)
                tm2 = s_dve.tick(nc.vector.reduce_sum(
                    corr_red[:], varbuf[:, 0:TT], axis=AXX))
                dve.wait(tm2)
                pa_state["tm3"] = s_dve.tick(nc.vector.tensor_tensor(
                    corr_t[:], corr_t[:], corr_red[:], ALU.add))
                pe.wait(tmu)

            def passA_finalize(call):
                dve.wait(pa_state["tm3"])
                s_dve.tick(nc.vector.tensor_copy(ctx_acc[0:1, 8:9], corr_t[:]))
                dve.wait(s_dve.tok)
                dve.wait(pa_state["ttr"][0])
                dve.wait(pa_state["ttr"][1])
                t_s1 = s_dve.tick(nc.vector.tensor_tensor(
                    ctx_parts[:, 0:8], ctx_parts[:, 0:8], ctx_parts[:, 8:16], ALU.add))
                t_s2 = s_dve.tick(nc.vector.tensor_tensor(
                    ctx_parts[:, 16:24], ctx_parts[:, 16:24], ctx_parts[:, 24:32], ALU.add))
                dve.wait([t_s1, t_s2])
                s_dve.tick(nc.vector.tensor_tensor(
                    ctx_acc[:, 0:8], ctx_parts[:, 0:8], ctx_parts[:, 16:24], ALU.add))
                syn.wait(s_dve.tok)
                sdma(dmaS, b_ctx_in[call].ap(), ctx_acc[:])
                pool.wait(dmaS.tok)
                icc = 0 if call == 0 else 6
                tcol = cc[icc].tick(nc.gpsimd.collective_compute(
                    "AllReduce", ALU.add, replica_groups=PAIRS,
                    ins=[b_ctx_in[call].ap()], outs=[b_ctx_out[call].ap()]))
                syn.wait(tcol)
                sdma(dmaS, ctx_acc[:], b_ctx_out[call].ap())
                return dmaS.tok

            def ln_bcast(g0, rdy=None):
                pe.wait(rdy)
                pe.wait(t_ones)
                tm = s_pe.tick(nc.tensor.matmul(
                    pC[:, 0:TT], onesrb[:, 0:128], mean_row[:, g0:g0 + TT],
                    start=True, stop=True))
                tr = s_pe.tick(nc.tensor.matmul(
                    pC[:, 512:512 + TT], onesr[:, 0:128], rstd_row[:, g0:g0 + TT],
                    start=True, stop=True))
                return [tm, tr]

            prev_rstd = [None]
            hook_state = {"pd0": None, "pd1": None, "var_rd": None}

            def ffn(call, xload=None, stats_done=False, pre_ctx=None,
                    post_group=None):
                with contextlib.ExitStack() as fst:
                    fsb = lambda n, s, d: fst.enter_context(
                        nc.sbuf_tensor(f"{n}_f{call}", s, d))
                    stateA = fsb("stateA", [128, NCH * TT], F32)
                    stateB = fsb("stateB", [128, NCH * TT], F32)
                    stateC = fsb("stateC", [128, NCH * TT], F32)
                    state_bf = fsb("state_bf", [128, NCH * TT], BF16)
                    state_bf1 = fsb("state_bf1", [128, NCH * TT], BF16)
                    h1b = fsb("h1b", [128, 8 * TT], BF16)
                    h3b = h1b
                    t1f = fsb("t1f", [128, TT], F32)
                    cb = fsb("cb", [128, TT], F32)
                    tw = [fsb(f"tw{i}", [128, TT], F32) for i in range(2)]
                    hfb = [fsb("hfb0", [128, TT], F32), fsb("hfb1", [128, TT], F32)]
                    var_row = fsb("var_row", [1, TT], F32)
                    tanh_bf = fsb("tanh_bf", [1, TT], BF16)

                    xbf = [state_bf[:, 0:TT], state_bf[:, TT:2 * TT]]
                    sq = [state_bf1[:, 0:TT], state_bf1[:, TT:2 * TT]]
                    if not stats_done:
                        lnst0 = {"xbf_free": [None, None], "sq_free": [None, None],
                                 "t6": prev_rstd[0]}
                        passA_begin()
                        scrapsA = [hfb[0], hfb[1]]
                        dumpsA = [state_bf1[:, 0:TT], state_bf1[:, TT:2 * TT]]
                        for g in range(NT):
                            if xload is not None:
                                xload(g, stateC)
                            prev_rstd[0] = ln_stats_group(g, xbf, sq, var_row,
                                                          lnst0)
                        for g in range(NT):
                            passA_group(g, scrapsA, dumpsA, var_row,
                                        prev_rstd[0], prev_rstd[0])
                        ctx_ld = passA_finalize(call)
                    else:
                        ctx_ld = pre_ctx

                    H6 = [(pA, 0), (pA, 512), (pB, 0), (pB, 512),
                          (pD, 0), (pD, 512)]
                    bank_free = [xst["bank"][i] for i in range(6)]
                    hk6 = [0]
                    PERM = [4, 5, 6, 7, 0, 1, 2, 3]
                    SGN = [+1, +1, -1, -1, -1, -1, +1, +1]
                    prev_ttr = None
                    prev_cb_reader = None
                    res_done = None

                    res_free = [None, None]
                    SBUFS = [stateA, stateB, stateC]
                    BFS = [state_bf, state_bf1]
                    IDX_I = [0, 2, 1, 0]
                    IDX_O = [1, 0, 2, 1]
                    passb_dve_last = None

                    def pass_b(g):
                        nonlocal passb_dve_last
                        g0 = g * TT
                        if g == 1:
                            dve.wait(pa_state["ttr"][0])
                            dve.wait(pa_state["ttr"][1])
                        pe.wait(passb_dve_last)
                        bct = ln_bcast(g0)
                        dst_s = SBUFS[IDX_I[g]]
                        dst_b = BFS[g % 2]
                        bf_free = s_pe.tok
                        toks = [None] * NCH
                        tbm = None
                        for chh in range(NCH):
                            dve.wait(bct)
                            dve.wait(tbm)
                            dve.wait(hook_state["var_rd"])
                            ta = s_dve.tick(nc.vector.tensor_tensor(
                                t1f[:], X[:, chh * T + g0: chh * T + g0 + TT],
                                pC[:, 0:TT], ALU.subtract))
                            dve.wait(ta)
                            tbm = s_dve.tick(nc.vector.tensor_tensor(
                                dst_s[:, chh * TT:(chh + 1) * TT], t1f[:],
                                pC[:, 512:512 + TT], ALU.mult))
                            dve.wait(tbm)
                            dve.wait(bf_free)
                            toks[chh] = s_dve.tick(nc.vector.tensor_copy(
                                dst_b[:, chh * TT:(chh + 1) * TT],
                                dst_s[:, chh * TT:(chh + 1) * TT]))
                        passb_dve_last = tbm
                        return toks

                    pe.wait(ld_w)
                    act.wait(ld_small)
                    cast_next = pass_b(0)

                    gc_state = {}

                    def emit_gc():
                        pe.wait(ctx_ld)
                        pe.wait(t_ones)
                        pe.wait(bank_free[4])
                        tcmm = s_pe.tick(nc.tensor.matmul(
                            pD[:, 13:14], onesr[:, 0:128], ctx_acc[0:1, 8:9],
                            start=True, stop=True))
                        dve.wait(tcmm)
                        tcpy = s_dve.tick(nc.vector.tensor_copy(corrb[:], pD[:, 13:14]))
                        dve.wait(tcpy)
                        tcb = s_dve.tick(nc.vector.tensor_scalar(
                            ctx_bf[:], ctx_acc[:, 0:8], corrb[:, 0:1], 1.0 / TFULL,
                            ALU.subtract, ALU.mult))
                        pe.wait(tcb)
                        pe.wait(ld_w)
                        tga = None
                        for fo in range(4):
                            pe.wait(tga)
                            for fi in range(8):
                                tg = s_pe.tick(nc.tensor.matmul(
                                    pD[:, fo:fo + 1],
                                    gw1t[:, fi * 512 + fo * 128: fi * 512 + (fo + 1) * 128],
                                    ctx_bf[:, fi:fi + 1], start=(fi == 0), stop=(fi == 7)))
                            act.wait(tg)
                            tga = s_act.tick(nc.scalar.activation(
                                g1_bf[:, fo:fo + 1], pD[:, fo:fo + 1], AF.Gelu,
                                bias=gb1t[:, fo:fo + 1]))
                        tg2 = None
                        for fo in range(2):
                            pe.wait(tga if tg2 is None else tg2)
                            pe.wait(s_act.tok)
                            for fi in range(4):
                                tg = s_pe.tick(nc.tensor.matmul(
                                    pD[:, 8 + fo:9 + fo],
                                    gw2t[:, fi * 256 + fo * 128: fi * 256 + (fo + 1) * 128],
                                    g1_bf[:, fi:fi + 1], start=(fi == 0), stop=(fi == 3)))
                            act.wait(tg)
                            tg2 = s_act.tick(nc.scalar.activation(
                                g2_bf[:, fo:fo + 1], pD[:, 8 + fo:9 + fo], AF.Gelu,
                                bias=gb2t[:, fo:fo + 1]))
                        pe.wait(s_act.tok)
                        for fi in range(2):
                            tg = s_pe.tick(nc.tensor.matmul(
                                pD[0:1, 12:13], gw3t[:, fi:fi + 1], g2_bf[:, fi:fi + 1],
                                start=(fi == 0), stop=(fi == 1)))
                        act.wait(tg)
                        act.wait(t_hgb3)
                        ttg = s_act.tick(nc.scalar.activation(
                            tgf_t[:], pD[0:1, 12:13], AF.Tanh, bias=hgb3_t[0:1, 0:1],
                            scale=0.5))
                        dve.wait([ttg, t_arx])
                        ta1 = s_dve.tick(nc.vector.tensor_tensor(
                            ab_row[:, 0:1], tgf_t[:], arx_t[:], ALU.mult))
                        dve.wait(ta1)
                        ta2 = s_dve.tick(nc.vector.tensor_scalar(
                            ab_row[:, 0:1], ab_row[:, 0:1], 0.1, None, ALU.add))
                        dve.wait(ta2)
                        ta3 = s_dve.tick(nc.vector.tensor_scalar(
                            ab_row[:, 1:2], ar_t[:], 0.3, None, ALU.mult))
                        pe.wait(ta3)
                        tab = s_pe.tick(nc.tensor.matmul(
                            pD[:, 14:16], onesr[:, 0:128], ab_row[:], start=True, stop=True))
                        dve.wait(tab)
                        tAB = s_dve.tick(nc.vector.tensor_copy(AB[:], pD[:, 14:16]))
                        pe.wait(tAB)

                        gc_state["tAB"] = tAB
                        bank_free[4] = tAB

                    for g in range(NT):
                        g0 = g * TT
                        cast_tok = cast_next
                        s_in = SBUFS[IDX_I[g]]
                        s_out = SBUFS[IDX_O[g]]
                        state_b = BFS[g % 2]
                        for cyc in range(3):
                            src = s_in if cyc % 2 == 0 else s_out
                            dst = s_out if cyc % 2 == 0 else s_in
                            ev1 = [None] * 8
                            stopt = [None] * 8
                            b1m = [(hk6[0] + i) % 6 for i in range(8)]
                            b2m = [(hk6[0] + 8 + i) % 6 for i in range(4)]
                            b3m = [(hk6[0] + 12 + i) % 6 for i in range(2)]
                            hk6[0] = (hk6[0] + 14) % 6
                            for fog in range(2):
                                fos = list(range(fog * 4, fog * 4 + 4))
                                for fi in range(8):
                                    pe.wait(cast_tok[fi])
                                    for fo in fos:
                                        bk = H6[b1m[fo]]
                                        if fi == 0:
                                            pe.wait(bank_free[b1m[fo]])
                                        tmm = s_pe.tick(nc.tensor.matmul(
                                            bk[0][:, bk[1]:bk[1] + TT],
                                            w1t[:, fi * 1024 + fo * 128: fi * 1024 + (fo + 1) * 128],
                                            state_b[:, fi * TT:(fi + 1) * TT],
                                            start=(fi == 0), stop=(fi == 7)))
                                        if fi == 7:
                                            stopt[fo] = tmm
                                for fo in fos:
                                    bk = H6[b1m[fo]]
                                    act.wait(stopt[fo])
                                    ev1[fo] = s_act.tick(nc.scalar.activation(
                                        h1b[:, fo * TT:(fo + 1) * TT],
                                        bk[0][:, bk[1]:bk[1] + TT],
                                        AF.Gelu, bias=b1t[:, fo:fo + 1]))
                                    bank_free[b1m[fo]] = ev1[fo]
                            ev2 = [None] * 4
                            for fi in range(8):
                                pe.wait(ev1[fi])
                                for fo in range(4):
                                    bk = H6[b2m[fo]]
                                    if fi == 0:
                                        pe.wait(bank_free[b2m[fo]])
                                    tmm = s_pe.tick(nc.tensor.matmul(
                                        bk[0][:, bk[1]:bk[1] + TT],
                                        w2t[:, fi * 512 + fo * 128: fi * 512 + (fo + 1) * 128],
                                        h1b[:, fi * TT:(fi + 1) * TT],
                                        start=(fi == 0), stop=(fi == 7)))
                                    if fi == 7:
                                        stopt[fo] = tmm
                            for fo in range(4):
                                bk = H6[b2m[fo]]
                                act.wait(stopt[fo])
                                ev2[fo] = s_act.tick(nc.scalar.activation(
                                    h1b[:, (2 + fo) * TT:(3 + fo) * TT],
                                    bk[0][:, bk[1]:bk[1] + TT],
                                    AF.Gelu, bias=b2t[:, fo:fo + 1]))
                                bank_free[b2m[fo]] = ev2[fo]
                            ev3 = [None] * 2
                            for fi in range(4):
                                pe.wait(ev2[fi])
                                for fo in range(2):
                                    bk = H6[b3m[fo]]
                                    if fi == 0:
                                        pe.wait(bank_free[b3m[fo]])
                                    tmm = s_pe.tick(nc.tensor.matmul(
                                        bk[0][:, bk[1]:bk[1] + TT],
                                        w3t[:, fi * 256 + fo * 128: fi * 256 + (fo + 1) * 128],
                                        h1b[:, (2 + fi) * TT:(3 + fi) * TT],
                                        start=(fi == 0), stop=(fi == 3)))
                                    if fi == 3:
                                        stopt[fo] = tmm
                            for fo in range(2):
                                bk = H6[b3m[fo]]
                                act.wait(stopt[fo])
                                ev3[fo] = s_act.tick(nc.scalar.activation(
                                    h3b[:, fo * TT:(fo + 1) * TT],
                                    bk[0][:, bk[1]:bk[1] + TT],
                                    AF.Gelu, bias=b3t[:, fo:fo + 1]))
                                bank_free[b3m[fo]] = ev3[fo]
                            if g == 0 and cyc == 0:
                                emit_gc()
                            pe.wait(bank_free[4])
                            for fi in range(2):
                                pe.wait(ev3[fi])
                                tz2 = s_pe.tick(nc.tensor.matmul(
                                    pD[0:1, 0:TT], w4t[:, fi:fi + 1],
                                    h3b[:, fi * TT:(fi + 1) * TT],
                                    start=(fi == 0), stop=(fi == 1)))
                            act.wait([tz2, t_hb4])
                            prev_ttr = s_act.tick(nc.scalar.activation(
                                tanh_bf[:, 0:TT], pD[0:1, 0:TT], AF.Tanh,
                                bias=hb4_t[0:1, 0:1], scale=0.5))
                            pe.wait(prev_ttr)
                            pe.wait(bank_free[5])
                            tbc = s_pe.tick(nc.tensor.matmul(
                                pD[:, 512:512 + TT], onesrb[:, 0:128],
                                tanh_bf[:, 0:TT], start=True, stop=True))
                            dve.wait([tbc, gc_state["tAB"]])
                            tcb2 = s_dve.tick(nc.vector.tensor_scalar(
                                cb[:], pD[:, 512:512 + TT], AB[:, 1:2], AB[:, 0:1],
                                ALU.mult, ALU.add))
                            prev_cb_reader = tcb2
                            bank_free[4] = prev_ttr
                            bank_free[5] = tcb2
                            pe.wait(tcb2)
                            tw_free = [None] * 2
                            tw1s = [None] * NCH
                            tw2s = [None] * NCH
                            for ch0 in range(0, NCH, 2):
                                for chh in (ch0, ch0 + 1):
                                    dve.wait(tcb2)
                                    dve.wait(tw_free[chh % 2])
                                    tw1s[chh] = s_dve.tick(nc.vector.tensor_tensor(
                                        tw[chh % 2][:],
                                        src[:, PERM[chh] * TT:(PERM[chh] + 1) * TT],
                                        cb[:], ALU.mult))
                                for chh in (ch0, ch0 + 1):
                                    dve.wait(tw1s[chh])
                                    tw2s[chh] = s_dve.tick(nc.vector.tensor_tensor(
                                        dst[:, chh * TT:(chh + 1) * TT],
                                        src[:, chh * TT:(chh + 1) * TT],
                                        tw[chh % 2][:],
                                        ALU.add if SGN[chh] > 0 else ALU.subtract))
                                    tw_free[chh % 2] = tw2s[chh]
                                if cyc < 2:
                                    for chh in (ch0, ch0 + 1):
                                        dve.wait(tw2s[chh])
                                        cast_tok[chh] = s_dve.tick(nc.vector.tensor_copy(
                                            state_b[:, chh * TT:(chh + 1) * TT],
                                            dst[:, chh * TT:(chh + 1) * TT]))
                                else:
                                    cast_tok[ch0] = tw2s[ch0]
                                    cast_tok[ch0 + 1] = tw2s[ch0 + 1]
                            if cyc == 1 and g + 1 < NT:
                                cast_next = pass_b(g + 1)
                        for chh in range(NCH):
                            hi = chh % 2
                            dve.wait(cast_tok[chh])
                            dve.wait(res_free[hi])
                            th = s_dve.tick(nc.vector.tensor_scalar(
                                hfb[hi][:], s_out[:, chh * TT:(chh + 1) * TT],
                                0.5, None, ALU.mult))
                            dve.wait(th)
                            res_free[hi] = s_dve.tick(nc.vector.tensor_tensor(
                                X[:, chh * T + g0: chh * T + g0 + TT],
                                X[:, chh * T + g0: chh * T + g0 + TT],
                                hfb[hi][:], ALU.add))
                    barrier()

            PB6 = [(pA, 0), (pA, 512), (pB, 0), (pB, 512), (pD, 0), (pD, 512)]
            xst = {"bank": [None] * 6, "slot": [None] * 4, "k": 0}

            def xload(g, stg):
                pe.wait(dmaS.tok)
                for i in range(4 * g, 4 * g + 4):
                    sl = i % 4
                    syn.wait(xst["slot"][sl])
                    sdma(slotY[sl], stg[:, sl * 1024:(sl + 1) * 1024],
                         xo[i * 128:(i + 1) * 128, :])
                    ldt = slotY[sl].tok
                    pe.wait(ldt)
                    last = None
                    for ch in range(NCH):
                        bnk = xst["k"] % 6
                        pt, off = PB6[bnk]
                        pe.wait(xst["bank"][bnk])
                        ttr = s_pe.tick(nc.tensor.transpose(
                            pt[:, off:off + 128],
                            stg[:, sl * 1024 + ch * 128: sl * 1024 + (ch + 1) * 128],
                            ident[:]))
                        dve.wait(ttr)
                        last = s_dve.tick(nc.vector.tensor_copy(
                            X[:, ch * T + i * 128: ch * T + (i + 1) * 128],
                            pt[:, off:off + 128]))
                        xst["bank"][bnk] = last
                        xst["k"] += 1
                    xst["slot"][sl] = last

            ffn(0, xload)

            with contextlib.ExitStack() as ast_:
                asbt = lambda n, s, d: ast_.enter_context(nc.sbuf_tensor(n, s, d))
                wuqt = asbt("wuqt", [128, 2 * 512], BF16)
                wukt = asbt("wukt", [128, 2 * 512], BF16)
                wuvt = asbt("wuvt", [128, 2 * 512], BF16)
                cqs_stack = contextlib.ExitStack()
                cq_sb = cqs_stack.enter_context(nc.sbuf_tensor("cq_sb", [128, 2 * TFULL], BF16))
                ckv_sb = cqs_stack.enter_context(nc.sbuf_tensor("ckv_sb", [128, 2 * TFULL], BF16))

                prj = contextlib.ExitStack()
                psbt = lambda n, s, d: prj.enter_context(nc.sbuf_tensor(n, s, d))
                xbf = [psbt("a_xbf0", [128, TT], BF16)[:], psbt("a_xbf1", [128, TT], BF16)[:]]
                sq = [psbt("a_sq0", [128, TT], BF16)[:], psbt("a_sq1", [128, TT], BF16)[:]]
                var_row = psbt("a_var", [1, TT], F32)
                t1f = psbt("a_t1f", [128, TT], F32)
                ybufs = [psbt("a_y2b0", [128, TT], BF16), psbt("a_y2b1", [128, TT], BF16)]
                y2g_sb2 = [psbt("y2gA", [128, NCH * TT], BF16),
                           psbt("y2gB", [128, NCH * TT], BF16)]
                y2g_rd = [[None] * NCH, [None] * NCH]
                slot_rd = [None] * 4
                wdqt = psbt("wdqt", [128, 8 * 256], BF16)
                wdkvt = psbt("wdkvt", [128, 8 * 256], BF16)

                gload(dmaW2, wdqt[:].rearrange("p (a n) -> p a n", a=8), wview(wdq, 128, 8, 256))
                gload(dmaW2, wdkvt[:].rearrange("p (a n) -> p a n", a=8), wview(wdkv, 128, 8, 256))
                ld_attw = dmaW2.tok
                gload(dmaW4, wuqt[:].rearrange("p (a n) -> p a n", a=2), wview(wuq, 128, 2, 512))
                gload(dmaW4, wukt[:].rearrange("p (a n) -> p a n", a=2), wview(wuk, 128, 2, 512))
                gload(dmaW4, wuvt[:].rearrange("p (a n) -> p a n", a=2), wview(wuv, 128, 2, 512))
                ld_attw2 = dmaW4.tok

                lnst = {"xbf_free": [None, None], "sq_free": [None, None],
                        "t6": prev_rstd[0]}
                t6s = [None] * NT

                def ln_stats0(g):
                    g0 = g * TT
                    pe.wait(t_ones)
                    mm_m = mm_s = None
                    for chh in range(NCH):
                        bi = chh % 2
                        act.wait(lnst["xbf_free"][bi])
                        tx = s_act.tick(nc.scalar.activation(
                            xbf[bi], X[:, chh * T + g0: chh * T + g0 + TT],
                            AF.Identity))
                        pe.wait(tx)
                        mm_m = s_pe.tick(nc.tensor.matmul(
                            pC[0:1, 0:TT], onesc[:], xbf[bi],
                            start=(chh == 0), stop=(chh == NCH - 1)))
                        lnst["xbf_free"][bi] = mm_m
                        act.wait(lnst["sq_free"][bi])
                        tsq = s_act.tick(nc.scalar.activation(
                            sq[bi], X[:, chh * T + g0: chh * T + g0 + TT],
                            AF.Square))
                        pe.wait(tsq)
                        mm_s = s_pe.tick(nc.tensor.matmul(
                            pC[0:1, 512:512 + TT], onesc[:], sq[bi],
                            start=(chh == 0), stop=(chh == NCH - 1)))
                        lnst["sq_free"][bi] = mm_s
                    dve.wait(mm_m)
                    t1 = s_dve.tick(nc.vector.tensor_scalar(
                        mean_row[:, g0:g0 + TT], pC[0:1, 0:TT], 1.0 / D, None,
                        ALU.mult))
                    dve.wait(mm_s)
                    dve.wait(lnst["t6"])
                    t2 = s_dve.tick(nc.vector.tensor_scalar(
                        var_row[:, 0:TT], pC[0:1, 512:512 + TT], 1.0 / D, None,
                        ALU.mult))
                    dve.wait([t1, t2])
                    t3 = s_dve.tick(nc.vector.tensor_tensor(
                        rstd_row[:, g0:g0 + TT], mean_row[:, g0:g0 + TT],
                        mean_row[:, g0:g0 + TT], ALU.mult))
                    dve.wait(t3)
                    t4 = s_dve.tick(nc.vector.tensor_tensor(
                        var_row[:, 0:TT], var_row[:, 0:TT],
                        rstd_row[:, g0:g0 + TT], ALU.subtract))
                    act.wait(t4)
                    act.wait(t_ones)
                    t5 = s_act.tick(nc.scalar.activation(
                        var_row[:, 0:TT], var_row[:, 0:TT], AF.Ln,
                        bias=eps_t[0:1, 0:1]))
                    act.wait(t5)
                    lnst["t6"] = s_act.tick(nc.scalar.activation(
                        rstd_row[:, g0:g0 + TT], var_row[:, 0:TT], AF.Exp,
                        scale=-0.5))
                    pe.wait([t1, t2])
                    return lnst["t6"]
                tal = [None] * NT
                tbl = [None] * NT
                pjbank = [(pA, 0), (pA, 512), (pB, 0), (pB, 512)]
                bank_ev = [None] * 4
                tbg = [[None] * NCH for _ in range(NT)]
                tb = None

                def emit_proj(g):
                    gb = g % 2
                    pe.wait(ld_attw)
                    stop4 = [None] * 4
                    for ci in range(4):
                        kind, co = ci // 2, ci % 2
                        pt, off = pjbank[ci]
                        for fi in range(NCH):
                            if fi == 0:
                                pe.wait(bank_ev[ci])
                            pe.wait(tbg[g][fi])
                            tmm = s_pe.tick(nc.tensor.matmul(
                                pt[:, off:off + TT],
                                (wdqt if kind == 0 else wdkvt)[
                                    :, fi * 256 + co * 128: fi * 256 + (co + 1) * 128],
                                y2g_sb2[gb][:, fi * TT:(fi + 1) * TT],
                                start=(fi == 0), stop=(fi == NCH - 1)))
                            if ci == 3:
                                y2g_rd[gb][fi] = tmm
                        stop4[ci] = tmm
                    for ci in range(4):
                        kind, co = ci // 2, ci % 2
                        pt, off = pjbank[ci]
                        sb = ybufs[ci % 2]
                        ysem = dmaY if ci % 2 == 0 else dmaY2
                        dve.wait(stop4[ci])
                        dve.wait(ysem.tok)
                        tev = s_dve.tick(nc.vector.tensor_copy(
                            sb[:], pt[:, off:off + TT]))
                        bank_ev[ci] = tev
                        syn.wait(tev)
                        sdma(ysem,
                             bass.AP(b_c_in[g // 2],
                                     (kind * 2 + co) * 128 * 2 * TT
                                     + (g % 2) * TT,
                                     [[2 * TT, 128], [1, TT]]),
                             sb[:])

                def emit_y2(g):
                    g0 = g * TT
                    pe.wait([tal[g - 1], tbl[g - 1]] if g >= 1 else None)
                    pe.wait(t6s[g])
                    pe.wait(t_ones)
                    bm = s_pe.tick(nc.tensor.matmul(
                        pD[:, 0:TT], onesrb[:, 0:128], mean_row[:, g0:g0 + TT],
                        start=True, stop=True))
                    br = s_pe.tick(nc.tensor.matmul(
                        pD[:, 512:512 + TT], onesr[:, 0:128],
                        rstd_row[:, g0:g0 + TT], start=True, stop=True))
                    ta = tb = None
                    for chh in range(NCH):
                        dve.wait([bm, br])
                        dve.wait(tb)
                        ta = s_dve.tick(nc.vector.tensor_tensor(
                            t1f[:], X[:, chh * T + g0: chh * T + g0 + TT],
                            pD[:, 0:TT], ALU.subtract))
                        dve.wait(ta)
                        dve.wait(y2g_rd[g % 2][chh])
                        tb = s_dve.tick(nc.vector.tensor_tensor(
                            y2g_sb2[g % 2][:, chh * TT:(chh + 1) * TT], t1f[:],
                            pD[:, 512:512 + TT], ALU.mult))
                        tbg[g][chh] = tb
                    tal[g] = ta
                    tbl[g] = tb

                t_cag = [None, None]

                def fire_cag(hf):
                    pool.wait(dmaY.tok)
                    pool.wait(dmaY2.tok)
                    t_cag[hf] = ccy[hf].tick(nc.gpsimd.collective_compute(
                        "AllGather", ALU.bypass, replica_groups=PAIRS,
                        ins=[b_c_in[hf].ap()], outs=[b_c_out[hf].ap()]))

                for g in range(NT):
                    t6s[g] = (ln_stats0(g) if g <= 1 else
                              ln_stats_group(g, xbf, sq, var_row, lnst))
                    emit_y2(g - 1) if g >= 1 else None
                    if g >= 2:
                        emit_proj(g - 2)
                    if g == 3:
                        fire_cag(0)
                prev_rstd[0] = t6s[NT - 1]
                emit_y2(NT - 1)
                emit_proj(NT - 2)
                emit_proj(NT - 1)
                fire_cag(1)

                sl = 0
                for hf in range(2):
                    for rank, kind, co in ((r, k, c) for r in range(2)
                                           for k in range(2) for c in range(2)):
                        syn.wait(t_cag[hf])
                        syn.wait(slot_rd[sl % 4])
                        dst = cq_sb if kind == 0 else ckv_sb
                        sdma(slotY[sl % 4],
                             dst[:, co * TFULL + rank * T + hf * 2 * TT:
                                 co * TFULL + rank * T + (hf + 1) * 2 * TT],
                             bass.AP(b_c_out[hf],
                                     (rank * 4 + kind * 2 + co) * 128 * 2 * TT,
                                     [[2 * TT, 128], [1, 2 * TT]]))
                        slot_rd[sl % 4] = slotY[sl % 4].tok
                        sl += 1
                for s_ in slotY:
                    pe.wait((s_, s_.c))
                barrier()
                prj.close()

                plp = contextlib.ExitStack()
                lsbt = lambda n, s, d: plp.enter_context(nc.sbuf_tensor(n, s, d))
                kpair = lsbt("kpair", [128, TFULL], BF16)
                qpair = lsbt("qpair", [128, TFULL], BF16)
                vhA = lsbt("vhA", [128, 32 * 65], BF16)
                vhB = lsbt("vhB", [128, 32 * 65], BF16)
                expb2 = [[lsbt(f"exp{h}{par}", [128, 1024], BF16)
                          for par in range(2)] for h in range(2)]
                slab_t = lsbt("slab", [128, TFULL], BF16)
                recT = lsbt("recT", [65, 1024], F32)
                recB = lsbt("recB", [65, 1024], BF16)
                rbc4 = lsbt("rbc4", [64, 1024], F32)

                vA_ones = vhA[:].rearrange("p (c k) -> p c k", k=65)[:, :, 64:65]
                vB_ones = vhB[:].rearrange("p (c k) -> p c k", k=65)[:, :, 64:65]
                t_vo = [s_pool.tick(nc.gpsimd.memset(vA_ones, 1.0)),
                        s_pool.tick(nc.gpsimd.memset(vB_ones, 1.0))]

                score_ps = [pA, pB]
                av_ps = [pC, pD]
                sbl = {}
                avl = {}
                ebl = {}
                rec_free = [None] * 2
                rbc_free = [None] * 2
                slab_store = None
                dve.wait(ld_w)
                pe.wait(ld_attw2)
                for p in range(4):
                    kev_g = [sbl.get((0, 0)), sbl.get((0, 1))]
                    for sub in range(8):
                        pt, off = (pA, 0) if sub % 2 == 0 else (pA, 512)
                        pe.wait(kev_g[sub % 2])
                        for a in range(2):
                            tkm = s_pe.tick(nc.tensor.matmul(
                                pt[:, off:off + TT],
                                wukt[:, a * 512 + p * 128: a * 512 + (p + 1) * 128],
                                ckv_sb[:, a * TFULL + sub * TT: a * TFULL + (sub + 1) * TT],
                                start=(a == 0), stop=(a == 1)))
                        dve.wait(tkm)
                        kev_g[sub % 2] = s_dve.tick(nc.vector.tensor_copy(
                            kpair[:, sub * TT:(sub + 1) * TT], pt[:, off:off + TT]))
                    vev_g = [sbl.get((1, 0)), sbl.get((1, 1))]
                    for tc in range(32):
                        pt, off = (pB, 0) if tc % 2 == 0 else (pB, 512)
                        pe.wait(vev_g[tc % 2])
                        for a in range(2):
                            tvm = s_pe.tick(nc.tensor.matmul(
                                pt[:, off:off + 128],
                                ckv_sb[:, a * TFULL + tc * 128: a * TFULL + (tc + 1) * 128],
                                wuvt[:, a * 512 + p * 128: a * 512 + (p + 1) * 128],
                                start=(a == 0), stop=(a == 1)))
                        dve.wait(tvm)
                        s_dve.tick(nc.vector.tensor_copy(
                            vhA[:, tc * 65: tc * 65 + 64], pt[:, off:off + 64]))
                        vev_g[tc % 2] = s_dve.tick(nc.vector.tensor_copy(
                            vhB[:, tc * 65: tc * 65 + 64], pt[:, off + 64:off + 128]))
                    qev_g = [kev_g[0], kev_g[1]]
                    for sub in range(8):
                        pt, off = (pA, 0) if sub % 2 == 0 else (pA, 512)
                        pe.wait(qev_g[sub % 2])
                        for a in range(2):
                            tqm = s_pe.tick(nc.tensor.matmul(
                                pt[:, off:off + TT],
                                wuqt[:, a * 512 + p * 128: a * 512 + (p + 1) * 128],
                                cq_sb[:, a * TFULL + sub * TT: a * TFULL + (sub + 1) * TT],
                                start=(a == 0), stop=(a == 1)))
                        dve.wait(tqm)
                        qev_g[sub % 2] = s_dve.tick(nc.vector.tensor_copy(
                            qpair[:, sub * TT:(sub + 1) * TT], pt[:, off:off + TT]))
                    pe.wait(t_vo)
                    sbl[(0, 0)] = qev_g[0]
                    sbl[(0, 1)] = qev_g[1]
                    sbl[(1, 0)] = vev_g[0]
                    sbl[(1, 1)] = vev_g[1]

                    for rnd in range(4):
                        tA = 2 * rnd
                        tB = 2 * rnd + 1
                        q0A = tA * 512
                        q0B = tB * 512
                        nchkA = q0A // 128 + 4
                        nchkB = q0B // 128 + 4
                        last_av = {}
                        pend = None

                        def emit_avs(c, texp_c):
                            k0 = c * 128
                            for h in range(2):
                                vh = vhA if h == 0 else vhB
                                eb = expb2[h][c % 2]
                                avts = []
                                if c < nchkA:
                                    cstA = max(0, k0 - q0A)
                                    pe.wait(texp_c[(h, 0)])
                                    if c == 0:
                                        pe.wait(avl.get((h, 0)))
                                    avts.append(s_pe.tick(nc.tensor.matmul(
                                        av_ps[h][0:65, cstA:512],
                                        vh[:, c * 65:(c + 1) * 65],
                                        eb[:, cstA:512],
                                        start=(c == 0), stop=(c == nchkA - 1))))
                                    last_av[(h, 0)] = avts[-1]
                                cstB = max(0, k0 - q0B)
                                pe.wait(texp_c[(h, 1)])
                                if c == 0:
                                    pe.wait(avl.get((h, 1)))
                                avts.append(s_pe.tick(nc.tensor.matmul(
                                    av_ps[h][0:65, 512 + cstB:1024],
                                    vh[:, c * 65:(c + 1) * 65],
                                    eb[:, 512 + cstB:1024],
                                    start=(c == 0), stop=(c == nchkB - 1))))
                                last_av[(h, 1)] = avts[-1]
                                ebl[(h, c % 2)] = avts

                        for c in range(nchkB):
                            k0 = c * 128
                            texp_c = {}
                            for h in range(2):
                                r0 = h * 64
                                if c < nchkA:
                                    cstA = max(0, k0 - q0A)
                                    pe.wait(sbl.get((h, 0)))
                                    s_pe.tick(nc.tensor.matmul(
                                        score_ps[h][:, cstA:512],
                                        kpair[r0:r0 + 64, k0:k0 + 128],
                                        qpair[r0:r0 + 64, q0A + cstA:q0A + 512],
                                        start=True, stop=True))
                                cstB = max(0, k0 - q0B)
                                pe.wait(sbl.get((h, 1)))
                                sc = s_pe.tick(nc.tensor.matmul(
                                    score_ps[h][:, 512 + cstB:1024],
                                    kpair[r0:r0 + 64, k0:k0 + 128],
                                    qpair[r0:r0 + 64, q0B + cstB:q0B + 512],
                                    start=True, stop=True))
                                lo = max(0, k0 - q0A) if c < nchkA else 512 + cstB
                                eb = expb2[h][c % 2]
                                act.wait(sc)
                                act.wait(ebl.get((h, c % 2)))
                                texp = s_act.tick(nc.scalar.activation(
                                    eb[:, lo:1024], score_ps[h][:, lo:1024],
                                    AF.Exp, scale=0.125))
                                sbl[(h, 0)] = texp if c < nchkA else sbl.get((h, 0))
                                sbl[(h, 1)] = texp
                                mka = mkb = texp
                                if c < nchkA and k0 >= q0A:
                                    dve.wait(texp)
                                    mka = s_dve.tick(nc.vector.tensor_tensor(
                                        eb[:, cstA:cstA + 128], eb[:, cstA:cstA + 128],
                                        triu[:], ALU.mult))
                                if k0 >= q0B:
                                    dve.wait(texp)
                                    mkb = s_dve.tick(nc.vector.tensor_tensor(
                                        eb[:, 512 + cstB:512 + cstB + 128],
                                        eb[:, 512 + cstB:512 + cstB + 128],
                                        triu[:], ALU.mult))
                                texp_c[(h, 0)] = mka
                                texp_c[(h, 1)] = mkb
                            if pend is not None:
                                emit_avs(*pend)
                            pend = (c, texp_c)
                        emit_avs(*pend)
                        streams = [(tA, 0), (tA, 1), (tB, 0), (tB, 1)]
                        for g2 in range(2):
                            tln = {}
                            tex = {}
                            tbc = {}
                            trbs = {}
                            grp = [(g2 * 2 + j, streams[g2 * 2 + j]) for j in range(2)]
                            for si, (t, h) in grp:
                                ti = t % 2
                                off = ti * 512
                                act.wait(last_av[(h, ti)])
                                act.wait(rec_free[si % 2])
                                tln[si] = s_act.tick(nc.scalar.activation(
                                    recT[64:65, (si % 2) * 512:(si % 2) * 512 + 512],
                                    av_ps[h][64:65, off:off + 512], AF.Ln))
                            for si, (t, h) in grp:
                                act.wait(tln[si])
                                tex[si] = s_act.tick(nc.scalar.activation(
                                    recB[64:65, (si % 2) * 512:(si % 2) * 512 + 512],
                                    recT[64:65, (si % 2) * 512:(si % 2) * 512 + 512],
                                    AF.Exp, scale=-1.0))
                            for si, (t, h) in grp:
                                ti = t % 2
                                off = ti * 512
                                pe.wait(tex[si])
                                pe.wait(sbl[(h, ti)])
                                tbc[si] = s_pe.tick(nc.tensor.matmul(
                                    score_ps[h][0:64, off:off + 512],
                                    ones64b[64:65, 0:64],
                                    recB[64:65, (si % 2) * 512:(si % 2) * 512 + 512],
                                    start=True, stop=True))
                                rec_free[si % 2] = tbc[si]
                            for si, (t, h) in grp:
                                ti = t % 2
                                off = ti * 512
                                dve.wait(tbc[si])
                                dve.wait(rbc_free[si % 2])
                                trbs[si] = s_dve.tick(nc.vector.tensor_copy(
                                    rbc4[0:64, (si % 2) * 512:(si % 2) * 512 + 512],
                                    score_ps[h][0:64, off:off + 512]))
                                sbl[(h, ti)] = trbs[si]
                            for si, (t, h) in grp:
                                ti = t % 2
                                off = ti * 512
                                q0 = t * 512
                                dve.wait(trbs[si])
                                dve.wait(slab_store)
                                tsl = s_dve.tick(nc.vector.tensor_tensor(
                                    slab_t[h * 64:h * 64 + 64, q0:q0 + 512],
                                    av_ps[h][0:64, off:off + 512],
                                    rbc4[0:64, (si % 2) * 512:(si % 2) * 512 + 512],
                                    ALU.mult))
                                rbc_free[si % 2] = tsl
                                avl[(h, ti)] = tsl
                    syn.wait(s_dve.tok)
                    sdma(dmaSL, b_at_in[p].ap(), slab_t[:])
                    slab_store = dmaSL.tok
                    if p < 3:
                        pool.wait(slab_store)
                        cc[2 + p].tick(nc.gpsimd.collective_compute(
                            "AllGather", ALU.bypass, replica_groups=PAIRS,
                            ins=[b_at_in[p].ap()], outs=[b_at_out[p].ap()]))
                    else:
                        slab3_store = slab_store
                barrier()
                plp.close()
                cqs_stack.close()

                with contextlib.ExitStack() as wst:
                    wsb = lambda n, s, d: wst.enter_context(nc.sbuf_tensor(n, s, d))
                    wot = wsb("wot", [128, 8 * 1024], BF16)
                    xbf3 = [wsb("w_xbf0", [128, TT], BF16)[:], wsb("w_xbf1", [128, TT], BF16)[:]]
                    sq3 = [wsb("w_sq0", [128, TT], BF16)[:], wsb("w_sq1", [128, TT], BF16)[:]]
                    var3 = wsb("w_var", [1, TT], F32)
                    scraps3 = [wsb("w_scr0", [128, TT], F32),
                               wsb("w_scr1", [128, TT], F32)]
                    gfull = wsb("gfull", [128, 8 * T], BF16)
                    ld_woc = [None] * 8
                    for hdc in range(8):
                        ld_woc[hdc] = gload(
                            dmaWC[hdc], wot[:, hdc * 1024:(hdc + 1) * 1024],
                            bass.AP(wo.tensor, hdc * 128 * 1024,
                                    [[1024, 128], [1, 1024]]))
                    pool.wait(ld_small)
                    gtok = [None] * 8

                    def emit_gather(p, r):
                        hdc = r * 4 + p
                        gsem = dmaG8[hdc]
                        pool.wait(cc[2 + p].tok)
                        pool.eq.indirect_dma_start(
                            out=gfull[:, hdc * T:(hdc + 1) * T], out_offset=None,
                            in_=b_at_out[p].ap(),
                            in_offset=bass.IndirectOffsetOnAxis(
                                ap=gidx[:, r:r + 1], axis=0),
                        ).then_inc(gsem.h, 16)
                        gsem.c += 16
                        gtok[hdc] = (gsem, gsem.c)

                    for p, r in ((0, 0), (1, 0), (2, 0), (0, 1), (1, 1),
                                 (2, 1)):
                        emit_gather(p, r)
                    pool.wait(slab3_store)
                    cc[5].tick(nc.gpsimd.collective_compute(
                        "AllGather", ALU.bypass, replica_groups=PAIRS,
                        ins=[b_at_in[3].ap()], outs=[b_at_out[3].ap()]))
                    emit_gather(3, 0)
                    emit_gather(3, 1)
                    bank_last = [None] * 4
                    wbank8 = [(pA, 0), (pA, 512), (pB, 0), (pB, 512)]
                    HDC_ORDER = [0, 1, 2, 4, 5, 6, 3, 7]
                    ln3 = {"xbf_free": [None, None], "sq_free": [None, None],
                           "t6": prev_rstd[0]}
                    passA_begin()
                    rdy_q = None
                    for qs in range(4):
                        for fop in range(2):
                            wo_stop = [None] * 4
                            for hi, hdc in enumerate(HDC_ORDER):
                                pe.wait(gtok[hdc])
                                pe.wait(ld_woc[hdc])
                                for fo4 in range(4):
                                    fo = fop * 4 + fo4
                                    if hi == 0:
                                        pe.wait(bank_last[fo4])
                                    tmm = s_pe.tick(nc.tensor.matmul(
                                        wbank8[fo4][0][:, wbank8[fo4][1]:wbank8[fo4][1] + 512],
                                        wot[:, hdc * 1024 + fo * 128: hdc * 1024 + (fo + 1) * 128],
                                        gfull[:, hdc * T + qs * 512: hdc * T + (qs + 1) * 512],
                                        start=(hi == 0), stop=(hi == 7)))
                                    if hi == 7:
                                        wo_stop[fo4] = tmm
                            for fo4 in range(4):
                                fo = fop * 4 + fo4
                                dve.wait(wo_stop[fo4])
                                bank_last[fo4] = s_dve.tick(nc.vector.tensor_tensor(
                                    X[:, fo * T + qs * 512: fo * T + (qs + 1) * 512],
                                    X[:, fo * T + qs * 512: fo * T + (qs + 1) * 512],
                                    wbank8[fo4][0][:, wbank8[fo4][1]:wbank8[fo4][1] + 512],
                                    ALU.add))
                        t6q = ln_stats_group(qs, xbf3, sq3, var3, ln3)
                        prev_rstd[0] = t6q
                        if qs > 0:
                            passA_group(qs - 1, scraps3, sq3, var3, rdy_q, t6q)
                            ln3["sq_free"] = [
                                [ln3["sq_free"][0], pa_state["ttr"][0]],
                                [ln3["sq_free"][1], pa_state["ttr"][1]]]
                        rdy_q = t6q
                    passA_group(3, scraps3, sq3, var3, rdy_q, rdy_q)
                    barrier()

            ctx_ld2 = passA_finalize(1)
            ffn(1, stats_done=True, pre_ctx=ctx_ld2)

            with nc.sbuf_tensor("xout", [128, 16 * 1024], F32) as xout:
                bank_free = [None] * 8
                k = 0
                for i in range(16):
                    for ch in range(NCH):
                        pt, off = PBANKS[k % 8]
                        pe.wait(bank_free[k % 8])
                        ttr = s_pe.tick(nc.tensor.transpose(
                            pt[:, off:off + 128],
                            X[:, ch * T + i * 128: ch * T + (i + 1) * 128],
                            ident[:]))
                        dve.wait(ttr)
                        bank_free[k % 8] = s_dve.tick(nc.vector.tensor_copy(
                            xout[:, i * 1024 + ch * 128: i * 1024 + (ch + 1) * 128],
                            pt[:, off:off + 128]))
                        k += 1
                    if ch == NCH - 1:
                        syn.wait(bank_free[(k - 1) % 8])
                        sdma(dmaO,
                             bass.AP(out.tensor, i * 128 * D, [[D, 128], [1, D]]),
                             xout[:, i * 1024:(i + 1) * 1024])
                syn.wait(dmaO.tok)

    return nc


_CACHED = {}


def kernel(L0_attn_x, L0_attn_W_dq, L0_attn_W_uq, L0_attn_W_dkv, L0_attn_W_uk,
           L0_attn_W_uv, L0_attn_W_o, L0_ffn_cn_w1, L0_ffn_cn_b1, L0_ffn_cn_w2,
           L0_ffn_cn_b2, L0_ffn_cn_w3, L0_ffn_cn_b3, L0_ffn_cn_w4, L0_ffn_cn_b4,
           L0_ffn_gc_w1, L0_ffn_gc_b1, L0_ffn_gc_w2, L0_ffn_gc_b2, L0_ffn_gc_w3,
           L0_ffn_gc_b3, L0_ffn_adaptive_range):
    if "nc" not in _CACHED:
        _CACHED["nc"] = build_program()
    nc = _CACHED["nc"]

    x = np.asarray(L0_attn_x, dtype=np.float32)
    c = np.ascontiguousarray
    common = {
        "wdq": c(np.asarray(L0_attn_W_dq, np.float32)),
        "wdkv": c(np.asarray(L0_attn_W_dkv, np.float32)),
        "wo": c(np.asarray(L0_attn_W_o, np.float32)),
        "w1": c(np.asarray(L0_ffn_cn_w1, np.float32)),
        "w2": c(np.asarray(L0_ffn_cn_w2, np.float32)),
        "w3": c(np.asarray(L0_ffn_cn_w3, np.float32)),
        "w4r": c(np.asarray(L0_ffn_cn_w4, np.float32).reshape(2, 128).T),
        "gw1": c(np.asarray(L0_ffn_gc_w1, np.float32)),
        "gw2": c(np.asarray(L0_ffn_gc_w2, np.float32)),
        "gw3r": c(np.asarray(L0_ffn_gc_w3, np.float32).reshape(2, 128).T),
        "b1r": c(np.asarray(L0_ffn_cn_b1, np.float32).reshape(8, 128).T),
        "b2r": c(np.asarray(L0_ffn_cn_b2, np.float32).reshape(4, 128).T),
        "b3r": c(np.asarray(L0_ffn_cn_b3, np.float32).reshape(2, 128).T),
        "b4": c(np.asarray(L0_ffn_cn_b4, np.float32).reshape(1, 1)),
        "gb1r": c(np.asarray(L0_ffn_gc_b1, np.float32).reshape(4, 128).T),
        "gb2r": c(np.asarray(L0_ffn_gc_b2, np.float32).reshape(2, 128).T),
        "gb3": c(np.asarray(L0_ffn_gc_b3, np.float32).reshape(1, 1)),
        "ar_in": c(np.asarray(L0_ffn_adaptive_range, np.float32).reshape(1, 1)),
        "ident_in": np.eye(128, dtype=np.float32),
        "triu_in": np.triu(np.ones((128, 128), dtype=np.float32)),
    }
    wuq_f = np.asarray(L0_attn_W_uq, np.float32)
    wuk_f = np.asarray(L0_attn_W_uk, np.float32)
    wuv_f = np.asarray(L0_attn_W_uv, np.float32)

    in_maps = []
    for core in range(8):
        b, j = core // 2, core % 2
        m = dict(common)
        m["xo"] = c(x[b, j * T:(j + 1) * T, :])
        m["wuq"] = c(wuq_f[:, j * 512:(j + 1) * 512])
        m["wuk"] = c(wuk_f[:, j * 512:(j + 1) * 512])
        m["wuv"] = c(wuv_f[:, j * 512:(j + 1) * 512])
        gi = np.zeros((128, 2), dtype=np.uint32)
        for r in range(2):
            gi[:, r] = r * 256 + np.arange(128, dtype=np.uint32) * 2 + j
        m["gidx_in"] = gi
        in_maps.append(m)

    res = run_bass_kernel_spmd(nc, in_maps, core_ids=list(range(8)))
    outp = np.empty((4, TFULL, D), dtype=np.float32)
    for core in range(8):
        b, j = core // 2, core % 2
        outp[b, j * T:(j + 1) * T, :] = res.results[core]["out"]
    return outp
